# version 10
# baseline (speedup 1.0000x reference)
import numpy as np
import ml_dtypes
import concourse.bass as bass
import concourse.mybir as mybir
from concourse.bass_utils import run_bass_kernel_spmd
from concourse.hw_specs import get_activation_tables

_ACT_SET_IDX = list(get_activation_tables("gen3").keys()).index(
    "natural_log_exp_and_others")

F32 = mybir.dt.float32
BF16 = mybir.dt.bfloat16
I8 = mybir.dt.int8
AF = mybir.ActivationFunctionType
ALU = mybir.AluOpType
AX = mybir.AxisListType

B = 2
T = 2048
D = 1024
H = 16
HD = 64
NCH = D // 128
KC = T // 128
TC = T // 128
QH = 1024
MAGIC = float(3 * 2 ** 22)
EPS = 1e-5
KB = 1024
VPAIR = 129
VCHUNK = VPAIR * (H // 2)
VW = VCHUNK * KC


def pieces(lo, hi=QH, step=512):
    out, p = [], lo
    while p < hi:
        q = min((p // step + 1) * step, hi)
        out.append((p, q))
        p = q
    return out


class Ctr:
    def __init__(self):
        self.w = 0; self.x = 0; self.te = 0; self.ve = 0
        self.se = 0; self.gp = 0; self.out = 0; self.cc = 0


class _MockIns:
    __slots__ = ("entry",)
    def __init__(self, entry):
        self.entry = entry
    def then_inc(self, sem, n):
        self.entry[3] = (sem, n)
        return self


class _MockEng:
    __slots__ = ("q", "dma")
    def __init__(self, q, dma=None):
        self.q = q
        self.dma = dma
    def __getattr__(self, name):
        q = self.q
        if name == "dma_start" and self.dma is not None:
            sem, cnt = self.dma
            def call_dma(*a, **k):
                if cnt[0] > 0:
                    q.append(["wait_ge", (sem, 16 * cnt[0]), {}, None])
                cnt[0] += 1
                entry = [name, a, k, None]
                q.append(entry)
                return _MockIns(entry)
            return call_dma
        if name == "dma_start_noser" and self.dma is not None:
            sem, cnt = self.dma
            def call_dma_ns(*a, **k):
                cnt[0] += 1
                entry = ["dma_start", a, k, None]
                q.append(entry)
                return _MockIns(entry)
            return call_dma_ns
        def call(*a, **k):
            entry = [name, a, k, None]
            q.append(entry)
            return _MockIns(entry)
        return call


class _Rec:
    def __init__(self):
        self.q = {n: [] for n in ("tensor", "vector", "scalar", "gpsimd", "sync")}
        self.dma_cfg = {}
    def set_dma(self, engine, sem):
        self.dma_cfg[engine] = (sem, [0])
    def tensor(self, f): f(_MockEng(self.q["tensor"]))
    def vector(self, f): f(_MockEng(self.q["vector"]))
    def scalar(self, f): f(_MockEng(self.q["scalar"]))
    def gpsimd(self, f): f(_MockEng(self.q["gpsimd"], self.dma_cfg.get("gpsimd")))
    def sync(self, f): f(_MockEng(self.q["sync"], self.dma_cfg.get("sync")))
    def replay(self, block):
        def mk(q):
            def run(eng):
                for name, a, k, ti in q:
                    ins = getattr(eng, name)(*a, **k)
                    if ti is not None:
                        ins.then_inc(*ti)
            return run
        block.tensor(mk(self.q["tensor"]))
        block.vector(mk(self.q["vector"]))
        block.scalar(mk(self.q["scalar"]))
        block.gpsimd(mk(self.q["gpsimd"]))
        block.sync(mk(self.q["sync"]))


def build_program(stop_after="full", dump=()):
    nc = bass.Bass("TRN2", target_bir_lowering=False, debug=False)

    d_xT = nc.dram_tensor("xT", [B, D, T], F32, kind="ExternalInput").ap()
    d_x = nc.dram_tensor("x", [B, T, D], F32, kind="ExternalInput").ap()
    d_wqT = nc.dram_tensor("wqT", [D, D], F32, kind="ExternalInput").ap()
    d_wkT = nc.dram_tensor("wkT", [D, D], F32, kind="ExternalInput").ap()
    d_wvT = nc.dram_tensor("wvT", [D, D], F32, kind="ExternalInput").ap()
    d_woT = nc.dram_tensor("woT", [D, D], F32, kind="ExternalInput").ap()
    d_w8T = nc.dram_tensor("w8T", [D, D], I8, kind="ExternalInput").ap()
    d_bq = nc.dram_tensor("bq_pm", [128, NCH], F32, kind="ExternalInput").ap()
    d_bk = nc.dram_tensor("bk_pm", [128, NCH], F32, kind="ExternalInput").ap()
    d_bv = nc.dram_tensor("bv_bc", [128, D], F32, kind="ExternalInput").ap()
    d_bo = nc.dram_tensor("bo_bc", [128, D], F32, kind="ExternalInput").ap()
    d_bf = nc.dram_tensor("bf_bc", [128, D], F32, kind="ExternalInput").ap()
    d_ws = nc.dram_tensor("ws_bc", [128, D], F32, kind="ExternalInput").ap()
    d_id = nc.dram_tensor("ident", [128, 128], BF16, kind="ExternalInput").ap()
    d_mk = nc.dram_tensor("mask", [128, 128], BF16, kind="ExternalInput").ap()
    d_out = nc.dram_tensor("out", [B, T, D], F32, kind="ExternalOutput").ap()
    d_cci = nc.dram_tensor("cc_in", [1, 8], F32).ap()
    d_cco = nc.dram_tensor("cc_out", [1, 8], F32).ap()

    BASE = 16512
    at_ = nc.alloc_sbuf_tensor_at
    cur = [BASE]

    def place(name, shape, dtype, align=64):
        sz = int(np.prod(shape[1:])) * mybir.dt.size(dtype)
        cur[0] = (cur[0] + align - 1) // align * align
        h = at_(name, list(shape), dtype, offset=cur[0])
        cur[0] += sz
        return h

    ident = place("ident", [128, 128], BF16)
    mask = place("mask", [128, 128], BF16)
    bq = place("bq", [128, NCH], F32)
    bk = place("bk", [128, NCH], F32)
    bv = place("bv", [128, D], BF16)
    bo = place("bo", [128, D], BF16)
    epsc = place("epsc", [128, 1], F32)
    magicc = place("magicc", [128, 1], F32)
    nmagicc = place("nmagicc", [128, 1], F32)
    xs_col = place("xs_col", [128, 1], F32)
    rxs_col = place("rxs_col", [128, 1], F32)
    stats = place("stats", [128, 256], F32)
    gmaxc = place("gmaxc", [128, 1], F32)
    gmax1 = place("gmax1", [1, 4], F32)
    xsc_tmp = place("xsc_tmp", [1, 4], F32)
    cc_stage = place("cc_stage", [1, 8], F32)
    rcpt = place("rcpt", [128, 40], F32)
    pt0 = place("pt0", [128, QH], BF16)
    pt1 = place("pt1", [128, QH], BF16)
    pt2 = place("pt2", [128, QH], BF16)
    pt3 = place("pt3", [128, QH], BF16)
    assert cur[0] <= BASE + 18 * KB, cur[0]

    UNION = BASE + 18 * KB
    bcastt = [at_("bcast0", [128, QH], F32, offset=UNION),
              at_("bcast1", [128, QH], F32, offset=UNION + 4 * KB)]
    ostg = [at_("ostg0", [128, QH], F32, offset=UNION + 8 * KB),
            at_("ostg1", [128, QH], F32, offset=UNION + 12 * KB),
            at_("ostg2", [128, QH], F32, offset=UNION + 16 * KB),
            at_("ostg3", [128, QH], F32, offset=UNION + 20 * KB)]
    wfull = at_("wfull", [128, NCH * D], BF16, offset=UNION)
    xstage = [at_("xstage0", [128, D], F32, offset=UNION),
              at_("xstage1", [128, D], F32, offset=UNION + 4 * KB)]
    wofull = at_("wofull", [128, NCH * D], BF16, offset=UNION + 8 * KB)
    qtmp = at_("qtmp", [128, D], F32, offset=UNION)
    d1t = at_("d1t", [128, D], F32, offset=UNION + 4 * KB)
    d2t = [at_("d2t0", [128, D], F32, offset=UNION + 8 * KB),
           at_("d2t1", [128, D], F32, offset=UNION + 12 * KB)]
    bff = at_("bff", [128, D], F32, offset=UNION + 16 * KB)
    wsc = at_("wsc", [128, D], F32, offset=UNION + 20 * KB)
    UNION_END = UNION + 24 * KB

    RXT = UNION_END
    xts = at_("xts", [128, NCH * T], BF16, offset=RXT)
    at1 = at_("at1", [128, NCH * T], BF16, offset=RXT)
    w8full = at_("w8full", [128, NCH * D], BF16, offset=RXT)
    xqt = [at_("xqt0", [128, D], BF16, offset=RXT + 16 * KB),
           at_("xqt1", [128, D], BF16, offset=RXT + 18 * KB),
           at_("xqt2", [128, D], BF16, offset=RXT + 20 * KB),
           at_("xqt3", [128, D], BF16, offset=RXT + 22 * KB)]
    ost = [at_("ost0", [128, D], F32, offset=RXT + 24 * KB),
           at_("ost1", [128, D], F32, offset=RXT + 28 * KB)]

    RQ = RXT + 32 * KB
    qt = at_("qt", [128, NCH * T], BF16, offset=RQ)
    RK = RQ + 32 * KB
    kt = at_("kt", [128, NCH * T], BF16, offset=RK)
    ln1_0 = at_("ln1_0", [128, TC * D], F32, offset=RQ)
    RV = RK + 32 * KB
    vaug = at_("vaug", [128, VW], BF16, offset=RV)
    RA0 = RV + 33 * KB
    at0 = at_("at0", [128, NCH * T], BF16, offset=RA0)
    ln1_1 = at_("ln1_1", [128, TC * D], F32, offset=RV)
    assert RA0 + 32 * KB <= 229344, RA0 + 32 * KB

    ATB = [at0, at1]
    LN1 = [ln1_0, ln1_1]
    DUMPABLE = {"qt": (qt, BF16, NCH * T), "kt": (kt, BF16, NCH * T),
                "vaug": (vaug, BF16, VW), "at0": (at0, BF16, NCH * T),
                "at1": (at1, BF16, NCH * T), "ln1_0": (ln1_0, F32, TC * D),
                "ln1_1": (ln1_1, F32, TC * D), "stats": (stats, F32, 256),
                "xs_col": (xs_col, F32, 1)}
    d_dbg = {n: nc.dram_tensor("dbg_" + n, [128, DUMPABLE[n][2]],
                               DUMPABLE[n][1], kind="ExternalOutput").ap()
             for n in dump}

    c = Ctr()

    with (
        nc.psum_tensor("psA", [128, 2048], F32) as psA,
        nc.psum_tensor("psB", [128, 2048], F32) as psB,
        nc.semaphore("sw") as sw,
        nc.semaphore("sx") as sx,
        nc.semaphore("ste") as ste,
        nc.semaphore("sve") as sve,
        nc.semaphore("sse") as sse,
        nc.semaphore("sgp") as sgp,
        nc.semaphore("sout") as sout,
        nc.semaphore("scc") as scc,
        nc.Block() as block,
    ):
        rec = _Rec()
        rec.set_dma('sync', sw)
        rec.set_dma('gpsimd', sx)
        def _(s):
            s.add_instruction(mybir.InstLoadActFuncSet(
                name=nc.get_next_instruction_name(),
                act_func_set_id=_ACT_SET_IDX, ins=[], outs=[]))
        rec.scalar(_)

        def _(s):
            for dst, src in [(ident, d_id), (mask, d_mk), (bq, d_bq), (bk, d_bk)]:
                s.dma_start(out=dst[:], in_=src).then_inc(sw, 16)
                c.w += 16
        rec.sync(_)
        consts_w = c.w

        def _(g):
            g.dma_start(out=bv[:], in_=d_bv).then_inc(sx, 16); c.x += 16
            g.dma_start(out=bo[:], in_=d_bo).then_inc(sx, 16); c.x += 16
        rec.gpsimd(_)
        bias_x = c.x

        def _(v):
            v.memset(epsc[:], EPS)
            v.memset(magicc[:], MAGIC)
            v.memset(nmagicc[:], -MAGIC)
            v.memset(cc_stage[:], 0.0)
            v.memset(bass.AP(vaug, 64, [[VW, 128], [VCHUNK, KC],
                                        [VPAIR, H // 2]]), 1.0).then_inc(sve, 1)
            c.ve += 1
        rec.vector(_)
        consts_ve = c.ve

        def load_xT(b, wait_te):
            def _(g):
                g.wait_ge(ste, wait_te)
                for ch in range(NCH):
                    dma = g.dma_start if ch == 0 else g.dma_start_noser
                    dma(
                        out=xts[:, T * ch:T * ch + T],
                        in_=d_xT[b, 128 * ch:128 * ch + 128, :],
                    ).then_inc(sx, 16)
                    c.x += 16
            rec.gpsimd(_)
            return c.x

        def load_w(src, wait_te, dst=None, wait_ve=None, wait_gp=None):
            dst = wfull if dst is None else dst
            def _(g):
                g.wait_ge(ste, wait_te)
                if wait_ve is not None:
                    g.wait_ge(sve, wait_ve)
                if wait_gp is not None:
                    g.wait_ge(sgp, wait_gp)
                g.dma_start(
                    out=dst.ap().rearrange("p (ch n) -> p ch n", ch=NCH),
                    in_=src.rearrange("(ch p) n -> p ch n", p=128),
                ).then_inc(sx, 16)
                c.x += 16
            rec.gpsimd(_)
            return c.x

        def load_w_chunked(src, wait_te, wait_ve=None, wait_gp=None):
            counts = []
            def _(g):
                for ch in range(NCH):
                    if ch == 0:
                        g.wait_ge(ste, wait_te)
                        if wait_ve is not None:
                            g.wait_ge(sve, wait_ve)
                        if wait_gp is not None:
                            g.wait_ge(sgp, wait_gp)
                    g.dma_start(
                        out=wfull[:, D * ch:D * ch + D],
                        in_=src[128 * ch:128 * ch + 128, :],
                    ).then_inc(sx, 16)
                    c.x += 16
                    counts.append(c.x)
            rec.gpsimd(_)
            return counts

        def proj_qk(bias_col, dst, x_ready, w_counts, prev_ve):
            evac_after = {}
            entry_se = c.se
            for nch in range(NCH):
                arena = psA if nch % 2 == 0 else psB
                def _(t, nch=nch, arena=arena):
                    t.wait_ge(sx, x_ready)
                    if nch < 2:
                        t.wait_ge(sse, entry_se)
                    t.wait_ge(sve, evac_after.get(nch - 2, prev_ve))
                    ins = None
                    for kc in range(NCH):
                        if nch == 0:
                            t.wait_ge(sx, w_counts[kc])
                        for tt in range(4):
                            ins = t.matmul(
                                arena[:, 512 * tt:512 * tt + 512],
                                wfull[:, D * kc + 128 * nch:
                                      D * kc + 128 * nch + 128],
                                xts[:, T * kc + 512 * tt:T * kc + 512 * tt + 512],
                                start=(kc == 0), stop=(kc == NCH - 1))
                    ins.then_inc(ste, 1)
                    c.te += 1
                rec.tensor(_)
                te_done = c.te
                def _(v, nch=nch, arena=arena, te_done=te_done):
                    v.wait_ge(ste, te_done)
                    v.wait_ge(sw, consts_w)
                    ins = None
                    for tt in range(4):
                        ins = v.tensor_scalar(
                            dst[:, T * nch + 512 * tt:T * nch + 512 * tt + 512],
                            arena[:, 512 * tt:512 * tt + 512],
                            bias_col[:, nch:nch + 1], None, ALU.add)
                    ins.then_inc(sve, 1)
                    c.ve += 1
                rec.vector(_)
                evac_after[nch] = c.ve
            return c.ve

        def proj_v(x_ready, w_counts, prev_ve):
            evac_after = {}
            entry_se = c.se
            for tc in range(TC):
                arena = psA if tc % 2 == 0 else psB
                def _(t, tc=tc, arena=arena):
                    t.wait_ge(sx, x_ready)
                    if tc < 2:
                        t.wait_ge(sse, entry_se)
                    t.wait_ge(sve, evac_after.get(tc - 2, prev_ve))
                    ins = None
                    for kc in range(NCH):
                        if tc == 0:
                            t.wait_ge(sx, w_counts[kc])
                        for nt in range(2):
                            ins = t.matmul(
                                arena[:, 512 * nt:512 * nt + 512],
                                xts[:, T * kc + 128 * tc:T * kc + 128 * tc + 128],
                                wfull[:, D * kc + 512 * nt:D * kc + 512 * nt + 512],
                                start=(kc == 0), stop=(kc == NCH - 1))
                    ins.then_inc(ste, 1)
                    c.te += 1
                rec.tensor(_)
                te_done = c.te
                def _(v, tc=tc, arena=arena, te_done=te_done):
                    v.wait_ge(ste, te_done)
                    v.wait_ge(sx, bias_x)
                    ins = None
                    for nt in range(2):
                        base = VCHUNK * tc + VPAIR * 4 * nt
                        dst = bass.AP(vaug, base,
                                      [[VW, 128], [VPAIR, 4], [65, 2], [1, 64]])
                        src = bass.AP(arena.ap().tensor,
                                      arena.ap().offset + 512 * nt,
                                      [[2048, 128], [128, 4], [64, 2], [1, 64]])
                        b_ap = bass.AP(bv, 512 * nt,
                                       [[D, 128], [128, 4], [64, 2], [1, 64]])
                        ins = v.tensor_tensor(dst, src, b_ap, ALU.add)
                    ins.then_inc(sve, 1)
                    c.ve += 1
                rec.vector(_)
                evac_after[tc] = c.ve
            return c.ve

        def attention(at_dst, ready_ve):
            entry_se = c.se
            entry_gp = c.gp
            seg_i = [0]
            half_i = [0]
            exp_after = {}
            pv_after = {}
            norm_after = {}
            stage_after = {}

            class Stream:
                def __init__(self, h, hf):
                    self.h, self.hf = h, hf
                    self.q0h = QH * hf
                    self.js = [j for j in range(KC) if 128 * j < self.q0h + QH]
                    self.hidx = half_i[0]
                    half_i[0] += 1
                    self.osl = self.hidx % 2
                    self.obuf = psB[:, QH * self.osl:QH * self.osl + QH]
                    self.segs = []
                    self.ch = h // 2
                    self.po = 64 * (h % 2)
                    self.pblk = VPAIR * (h // 2 % (H // 2))

            def emit_s(st, j):
                qs = max(st.q0h, 128 * j)
                lo = qs - st.q0h
                si = seg_i[0]
                sp = si % 2
                sps = psA[:, QH * sp:QH * sp + QH]
                ptb = (pt0, pt1, pt2, pt3)[si % 4]
                def _(t):
                    t.wait_ge(sve, ready_ve)
                    if si >= 2:
                        t.wait_ge(sse, exp_after[si - 2])
                    else:
                        t.wait_ge(sse, entry_se)
                    ins = None
                    for (plo, phi) in pieces(lo):
                        ins = t.matmul(
                            sps[:, plo:phi],
                            kt[st.po:st.po + 64,
                               T * st.ch + 128 * j:T * st.ch + 128 * j + 128],
                            qt[st.po:st.po + 64,
                               T * st.ch + st.q0h + plo:T * st.ch + st.q0h + phi],
                            start=True, stop=True)
                    ins.then_inc(ste, 1)
                    c.te += 1
                rec.tensor(_)
                s_done = c.te
                def _(sc):
                    sc.wait_ge(ste, s_done)
                    if si >= 4:
                        sc.wait_ge(ste, pv_after[si - 4])
                    sc.activation(ptb[:, lo:QH], sps[:, lo:QH], AF.Exp,
                                  scale=0.125).then_inc(sse, 1)
                    c.se += 1
                rec.scalar(_)
                exp_after[si] = c.se
                e_done = c.se
                m_done = None
                if qs == 128 * j:
                    def _(v):
                        v.wait_ge(sse, e_done)
                        v.tensor_tensor(ptb[:, lo:lo + 128],
                                        ptb[:, lo:lo + 128], mask[:],
                                        ALU.mult).then_inc(sve, 1)
                        c.ve += 1
                    rec.vector(_)
                    m_done = c.ve
                st.segs.append((j, lo, ptb, e_done, m_done, si))
                seg_i[0] += 1

            def emit_pv(st, k):
                (j, lo, ptb, e_done, m_done, si) = st.segs[k]
                first = (k == 0)
                last_j = st.js[-1]
                def _(t):
                    if first:
                        if st.hidx >= 2:
                            t.wait_ge(sve, stage_after[st.hidx - 2])
                        else:
                            t.wait_ge(sve, ready_ve)
                    if m_done is not None:
                        t.wait_ge(sve, m_done)
                    else:
                        t.wait_ge(sse, e_done)
                    if st.h % 2 == 0:
                        lhs = vaug[:, VCHUNK * j + st.pblk:
                                   VCHUNK * j + st.pblk + 65]
                        nrow = 65
                    else:
                        lhs = vaug[:, VCHUNK * j + st.pblk + 1:
                                   VCHUNK * j + st.pblk + 129]
                        nrow = 128
                    ins = None
                    for (plo, phi) in pieces(lo):
                        bank_last = min((st.q0h + phi - 1) // 128, last_j)
                        ins = t.matmul(st.obuf[0:nrow, plo:phi], lhs,
                                       ptb[:, plo:phi],
                                       start=(j == 0), stop=(j == bank_last))
                    ins.then_inc(ste, 1)
                    c.te += 1
                rec.tensor(_)
                pv_after[si] = c.te

            def emit_norm_stage(st):
                pv_done = pv_after[st.segs[-1][5]]
                stg = ostg[st.hidx % 4]
                even = (st.h % 2 == 0)
                def _(v):
                    v.wait_ge(ste, pv_done)
                    v.wait_ge(sgp, norm_after.get(st.hidx - 4, entry_gp))
                    if even:
                        v.tensor_copy(stg[0:65, :], st.obuf[0:65, :]) \
                         .then_inc(sve, 1)
                    else:
                        v.tensor_copy(stg[32:64, :], st.obuf[32:64, :])
                        v.tensor_copy(stg[64:128, :], st.obuf[64:128, :]) \
                         .then_inc(sve, 1)
                    c.ve += 1
                    if not even:
                        v.wait_ge(sve, c.ve)
                        v.tensor_scalar(stg[32:64, :], stg[32:64, :], 1e-20,
                                        1e18, ALU.max, ALU.min).then_inc(sve, 1)
                        c.ve += 1
                rec.vector(_)
                stage_after[st.hidx] = c.ve
                return c.ve

            def emit_norm_finish(st, cp_done):
                h, ch, q0h = st.h, st.ch, st.q0h
                stg = ostg[st.hidx % 4]
                bcast = bcastt[(st.hidx % 4) // 2]
                even = (h % 2 == 0)
                dr = stg[64:65, :] if even else stg[32:64, :]
                rr = stg[96:97, :] if even else stg[0:32, :]
                def _(sc):
                    sc.wait_ge(sve, cp_done)
                    sc.activation(rr, dr, AF.Ln).then_inc(sse, 1)
                    c.se += 1
                    sc.wait_ge(sse, c.se)
                    sc.activation(rr, rr, AF.Exp, scale=-1.0).then_inc(sse, 1)
                    c.se += 1
                rec.scalar(_)
                r_done = c.se
                r0, r1 = (0, 64) if even else (64, 128)
                def _(sc):
                    sc.wait_ge(sse, r_done)
                    rsrc = stg[96:97, :] if even else stg[31:32, :]
                    sc.dma_start(
                        out=bcast[r0:r1, :],
                        in_=rsrc.unsqueeze(1)
                            .broadcast_to((1, r1 - r0, QH))).then_inc(sw, 16)
                    c.w += 16
                rec.sync(_)
                b_done = c.w
                def _(g):
                    g.wait_ge(sw, b_done)
                    g.tensor_tensor(
                        at_dst[r0:r1, T * ch + q0h:T * ch + q0h + QH],
                        stg[0:64, :] if even else stg[64:128, :],
                        bcast[r0:r1, :], ALU.mult).then_inc(sgp, 1)
                    c.gp += 1
                rec.gpsimd(_)
                norm_after[st.hidx] = c.gp

            pending = []
            for hp in range(0, H, 2):
                for hf in range(2):
                    sa = Stream(hp, hf)
                    sb = Stream(hp + 1, hf)
                    n = len(sa.js)
                    for k in range(n):
                        emit_s(sa, sa.js[k])
                        emit_s(sb, sb.js[k])
                        if k == 2:
                            for st, cp in pending:
                                emit_norm_finish(st, cp)
                            pending = []
                        if k >= 1:
                            emit_pv(sa, k - 1)
                            emit_pv(sb, k - 1)
                    emit_pv(sa, n - 1)
                    emit_pv(sb, n - 1)
                    pending.append((sa, emit_norm_stage(sa)))
                    pending.append((sb, emit_norm_stage(sb)))
            for st, cp in pending:
                emit_norm_finish(st, cp)
            return c.ve, c.gp

        def wo_ln1(bidx, at_src, attn_ve, attn_se, wo_ready, first_xstage_te, attn_gp=0):
            ln1 = LN1[bidx]
            ev_after = {}
            mm_after = {}
            entry_ve = c.ve

            def head(tc):
                arena = psA if tc % 2 == 0 else psB
                xst = xstage[tc % 2]
                def _(s):
                    if tc < 2:
                        s.wait_ge(ste, first_xstage_te)
                        s.wait_ge(sve, entry_ve)
                        s.wait_ge(sgp, attn_gp)
                    else:
                        s.wait_ge(sve, ev_after[tc - 2])
                    s.dma_start(out=xst[:],
                                in_=d_x[bidx, 128 * tc:128 * tc + 128, :]) \
                        .then_inc(sw, 16)
                    c.w += 16
                rec.sync(_)
                x_done = c.w
                def _(t):
                    t.wait_ge(sve, attn_ve)
                    t.wait_ge(sse, attn_se)
                    t.wait_ge(sgp, attn_gp)
                    t.wait_ge(sx, wo_ready)
                    t.wait_ge(sve, ev_after.get(tc - 2, entry_ve))
                    ins = None
                    for dch in range(NCH):
                        for nt in range(2):
                            ins = t.matmul(
                                arena[:, 512 * nt:512 * nt + 512],
                                at_src[:, T * dch + 128 * tc:
                                       T * dch + 128 * tc + 128],
                                wofull[:, D * dch + 512 * nt:
                                       D * dch + 512 * nt + 512],
                                start=(dch == 0), stop=(dch == NCH - 1))
                    ins.then_inc(ste, 1)
                    c.te += 1
                rec.tensor(_)
                mm_after[tc] = c.te
                return x_done

            def tail(tc, x_done):
                arena = psA if tc % 2 == 0 else psB
                xst = xstage[tc % 2]
                si = 200 + 16 * (tc % 2)
                lt = ln1[:, D * tc:D * tc + D]
                def _(v):
                    v.wait_ge(ste, mm_after[tc])
                    v.wait_ge(sw, x_done)
                    v.tensor_tensor(lt, arena[:, 0:D], bo[:], ALU.add) \
                     .then_inc(sve, 1)
                    c.ve += 1
                    v.wait_ge(sve, c.ve)
                    v.tensor_tensor(lt, lt, xst[:], ALU.add).then_inc(sve, 1)
                    c.ve += 1
                    v.wait_ge(sve, c.ve)
                    v.reduce_sum(stats[:, si:si + 1], lt, axis=AX.X) \
                     .then_inc(sve, 1)
                    c.ve += 1
                rec.vector(_)
                r1_ve = c.ve
                ev_after[tc] = c.ve
                def _(s):
                    s.wait_ge(sve, r1_ve - 1)
                    s.wait_ge(sse, c.se)
                    s.activation(pt0[:], lt, AF.Square,
                                 accum_out=stats[:, si + 1:si + 2]) \
                     .then_inc(sse, 1)
                    c.se += 1
                rec.scalar(_)
                sq_se = c.se
                def _(v):
                    v.wait_ge(sve, r1_ve)
                    v.wait_ge(sse, sq_se)
                    v.scalar_tensor_tensor(stats[:, si + 2:si + 3],
                                           stats[:, si:si + 1], 1.0 / D,
                                           stats[:, si:si + 1],
                                           ALU.mult, ALU.mult).then_inc(sve, 1)
                    c.ve += 1
                    v.tensor_scalar(stats[:, si + 3:si + 4],
                                    stats[:, si:si + 1], 1.0 / D, None,
                                    ALU.mult).then_inc(sve, 1)
                    c.ve += 1
                    v.wait_ge(sve, c.ve)
                    v.tensor_tensor(stats[:, si + 4:si + 5],
                                    stats[:, si + 1:si + 2],
                                    stats[:, si + 2:si + 3], ALU.subtract) \
                     .then_inc(sve, 1)
                    c.ve += 1
                    v.wait_ge(sve, c.ve)
                    v.tensor_copy(rcpt[:, 38:39],
                                  stats[:, si + 4:si + 5]).then_inc(sve, 1)
                    c.ve += 1
                rec.vector(_)
                u_ve = c.ve
                def _(s):
                    s.wait_ge(sve, u_ve)
                    s.activation(stats[:, si + 5:si + 6],
                                 stats[:, si + 4:si + 5], AF.Ln,
                                 bias=epsc[:], scale=1.0 / D).then_inc(sse, 1)
                    c.se += 1
                    s.wait_ge(sse, c.se)
                    s.activation(stats[:, si + 6:si + 7],
                                 stats[:, si + 5:si + 6], AF.Exp,
                                 scale=-0.5).then_inc(sse, 1)
                    c.se += 1
                rec.scalar(_)
                rs_se = c.se
                def _(v):
                    v.wait_ge(sse, rs_se)
                    v.scalar_tensor_tensor(rcpt[:, 30 + tc % 2:31 + tc % 2],
                                           stats[:, si + 3:si + 4], -1.0,
                                           stats[:, si + 6:si + 7],
                                           ALU.mult, ALU.mult).then_inc(sve, 1)
                    c.ve += 1
                    v.wait_ge(sve, c.ve)
                    v.tensor_copy(rcpt[:, 32 + tc % 2:33 + tc % 2],
                                  stats[:, si + 6:si + 7]).then_inc(sve, 1)
                    c.ve += 1
                rec.vector(_)
                mrs_ve = c.ve
                def _(s):
                    s.wait_ge(sve, mrs_ve)
                    s.activation(lt, lt, AF.Identity,
                                 bias=rcpt[:, 30 + tc % 2:31 + tc % 2],
                                 scale=rcpt[:, 32 + tc % 2:33 + tc % 2]) \
                     .then_inc(sse, 1)
                    c.se += 1
                rec.scalar(_)
                nrm_se = c.se
                def _(v):
                    v.wait_ge(sse, nrm_se)
                    v.reduce_max(stats[:, 224 + bidx * TC + tc:
                                       225 + bidx * TC + tc], lt,
                                 axis=AX.X, apply_absolute_value=True) \
                     .then_inc(sve, 1)
                    c.ve += 1
                rec.vector(_)

            xd = {}
            for tc in range(TC):
                xd[tc] = head(tc)
                if tc >= 1:
                    tail(tc - 1, xd[tc - 1])
            tail(TC - 1, xd[TC - 1])
            return c.ve, c.se

        def global_scale(prev_ve):
            def _(v):
                v.wait_ge(sve, prev_ve)
                v.reduce_max(gmaxc[:], stats[:, 224:224 + 2 * TC], axis=AX.X) \
                 .then_inc(sve, 1)
                c.ve += 1
                v.wait_ge(sve, c.ve)
                v.tensor_copy(rcpt[:, 38:39], gmaxc[:]).then_inc(sve, 1)
                c.ve += 1
            rec.vector(_)
            g_ve = c.ve
            def _(g):
                g.wait_ge(sve, g_ve)
                g.reduce_max(gmax1[0:1, 0:1], gmaxc[:], axis=AX.XYZWC) \
                 .then_inc(sgp, 1)
                c.gp += 1
                g.wait_ge(sgp, c.gp)
                g.tensor_copy(gmax1[0:1, 3:4], gmax1[0:1, 0:1]).then_inc(sgp, 1)
                c.gp += 1
            rec.gpsimd(_)
            g_gp = c.gp
            def _(v):
                v.wait_ge(sgp, g_gp)
                v.tensor_scalar(cc_stage[0:1, 0:1], gmax1[0:1, 0:1],
                                1.0 / 127.0, EPS, ALU.mult, ALU.max) \
                 .then_inc(sve, 1)
                c.ve += 1
                v.wait_ge(sve, c.ve)
                v.tensor_copy(xsc_tmp[0:1, 1:2], cc_stage[0:1, 0:1]) \
                 .then_inc(sve, 1)
                c.ve += 1
            rec.vector(_)
            st_ve = c.ve
            def _(g):
                g.wait_ge(sve, st_ve)
                g.dma_start(out=d_cci, in_=cc_stage[:]).then_inc(sx, 16)
                c.x += 16
                g.wait_ge(sx, c.x)
                g.collective_compute(
                    "AllReduce", ALU.max, replica_groups=[list(range(8))],
                    ins=[d_cci.opt()], outs=[d_cco.opt()]).then_inc(scc, 1)
                c.cc += 1
                g.wait_ge(scc, c.cc)
                g.dma_start(out=xsc_tmp[0:1, 0:1], in_=d_cco[0:1, 0:1]) \
                 .then_inc(sx, 16)
                c.x += 16
                g.wait_ge(sx, c.x)
                g.dma_start(out=xs_col[:],
                            in_=xsc_tmp[0:1, 0:1].unsqueeze(1)
                            .broadcast_to((1, 128, 1))).then_inc(sx, 16)
                c.x += 16
            rec.gpsimd(_)
            xs_x = c.x
            def _(v):
                v.wait_ge(sx, xs_x)
                v.reciprocal(gmax1[0:1, 2:3], xsc_tmp[0:1, 0:1]).then_inc(sve, 1)
                c.ve += 1
                v.wait_ge(sve, c.ve)
                v.tensor_copy(xsc_tmp[0:1, 2:3], gmax1[0:1, 2:3]).then_inc(sve, 1)
                c.ve += 1
            rec.vector(_)
            rx_ve = c.ve
            def _(g):
                g.wait_ge(sve, rx_ve)
                g.dma_start(out=rxs_col[:],
                            in_=gmax1[0:1, 2:3].unsqueeze(1)
                            .broadcast_to((1, 128, 1))).then_inc(sx, 16)
                c.x += 16
            rec.gpsimd(_)
            return c.x

        def ffn(wait_x, w8_ready, wsbf_w):
            N = B * TC
            copy_after = {}
            mm_after = {}
            sqse_after = {}
            deq_after = {}
            out_after = {}
            r2ve_after = {}

            def head(i):
                bidx, tc = i // TC, i % TC
                par4 = i % 4
                lt = LN1[bidx][:, D * tc:D * tc + D]
                ptr = psB[:, 512 * par4:512 * par4 + 512].bitcast(BF16)
                pfa = psA[:, QH * (i % 2):QH * (i % 2) + QH]
                xq = xqt[par4]
                def _(sc):
                    sc.wait_ge(sx, wait_x)
                    sc.wait_ge(sw, wsbf_w)
                    sc.wait_ge(sse, c.se)
                    if i >= 4:
                        sc.wait_ge(ste, mm_after[i - 4])
                    sc.activation(qtmp[:], lt, AF.Identity, bias=magicc[:],
                                  scale=rxs_col[:]).then_inc(sse, 1)
                    c.se += 1
                    sc.wait_ge(sse, c.se)
                    sc.activation(xq[:], qtmp[:], AF.Identity,
                                  bias=nmagicc[:], scale=1.0).then_inc(sse, 1)
                    c.se += 1
                rec.scalar(_)
                q2_se = c.se
                def _(t):
                    t.wait_ge(sse, q2_se)
                    if i >= 4:
                        t.wait_ge(sve, copy_after[i - 4])
                    ins = None
                    for cc8 in range(NCH):
                        ins = t.transpose(ptr[:, 128 * cc8:128 * cc8 + 128],
                                          xq[:, 128 * cc8:128 * cc8 + 128],
                                          ident[:])
                    ins.then_inc(ste, 1)
                    c.te += 1
                rec.tensor(_)
                tr_done = c.te
                def _(v):
                    v.wait_ge(ste, tr_done)
                    v.tensor_copy(xq[:], ptr[:]).then_inc(sve, 1)
                    c.ve += 1
                rec.vector(_)
                copy_after[i] = c.ve
                def _(t):
                    t.wait_ge(sve, copy_after[i])
                    t.wait_ge(sx, w8_ready)
                    if i >= 2:
                        t.wait_ge(sve, deq_after[i - 2])
                    ins = None
                    for kc8 in range(NCH):
                        for nt in range(2):
                            ins = t.matmul(
                                pfa[:, 512 * nt:512 * nt + 512],
                                xq[:, 128 * kc8:128 * kc8 + 128],
                                w8full[:, D * kc8 + 512 * nt:
                                       D * kc8 + 512 * nt + 512],
                                start=(kc8 == 0), stop=(kc8 == NCH - 1))
                    ins.then_inc(ste, 1)
                    c.te += 1
                rec.tensor(_)
                mm_after[i] = c.te

            def tail(i):
                bidx, tc = i // TC, i % TC
                par = i % 2
                lt = LN1[bidx][:, D * tc:D * tc + D]
                pfa = psA[:, QH * par:QH * par + QH]
                si = 240 + 8 * par
                def _(v):
                    v.wait_ge(ste, mm_after[i])
                    if i >= 2:
                        v.wait_ge(sse, sqse_after[i - 2])
                        v.wait_ge(sw, out_after[i - 2])
                    v.scalar_tensor_tensor(d1t[:], pfa[:, 0:D], xs_col[:],
                                           wsc[:], ALU.mult, ALU.mult) \
                     .then_inc(sve, 1)
                    c.ve += 1
                    v.wait_ge(sve, c.ve)
                    v.tensor_tensor(d2t[par][:], d1t[:], bff[:], ALU.add) \
                     .then_inc(sve, 1)
                    c.ve += 1
                    v.wait_ge(sve, c.ve)
                    v.tensor_tensor(d2t[par][:], d2t[par][:], lt, ALU.add) \
                     .then_inc(sve, 1)
                    c.ve += 1
                    v.wait_ge(sve, c.ve)
                    v.reduce_sum(stats[:, si:si + 1], d2t[par][:], axis=AX.X) \
                     .then_inc(sve, 1)
                    c.ve += 1
                rec.vector(_)
                r2_ve = c.ve
                deq_after[i] = c.ve - 3
                r2ve_after[i] = c.ve
                def _(sc):
                    sc.wait_ge(sve, r2_ve - 1)
                    sc.wait_ge(sse, c.se)
                    sc.activation(pt0[:], d2t[par][:], AF.Square,
                                  accum_out=stats[:, si + 1:si + 2]) \
                     .then_inc(sse, 1)
                    c.se += 1
                    sq_se = c.se
                    sqse_after[i] = c.se
                rec.scalar(_)
                sq_se = sqse_after[i]
                def _(v):
                    v.wait_ge(sve, r2_ve)
                    v.wait_ge(sse, sq_se)
                    v.scalar_tensor_tensor(stats[:, si + 2:si + 3],
                                           stats[:, si:si + 1], 1.0 / D,
                                           stats[:, si:si + 1],
                                           ALU.mult, ALU.mult).then_inc(sve, 1)
                    c.ve += 1
                    v.tensor_scalar(stats[:, si + 3:si + 4],
                                    stats[:, si:si + 1], 1.0 / D, None,
                                    ALU.mult).then_inc(sve, 1)
                    c.ve += 1
                    v.wait_ge(sve, c.ve)
                    v.tensor_tensor(stats[:, si + 4:si + 5],
                                    stats[:, si + 1:si + 2],
                                    stats[:, si + 2:si + 3], ALU.subtract) \
                     .then_inc(sve, 1)
                    c.ve += 1
                    v.wait_ge(sve, c.ve)
                    v.tensor_copy(rcpt[:, 38:39],
                                  stats[:, si + 4:si + 5]).then_inc(sve, 1)
                    c.ve += 1
                rec.vector(_)
                u2_ve = c.ve
                def _(sc):
                    sc.wait_ge(sve, u2_ve)
                    sc.activation(stats[:, si + 5:si + 6],
                                  stats[:, si + 4:si + 5], AF.Ln,
                                  bias=epsc[:], scale=1.0 / D).then_inc(sse, 1)
                    c.se += 1
                    sc.wait_ge(sse, c.se)
                    sc.activation(rcpt[:, 34 + par:35 + par],
                                  stats[:, si + 5:si + 6], AF.Exp,
                                  scale=-0.5).then_inc(sse, 1)
                    c.se += 1
                    sc.wait_ge(sse, c.se)
                    sc.copy(rcpt[:, 36:37],
                            rcpt[:, 34 + par:35 + par]).then_inc(sse, 1)
                    c.se += 1
                rec.scalar(_)
                sig2_se = c.se
                def _(v):
                    v.wait_ge(sse, sig2_se)
                    v.tensor_scalar(ost[par][:], d2t[par][:],
                                    stats[:, si + 3:si + 4],
                                    rcpt[:, 34 + par:35 + par],
                                    ALU.subtract, ALU.mult).then_inc(sve, 1)
                    c.ve += 1
                rec.vector(_)
                n2_ve = c.ve
                def _(sc):
                    sc.wait_ge(sve, n2_ve)
                    sc.dma_start(out=d_out[bidx, 128 * tc:128 * tc + 128, :],
                                 in_=ost[par][:]).then_inc(sw, 16)
                    c.w += 16
                rec.sync(_)
                out_after[i] = c.w

            for i in range(N):
                head(i)
                if i >= 1:
                    tail(i - 1)
            tail(N - 1)
            return c.w

        x_rdy = load_xT(0, 0)
        w_cts = load_w_chunked(d_wqT, 0)
        ve_ = proj_qk(bq, qt, x_rdy, w_cts, consts_ve)
        w_cts = load_w_chunked(d_wkT, c.te)
        ve_ = proj_qk(bk, kt, x_rdy, w_cts, ve_)
        w_cts = load_w_chunked(d_wvT, c.te)
        ve_ = proj_v(x_rdy, w_cts, ve_)
        projV0_te = c.te

        if stop_after == "proj0":
            _finish(nc, rec, c, sw, sx, ste, sve, sse, sgp, sout,
                    d_dbg, DUMPABLE)
            rec.replay(block)
            return nc

        a_ve, a_gp = attention(at0, ve_)
        a_se = c.se
        if stop_after == "attn0":
            _finish(nc, rec, c, sw, sx, ste, sve, sse, sgp, sout,
                    d_dbg, DUMPABLE)
            rec.replay(block)
            return nc

        x_rdy = load_xT(1, projV0_te)
        w_cts = load_w_chunked(d_wqT, c.te, wait_ve=a_ve, wait_gp=a_gp)
        ve_ = proj_qk(bq, qt, x_rdy, w_cts, a_ve)
        w_cts = load_w_chunked(d_wkT, c.te)
        ve_ = proj_qk(bk, kt, x_rdy, w_cts, ve_)
        w_cts = load_w_chunked(d_wvT, c.te)
        ve_ = proj_v(x_rdy, w_cts, ve_)
        projV1_te = c.te
        a_ve, a_gp = attention(at1, ve_)
        a_se = c.se

        wo_rdy = load_w(d_woT, c.te, dst=wofull, wait_ve=a_ve, wait_gp=a_gp)
        ve_, se_ = wo_ln1(0, at0, a_ve, a_se, wo_rdy, projV1_te, a_gp)
        if stop_after == "wo0r":
            _finish(nc, rec, c, sw, sx, ste, sve, sse, sgp, sout,
                    d_dbg, DUMPABLE)
            rec.replay(block)
            return nc
        if stop_after == "wo0":
            _finish(nc, rec, c, sw, sx, ste, sve, sse, sgp, sout,
                    d_dbg, DUMPABLE)
            rec.replay(block)
            return nc
        ve_, se_ = wo_ln1(1, at1, a_ve, a_se, wo_rdy, projV1_te, a_gp)

        wo1_te = c.te
        xs_x = global_scale(c.ve)
        w8_rdy = load_w(d_w8T, wo1_te, dst=w8full)
        def _(g):
            g.wait_ge(ste, wo1_te)
            g.dma_start(out=bff[:], in_=d_bf).then_inc(sx, 16)
            c.x += 16
        rec.gpsimd(_)
        bff_x = c.x
        def _(s):
            s.wait_ge(ste, wo1_te)
            s.dma_start(out=wsc[:], in_=d_ws).then_inc(sw, 16)
            c.w += 16
        rec.sync(_)
        wsc_w = c.w

        ffn(max(xs_x, bff_x), w8_rdy, wsc_w)

        _finish(nc, rec, c, sw, sx, ste, sve, sse, sgp, sout,
                d_dbg, DUMPABLE)
        rec.replay(block)
    return nc


def _finish(nc, rec, c, sw, sx, ste, sve, sse, sgp, sout, d_dbg, DUMPABLE):
    def _(s):
        s.wait_ge(sve, c.ve)
        s.wait_ge(sse, c.se)
        s.wait_ge(ste, c.te)
        for n, dap in d_dbg.items():
            s.dma_start(out=dap, in_=DUMPABLE[n][0][:]).then_inc(sw, 16)
            c.w += 16
        s.wait_ge(sx, c.x)
        s.wait_ge(sw, c.w)
        s.wait_ge(sgp, c.gp)
    rec.sync(_)
    def _(g):
        g.wait_ge(sw, c.w)
    rec.gpsimd(_)


def _prep_shared(inputs):
    f32 = np.float32
    ca = np.ascontiguousarray
    sh = {}
    sh["wqT"] = ca(np.asarray(inputs["L0_attn_Wq"], f32).T)
    sh["wkT"] = ca(np.asarray(inputs["L0_attn_Wk"], f32).T)
    sh["wvT"] = ca(np.asarray(inputs["L0_attn_Wv"], f32).T)
    sh["woT"] = ca(np.asarray(inputs["L0_attn_Wo"], f32).T)
    sh["w8T"] = ca(np.asarray(inputs["L0_ffn_weight_int8"]).T)
    sh["bq_pm"] = ca(np.asarray(inputs["L0_attn_bq"], f32).reshape(NCH, 128).T)
    sh["bk_pm"] = ca(np.asarray(inputs["L0_attn_bk"], f32).reshape(NCH, 128).T)
    sh["bv_bc"] = ca(np.broadcast_to(np.asarray(inputs["L0_attn_bv"], f32), (128, D)))
    sh["bo_bc"] = ca(np.broadcast_to(np.asarray(inputs["L0_attn_bo"], f32), (128, D)))
    sh["bf_bc"] = ca(np.broadcast_to(np.asarray(inputs["L0_ffn_bias"], f32), (128, D)))
    ws = np.asarray(inputs["L0_ffn_weight_scale"], f32)[:, 0]
    sh["ws_bc"] = ca(np.broadcast_to(ws, (128, D)))
    sh["ident"] = np.eye(128, dtype=ml_dtypes.bfloat16)
    sh["mask"] = np.triu(np.ones((128, 128), dtype=ml_dtypes.bfloat16))
    return sh


def kernel(**inputs):
    x = np.asarray(inputs["L0_attn_x"], np.float32)
    sh = _prep_shared(inputs)
    nc = build_program()
    in_maps = []
    for cid in range(8):
        shard = np.ascontiguousarray(x[B * cid:B * cid + B])
        shardT = np.ascontiguousarray(shard.transpose(0, 2, 1))
        m = dict(sh)
        m["x"] = shard
        m["xT"] = shardT
        in_maps.append(m)
    res = run_bass_kernel_spmd(nc, in_maps, core_ids=list(range(8)))
    return np.concatenate([r["out"] for r in res.results], axis=0)


# version 11
# speedup vs baseline: 1.0027x; 1.0027x over previous
import numpy as np
import ml_dtypes
import concourse.bass as bass
import concourse.mybir as mybir
from concourse.bass_utils import run_bass_kernel_spmd
from concourse.hw_specs import get_activation_tables

_ACT_SET_IDX = list(get_activation_tables("gen3").keys()).index(
    "natural_log_exp_and_others")

F32 = mybir.dt.float32
BF16 = mybir.dt.bfloat16
I8 = mybir.dt.int8
AF = mybir.ActivationFunctionType
ALU = mybir.AluOpType
AX = mybir.AxisListType

B = 2
T = 2048
D = 1024
H = 16
HD = 64
NCH = D // 128
KC = T // 128
TC = T // 128
QH = 1024
MAGIC = float(3 * 2 ** 22)
EPS = 1e-5
KB = 1024
VPAIR = 129
VCHUNK = VPAIR * (H // 2)
VW = VCHUNK * KC


def pieces(lo, hi=QH, step=512):
    out, p = [], lo
    while p < hi:
        q = min((p // step + 1) * step, hi)
        out.append((p, q))
        p = q
    return out


class Ctr:
    def __init__(self):
        self.w = 0; self.x = 0; self.te = 0; self.ve = 0
        self.se = 0; self.gp = 0; self.out = 0; self.cc = 0


class _MockIns:
    __slots__ = ("entry",)
    def __init__(self, entry):
        self.entry = entry
    def then_inc(self, sem, n):
        self.entry[3] = (sem, n)
        return self


class _MockEng:
    __slots__ = ("q", "dma")
    def __init__(self, q, dma=None):
        self.q = q
        self.dma = dma
    def __getattr__(self, name):
        q = self.q
        if name == "dma_start" and self.dma is not None:
            sem, cnt = self.dma
            def call_dma(*a, **k):
                if cnt[0] > 0:
                    q.append(["wait_ge", (sem, 16 * cnt[0]), {}, None])
                cnt[0] += 1
                entry = [name, a, k, None]
                q.append(entry)
                return _MockIns(entry)
            return call_dma
        if name == "dma_start_noser" and self.dma is not None:
            sem, cnt = self.dma
            def call_dma_ns(*a, **k):
                cnt[0] += 1
                entry = ["dma_start", a, k, None]
                q.append(entry)
                return _MockIns(entry)
            return call_dma_ns
        def call(*a, **k):
            entry = [name, a, k, None]
            q.append(entry)
            return _MockIns(entry)
        return call


class _Rec:
    def __init__(self):
        self.q = {n: [] for n in ("tensor", "vector", "scalar", "gpsimd", "sync")}
        self.dma_cfg = {}
    def set_dma(self, engine, sem):
        self.dma_cfg[engine] = (sem, [0])
    def tensor(self, f): f(_MockEng(self.q["tensor"]))
    def vector(self, f): f(_MockEng(self.q["vector"]))
    def scalar(self, f): f(_MockEng(self.q["scalar"]))
    def gpsimd(self, f): f(_MockEng(self.q["gpsimd"], self.dma_cfg.get("gpsimd")))
    def sync(self, f): f(_MockEng(self.q["sync"], self.dma_cfg.get("sync")))
    def replay(self, block):
        def mk(q):
            def run(eng):
                for name, a, k, ti in q:
                    ins = getattr(eng, name)(*a, **k)
                    if ti is not None:
                        ins.then_inc(*ti)
            return run
        block.tensor(mk(self.q["tensor"]))
        block.vector(mk(self.q["vector"]))
        block.scalar(mk(self.q["scalar"]))
        block.gpsimd(mk(self.q["gpsimd"]))
        block.sync(mk(self.q["sync"]))


def build_program(stop_after="full", dump=()):
    nc = bass.Bass("TRN2", target_bir_lowering=False, debug=False)

    d_xT = nc.dram_tensor("xT", [B, D, T], F32, kind="ExternalInput").ap()
    d_x = nc.dram_tensor("x", [B, T, D], F32, kind="ExternalInput").ap()
    d_wqT = nc.dram_tensor("wqT", [D, D], F32, kind="ExternalInput").ap()
    d_wkT = nc.dram_tensor("wkT", [D, D], F32, kind="ExternalInput").ap()
    d_wvT = nc.dram_tensor("wvT", [D, D], F32, kind="ExternalInput").ap()
    d_woT = nc.dram_tensor("woT", [D, D], F32, kind="ExternalInput").ap()
    d_w8T = nc.dram_tensor("w8T", [D, D], I8, kind="ExternalInput").ap()
    d_bq = nc.dram_tensor("bq_pm", [128, NCH], F32, kind="ExternalInput").ap()
    d_bk = nc.dram_tensor("bk_pm", [128, NCH], F32, kind="ExternalInput").ap()
    d_bv = nc.dram_tensor("bv_bc", [128, D], F32, kind="ExternalInput").ap()
    d_bo = nc.dram_tensor("bo_bc", [128, D], F32, kind="ExternalInput").ap()
    d_bf = nc.dram_tensor("bf_bc", [128, D], F32, kind="ExternalInput").ap()
    d_ws = nc.dram_tensor("ws_bc", [128, D], F32, kind="ExternalInput").ap()
    d_id = nc.dram_tensor("ident", [128, 128], BF16, kind="ExternalInput").ap()
    d_mk = nc.dram_tensor("mask", [128, 128], BF16, kind="ExternalInput").ap()
    d_out = nc.dram_tensor("out", [B, T, D], F32, kind="ExternalOutput").ap()
    d_cci = nc.dram_tensor("cc_in", [1, 8], F32).ap()
    d_cco = nc.dram_tensor("cc_out", [1, 8], F32).ap()

    BASE = 16512
    at_ = nc.alloc_sbuf_tensor_at
    cur = [BASE]

    def place(name, shape, dtype, align=64):
        sz = int(np.prod(shape[1:])) * mybir.dt.size(dtype)
        cur[0] = (cur[0] + align - 1) // align * align
        h = at_(name, list(shape), dtype, offset=cur[0])
        cur[0] += sz
        return h

    ident = place("ident", [128, 128], BF16)
    mask = place("mask", [128, 128], BF16)
    bq = place("bq", [128, NCH], F32)
    bk = place("bk", [128, NCH], F32)
    bv = place("bv", [128, D], BF16)
    bo = place("bo", [128, D], BF16)
    epsc = place("epsc", [128, 1], F32)
    magicc = place("magicc", [128, 1], F32)
    nmagicc = place("nmagicc", [128, 1], F32)
    xs_col = place("xs_col", [128, 1], F32)
    rxs_col = place("rxs_col", [128, 1], F32)
    stats = place("stats", [128, 256], F32)
    gmaxc = place("gmaxc", [128, 1], F32)
    gmax1 = place("gmax1", [1, 4], F32)
    xsc_tmp = place("xsc_tmp", [1, 4], F32)
    cc_stage = place("cc_stage", [1, 8], F32)
    rcpt = place("rcpt", [128, 40], F32)
    pt0 = place("pt0", [128, QH], BF16)
    pt1 = place("pt1", [128, QH], BF16)
    pt2 = place("pt2", [128, QH], BF16)
    pt3 = place("pt3", [128, QH], BF16)
    assert cur[0] <= BASE + 18 * KB, cur[0]

    UNION = BASE + 18 * KB
    bcastt = [at_("bcast0", [128, QH], F32, offset=UNION),
              at_("bcast1", [128, QH], F32, offset=UNION + 4 * KB)]
    ostg = [at_("ostg0", [128, QH], F32, offset=UNION + 8 * KB),
            at_("ostg1", [128, QH], F32, offset=UNION + 12 * KB),
            at_("ostg2", [128, QH], F32, offset=UNION + 16 * KB),
            at_("ostg3", [128, QH], F32, offset=UNION + 20 * KB)]
    wfull = at_("wfull", [128, NCH * D], BF16, offset=UNION)
    xstage = [at_("xstage0", [128, D], F32, offset=UNION),
              at_("xstage1", [128, D], F32, offset=UNION + 4 * KB)]
    wofull = at_("wofull", [128, NCH * D], BF16, offset=UNION + 8 * KB)
    qtmp = at_("qtmp", [128, D], F32, offset=UNION)
    d1t = at_("d1t", [128, D], F32, offset=UNION + 4 * KB)
    d2t = [at_("d2t0", [128, D], F32, offset=UNION + 8 * KB),
           at_("d2t1", [128, D], F32, offset=UNION + 12 * KB)]
    bff = at_("bff", [128, D], F32, offset=UNION + 16 * KB)
    wsc = at_("wsc", [128, D], F32, offset=UNION + 20 * KB)
    UNION_END = UNION + 24 * KB

    RXT = UNION_END
    xts = at_("xts", [128, NCH * T], BF16, offset=RXT)
    at1 = at_("at1", [128, NCH * T], BF16, offset=RXT)
    w8full = at_("w8full", [128, NCH * D], BF16, offset=RXT)
    xqt = [at_("xqt0", [128, D], BF16, offset=RXT + 16 * KB),
           at_("xqt1", [128, D], BF16, offset=RXT + 18 * KB),
           at_("xqt2", [128, D], BF16, offset=RXT + 20 * KB),
           at_("xqt3", [128, D], BF16, offset=RXT + 22 * KB)]
    ost = [at_("ost0", [128, D], F32, offset=RXT + 24 * KB),
           at_("ost1", [128, D], F32, offset=RXT + 28 * KB)]

    RQ = RXT + 32 * KB
    qt = at_("qt", [128, NCH * T], BF16, offset=RQ)
    RK = RQ + 32 * KB
    kt = at_("kt", [128, NCH * T], BF16, offset=RK)
    ln1_0 = at_("ln1_0", [128, TC * D], F32, offset=RQ)
    RV = RK + 32 * KB
    vaug = at_("vaug", [128, VW], BF16, offset=RV)
    RA0 = RV + 33 * KB
    at0 = at_("at0", [128, NCH * T], BF16, offset=RA0)
    ln1_1 = at_("ln1_1", [128, TC * D], F32, offset=RV)
    assert RA0 + 32 * KB <= 229344, RA0 + 32 * KB

    ATB = [at0, at1]
    LN1 = [ln1_0, ln1_1]
    DUMPABLE = {"qt": (qt, BF16, NCH * T), "kt": (kt, BF16, NCH * T),
                "vaug": (vaug, BF16, VW), "at0": (at0, BF16, NCH * T),
                "at1": (at1, BF16, NCH * T), "ln1_0": (ln1_0, F32, TC * D),
                "ln1_1": (ln1_1, F32, TC * D), "stats": (stats, F32, 256),
                "xs_col": (xs_col, F32, 1)}
    d_dbg = {n: nc.dram_tensor("dbg_" + n, [128, DUMPABLE[n][2]],
                               DUMPABLE[n][1], kind="ExternalOutput").ap()
             for n in dump}

    c = Ctr()

    with (
        nc.psum_tensor("psA", [128, 2048], F32) as psA,
        nc.psum_tensor("psB", [128, 2048], F32) as psB,
        nc.semaphore("sw") as sw,
        nc.semaphore("sx") as sx,
        nc.semaphore("ste") as ste,
        nc.semaphore("sve") as sve,
        nc.semaphore("sse") as sse,
        nc.semaphore("sgp") as sgp,
        nc.semaphore("sout") as sout,
        nc.semaphore("scc") as scc,
        nc.Block() as block,
    ):
        rec = _Rec()
        rec.set_dma('sync', sw)
        rec.set_dma('gpsimd', sx)
        def _(s):
            s.add_instruction(mybir.InstLoadActFuncSet(
                name=nc.get_next_instruction_name(),
                act_func_set_id=_ACT_SET_IDX, ins=[], outs=[]))
        rec.scalar(_)

        def _(s):
            for dst, src in [(ident, d_id), (mask, d_mk), (bq, d_bq), (bk, d_bk)]:
                s.dma_start(out=dst[:], in_=src).then_inc(sw, 16)
                c.w += 16
        rec.sync(_)
        consts_w = c.w

        def _(g):
            g.dma_start(out=bv[:], in_=d_bv).then_inc(sx, 16); c.x += 16
            g.dma_start(out=bo[:], in_=d_bo).then_inc(sx, 16); c.x += 16
        rec.gpsimd(_)
        bias_x = c.x

        def _(v):
            v.memset(epsc[:], EPS)
            v.memset(magicc[:], MAGIC)
            v.memset(nmagicc[:], -MAGIC)
            v.memset(cc_stage[:], 0.0)
            v.memset(bass.AP(vaug, 64, [[VW, 128], [VCHUNK, KC],
                                        [VPAIR, H // 2]]), 1.0).then_inc(sve, 1)
            c.ve += 1
        rec.vector(_)
        consts_ve = c.ve

        def load_xT(b, wait_te):
            def _(g):
                g.wait_ge(ste, wait_te)
                for ch in range(NCH):
                    dma = g.dma_start if ch == 0 else g.dma_start_noser
                    dma(
                        out=xts[:, T * ch:T * ch + T],
                        in_=d_xT[b, 128 * ch:128 * ch + 128, :],
                    ).then_inc(sx, 16)
                    c.x += 16
            rec.gpsimd(_)
            return c.x

        def load_w(src, wait_te, dst=None, wait_ve=None, wait_gp=None):
            dst = wfull if dst is None else dst
            def _(g):
                g.wait_ge(ste, wait_te)
                if wait_ve is not None:
                    g.wait_ge(sve, wait_ve)
                if wait_gp is not None:
                    g.wait_ge(sgp, wait_gp)
                g.dma_start(
                    out=dst.ap().rearrange("p (ch n) -> p ch n", ch=NCH),
                    in_=src.rearrange("(ch p) n -> p ch n", p=128),
                ).then_inc(sx, 16)
                c.x += 16
            rec.gpsimd(_)
            return c.x

        def load_w_chunked(src, wait_te, wait_ve=None, wait_gp=None):
            counts = []
            def _(g):
                for ch in range(NCH):
                    if ch == 0:
                        g.wait_ge(ste, wait_te)
                        if wait_ve is not None:
                            g.wait_ge(sve, wait_ve)
                        if wait_gp is not None:
                            g.wait_ge(sgp, wait_gp)
                    g.dma_start(
                        out=wfull[:, D * ch:D * ch + D],
                        in_=src[128 * ch:128 * ch + 128, :],
                    ).then_inc(sx, 16)
                    c.x += 16
                    counts.append(c.x)
            rec.gpsimd(_)
            return counts

        def proj_qk(bias_col, dst, x_ready, w_counts, prev_ve):
            evac_after = {}
            entry_se = c.se
            for nch in range(NCH):
                arena = psA if nch % 2 == 0 else psB
                def _(t, nch=nch, arena=arena):
                    t.wait_ge(sx, x_ready)
                    if nch < 2:
                        t.wait_ge(sse, entry_se)
                    t.wait_ge(sve, evac_after.get(nch - 2, prev_ve))
                    ins = None
                    for kc in range(NCH):
                        if nch == 0:
                            t.wait_ge(sx, w_counts[kc])
                        for tt in range(4):
                            ins = t.matmul(
                                arena[:, 512 * tt:512 * tt + 512],
                                wfull[:, D * kc + 128 * nch:
                                      D * kc + 128 * nch + 128],
                                xts[:, T * kc + 512 * tt:T * kc + 512 * tt + 512],
                                start=(kc == 0), stop=(kc == NCH - 1))
                    ins.then_inc(ste, 1)
                    c.te += 1
                rec.tensor(_)
                te_done = c.te
                def _(v, nch=nch, arena=arena, te_done=te_done):
                    v.wait_ge(ste, te_done)
                    v.wait_ge(sw, consts_w)
                    ins = None
                    for tt in range(4):
                        ins = v.tensor_scalar(
                            dst[:, T * nch + 512 * tt:T * nch + 512 * tt + 512],
                            arena[:, 512 * tt:512 * tt + 512],
                            bias_col[:, nch:nch + 1], None, ALU.add)
                    ins.then_inc(sve, 1)
                    c.ve += 1
                rec.vector(_)
                evac_after[nch] = c.ve
            return c.ve

        def proj_v(x_ready, w_counts, prev_ve):
            evac_after = {}
            entry_se = c.se
            for tc in range(TC):
                arena = psA if tc % 2 == 0 else psB
                def _(t, tc=tc, arena=arena):
                    t.wait_ge(sx, x_ready)
                    if tc < 2:
                        t.wait_ge(sse, entry_se)
                    t.wait_ge(sve, evac_after.get(tc - 2, prev_ve))
                    ins = None
                    for kc in range(NCH):
                        if tc == 0:
                            t.wait_ge(sx, w_counts[kc])
                        for nt in range(2):
                            ins = t.matmul(
                                arena[:, 512 * nt:512 * nt + 512],
                                xts[:, T * kc + 128 * tc:T * kc + 128 * tc + 128],
                                wfull[:, D * kc + 512 * nt:D * kc + 512 * nt + 512],
                                start=(kc == 0), stop=(kc == NCH - 1))
                    ins.then_inc(ste, 1)
                    c.te += 1
                rec.tensor(_)
                te_done = c.te
                def _(v, tc=tc, arena=arena, te_done=te_done):
                    v.wait_ge(ste, te_done)
                    v.wait_ge(sx, bias_x)
                    ins = None
                    for nt in range(2):
                        base = VCHUNK * tc + VPAIR * 4 * nt
                        dst = bass.AP(vaug, base,
                                      [[VW, 128], [VPAIR, 4], [65, 2], [1, 64]])
                        src = bass.AP(arena.ap().tensor,
                                      arena.ap().offset + 512 * nt,
                                      [[2048, 128], [128, 4], [64, 2], [1, 64]])
                        b_ap = bass.AP(bv, 512 * nt,
                                       [[D, 128], [128, 4], [64, 2], [1, 64]])
                        ins = v.tensor_tensor(dst, src, b_ap, ALU.add)
                    ins.then_inc(sve, 1)
                    c.ve += 1
                rec.vector(_)
                evac_after[tc] = c.ve
            return c.ve

        def attention(at_dst, ready_ve):
            entry_se = c.se
            entry_gp = c.gp
            seg_i = [0]
            half_i = [0]
            exp_after = {}
            pv_after = {}
            norm_after = {}
            stage_after = {}

            class Stream:
                def __init__(self, h, hf):
                    self.h, self.hf = h, hf
                    self.q0h = QH * hf
                    self.js = [j for j in range(KC) if 128 * j < self.q0h + QH]
                    self.hidx = half_i[0]
                    half_i[0] += 1
                    self.osl = self.hidx % 2
                    self.obuf = psB[:, QH * self.osl:QH * self.osl + QH]
                    self.segs = []
                    self.ch = h // 2
                    self.po = 64 * (h % 2)
                    self.pblk = VPAIR * (h // 2 % (H // 2))

            def emit_s(st, j):
                qs = max(st.q0h, 128 * j)
                lo = qs - st.q0h
                si = seg_i[0]
                sp = si % 2
                sps = psA[:, QH * sp:QH * sp + QH]
                ptb = (pt0, pt1, pt2, pt3)[si % 4]
                def _(t):
                    t.wait_ge(sve, ready_ve)
                    if si >= 2:
                        t.wait_ge(sse, exp_after[si - 2])
                    else:
                        t.wait_ge(sse, entry_se)
                    ins = None
                    for (plo, phi) in pieces(lo):
                        ins = t.matmul(
                            sps[:, plo:phi],
                            kt[st.po:st.po + 64,
                               T * st.ch + 128 * j:T * st.ch + 128 * j + 128],
                            qt[st.po:st.po + 64,
                               T * st.ch + st.q0h + plo:T * st.ch + st.q0h + phi],
                            start=True, stop=True)
                    ins.then_inc(ste, 1)
                    c.te += 1
                rec.tensor(_)
                s_done = c.te
                def _(sc):
                    sc.wait_ge(ste, s_done)
                    if si >= 4:
                        sc.wait_ge(ste, pv_after[si - 4])
                    sc.activation(ptb[:, lo:QH], sps[:, lo:QH], AF.Exp,
                                  scale=0.125).then_inc(sse, 1)
                    c.se += 1
                rec.scalar(_)
                exp_after[si] = c.se
                e_done = c.se
                m_done = None
                if qs == 128 * j:
                    def _(v):
                        v.wait_ge(sse, e_done)
                        v.tensor_tensor(ptb[:, lo:lo + 128],
                                        ptb[:, lo:lo + 128], mask[:],
                                        ALU.mult).then_inc(sve, 1)
                        c.ve += 1
                    rec.vector(_)
                    m_done = c.ve
                st.segs.append((j, lo, ptb, e_done, m_done, si))
                seg_i[0] += 1

            def emit_pv(st, k):
                (j, lo, ptb, e_done, m_done, si) = st.segs[k]
                first = (k == 0)
                last_j = st.js[-1]
                def _(t):
                    if first:
                        if st.hidx >= 2:
                            t.wait_ge(sve, stage_after[st.hidx - 2])
                        else:
                            t.wait_ge(sve, ready_ve)
                    if m_done is not None:
                        t.wait_ge(sve, m_done)
                    else:
                        t.wait_ge(sse, e_done)
                    if st.h % 2 == 0:
                        lhs = vaug[:, VCHUNK * j + st.pblk:
                                   VCHUNK * j + st.pblk + 65]
                        nrow = 65
                    else:
                        lhs = vaug[:, VCHUNK * j + st.pblk + 1:
                                   VCHUNK * j + st.pblk + 129]
                        nrow = 128
                    ins = None
                    for (plo, phi) in pieces(lo):
                        bank_last = min((st.q0h + phi - 1) // 128, last_j)
                        ins = t.matmul(st.obuf[0:nrow, plo:phi], lhs,
                                       ptb[:, plo:phi],
                                       start=(j == 0), stop=(j == bank_last))
                    ins.then_inc(ste, 1)
                    c.te += 1
                rec.tensor(_)
                pv_after[si] = c.te

            def emit_norm_stage(st):
                pv_done = pv_after[st.segs[-1][5]]
                stg = ostg[st.hidx % 4]
                even = (st.h % 2 == 0)
                def _(v):
                    v.wait_ge(ste, pv_done)
                    v.wait_ge(sgp, norm_after.get(st.hidx - 4, entry_gp))
                    if even:
                        v.tensor_copy(stg[0:65, :], st.obuf[0:65, :]) \
                         .then_inc(sve, 1)
                    else:
                        v.tensor_copy(stg[32:64, :], st.obuf[32:64, :])
                        v.tensor_copy(stg[64:128, :], st.obuf[64:128, :]) \
                         .then_inc(sve, 1)
                    c.ve += 1
                    if not even:
                        v.wait_ge(sve, c.ve)
                        v.tensor_scalar(stg[32:64, :], stg[32:64, :], 1e-20,
                                        1e18, ALU.max, ALU.min).then_inc(sve, 1)
                        c.ve += 1
                rec.vector(_)
                stage_after[st.hidx] = c.ve
                return c.ve

            def emit_norm_finish(st, cp_done):
                h, ch, q0h = st.h, st.ch, st.q0h
                stg = ostg[st.hidx % 4]
                bcast = bcastt[(st.hidx % 4) // 2]
                even = (h % 2 == 0)
                dr = stg[64:65, :] if even else stg[32:64, :]
                rr = stg[96:97, :] if even else stg[0:32, :]
                def _(sc):
                    sc.wait_ge(sve, cp_done)
                    sc.activation(rr, dr, AF.Ln).then_inc(sse, 1)
                    c.se += 1
                    sc.wait_ge(sse, c.se)
                    sc.activation(rr, rr, AF.Exp, scale=-1.0).then_inc(sse, 1)
                    c.se += 1
                rec.scalar(_)
                r_done = c.se
                r0, r1 = (0, 64) if even else (64, 128)
                def _(sc):
                    sc.wait_ge(sse, r_done)
                    rsrc = stg[96:97, :] if even else stg[31:32, :]
                    sc.dma_start(
                        out=bcast[r0:r1, :],
                        in_=rsrc.unsqueeze(1)
                            .broadcast_to((1, r1 - r0, QH))).then_inc(sw, 16)
                    c.w += 16
                rec.sync(_)
                b_done = c.w
                def _(g):
                    g.wait_ge(sw, b_done)
                    g.tensor_tensor(
                        at_dst[r0:r1, T * ch + q0h:T * ch + q0h + QH],
                        stg[0:64, :] if even else stg[64:128, :],
                        bcast[r0:r1, :], ALU.mult).then_inc(sgp, 1)
                    c.gp += 1
                rec.gpsimd(_)
                norm_after[st.hidx] = c.gp

            pending = []
            for hp in range(0, H, 2):
                for hf in range(2):
                    sa = Stream(hp, hf)
                    sb = Stream(hp + 1, hf)
                    n = len(sa.js)
                    for k in range(n):
                        emit_s(sa, sa.js[k])
                        emit_s(sb, sb.js[k])
                        if k == 2:
                            for st, cp in pending:
                                emit_norm_finish(st, cp)
                            pending = []
                        if k >= 1:
                            emit_pv(sa, k - 1)
                            emit_pv(sb, k - 1)
                    emit_pv(sa, n - 1)
                    emit_pv(sb, n - 1)
                    pending.append((sa, emit_norm_stage(sa)))
                    pending.append((sb, emit_norm_stage(sb)))
            for st, cp in pending:
                emit_norm_finish(st, cp)
            return c.ve, c.gp

        def wo_ln1(bidx, at_src, attn_ve, attn_se, wo_ready, first_xstage_te, attn_gp=0):
            ln1 = LN1[bidx]
            ev_after = {}
            entry_ve = c.ve
            for tc in range(TC):
                arena = psA if tc % 2 == 0 else psB
                xst = xstage[tc % 2]
                def _(s):
                    if tc < 2:
                        s.wait_ge(ste, first_xstage_te)
                        s.wait_ge(sve, entry_ve)
                        s.wait_ge(sgp, attn_gp)
                    else:
                        s.wait_ge(sve, ev_after[tc - 2])
                    s.dma_start(out=xst[:],
                                in_=d_x[bidx, 128 * tc:128 * tc + 128, :]) \
                        .then_inc(sw, 16)
                    c.w += 16
                rec.sync(_)
                x_done = c.w
                def _(t):
                    t.wait_ge(sve, attn_ve)
                    t.wait_ge(sse, attn_se)
                    t.wait_ge(sgp, attn_gp)
                    t.wait_ge(sx, wo_ready)
                    t.wait_ge(sve, ev_after.get(tc - 2, entry_ve))
                    ins = None
                    for dch in range(NCH):
                        for nt in range(2):
                            ins = t.matmul(
                                arena[:, 512 * nt:512 * nt + 512],
                                at_src[:, T * dch + 128 * tc:
                                       T * dch + 128 * tc + 128],
                                wofull[:, D * dch + 512 * nt:
                                       D * dch + 512 * nt + 512],
                                start=(dch == 0), stop=(dch == NCH - 1))
                    ins.then_inc(ste, 1)
                    c.te += 1
                rec.tensor(_)
                mm_done = c.te
                si = bidx * TC + tc
                lt = ln1[:, D * tc:D * tc + D]
                def _(v):
                    v.wait_ge(ste, mm_done)
                    v.wait_ge(sw, x_done)
                    v.tensor_tensor(lt, arena[:, 0:D], bo[:], ALU.add) \
                     .then_inc(sve, 1)
                    c.ve += 1
                    v.wait_ge(sve, c.ve)
                    v.tensor_tensor(lt, lt, xst[:], ALU.add).then_inc(sve, 1)
                    c.ve += 1
                    v.wait_ge(sve, c.ve)
                    v.reduce_sum(stats[:, si:si + 1], lt, axis=AX.X) \
                     .then_inc(sve, 1)
                    c.ve += 1
                rec.vector(_)
                r1_ve = c.ve
                def _(s):
                    s.wait_ge(sve, r1_ve - 1)
                    s.wait_ge(sse, c.se)
                    s.activation(pt0[:], lt, AF.Square,
                                 accum_out=stats[:, 64 + si:64 + si + 1]) \
                     .then_inc(sse, 1)
                    c.se += 1
                rec.scalar(_)
                ev_after[tc] = c.ve
            return c.ve, c.se

        def ln1_var(bidx, prev_ve, prev_se):
            s0 = bidx * TC
            sums = stats[:, s0:s0 + TC]
            sumsq = stats[:, 64 + s0:64 + s0 + TC]
            tcol = stats[:, 128 + s0:128 + s0 + TC]
            ucol = stats[:, 160 + s0:160 + s0 + TC]
            mean = stats[:, 192 + s0:192 + s0 + TC]
            sigc = stats[:, 96 + s0:96 + s0 + TC]
            def _(v):
                v.wait_ge(sve, prev_ve)
                v.wait_ge(sse, prev_se)
                v.scalar_tensor_tensor(tcol, sums, 1.0 / D, sums,
                                       ALU.mult, ALU.mult).then_inc(sve, 1)
                c.ve += 1
                v.tensor_scalar(mean, sums, 1.0 / D, None, ALU.mult) \
                 .then_inc(sve, 1)
                c.ve += 1
                v.wait_ge(sve, c.ve)
                v.tensor_tensor(ucol, sumsq, tcol, ALU.subtract).then_inc(sve, 1)
                c.ve += 1
                v.wait_ge(sve, c.ve)
                v.tensor_copy(rcpt[:, 38:39], ucol[:, 0:1]).then_inc(sve, 1)
                c.ve += 1
            rec.vector(_)
            u_ve = c.ve
            rstd = rcpt[:, 16 * bidx:16 * bidx + TC]
            def _(s):
                s.wait_ge(sve, u_ve)
                s.activation(sigc, ucol, AF.Ln, bias=epsc[:], scale=1.0 / D) \
                 .then_inc(sse, 1)
                c.se += 1
                s.wait_ge(sse, c.se)
                s.activation(rstd, sigc, AF.Exp, scale=-0.5).then_inc(sse, 1)
                c.se += 1
                s.wait_ge(sse, c.se)
                s.copy(rcpt[:, 39:40], rstd[:, 0:1]).then_inc(sse, 1)
                c.se += 1
            rec.scalar(_)
            sig_se = c.se
            nmrs = stats[:, 128 + s0:128 + s0 + TC]
            def _(v):
                v.wait_ge(sse, sig_se)
                v.scalar_tensor_tensor(nmrs, mean, -1.0, rstd,
                                       ALU.mult, ALU.mult).then_inc(sve, 1)
                c.ve += 1
                v.wait_ge(sve, c.ve)
                v.tensor_copy(rcpt[:, 38:39], nmrs[:, 0:1]).then_inc(sve, 1)
                c.ve += 1
            rec.vector(_)
            return c.ve

        def ln1_apply(bidx, prev_ve):
            ln1 = LN1[bidx]
            s0 = bidx * TC
            rstd = rcpt[:, 16 * bidx:16 * bidx + TC]
            nmrs = stats[:, 128 + s0:128 + s0 + TC]
            def _(s):
                s.wait_ge(sve, prev_ve)
                ins = None
                for tc in range(TC):
                    lt = ln1[:, D * tc:D * tc + D]
                    ins = s.activation(lt, lt, AF.Identity,
                                       bias=nmrs[:, tc:tc + 1],
                                       scale=rstd[:, tc:tc + 1])
                ins.then_inc(sse, 1)
                c.se += 1
            rec.scalar(_)
            nrm_se = c.se
            def _(v):
                v.wait_ge(sse, nrm_se)
                ins = None
                for tc in range(TC):
                    lt = ln1[:, D * tc:D * tc + D]
                    ins = v.reduce_max(stats[:, 224 + s0 + tc:224 + s0 + tc + 1],
                                       lt, axis=AX.X, apply_absolute_value=True)
                ins.then_inc(sve, 1)
                c.ve += 1
            rec.vector(_)
            return c.ve

        def global_scale(prev_ve):
            def _(v):
                v.wait_ge(sve, prev_ve)
                v.reduce_max(gmaxc[:], stats[:, 224:224 + 2 * TC], axis=AX.X) \
                 .then_inc(sve, 1)
                c.ve += 1
                v.wait_ge(sve, c.ve)
                v.tensor_copy(rcpt[:, 38:39], gmaxc[:]).then_inc(sve, 1)
                c.ve += 1
            rec.vector(_)
            g_ve = c.ve
            def _(g):
                g.wait_ge(sve, g_ve)
                g.reduce_max(gmax1[0:1, 0:1], gmaxc[:], axis=AX.XYZWC) \
                 .then_inc(sgp, 1)
                c.gp += 1
                g.wait_ge(sgp, c.gp)
                g.tensor_copy(gmax1[0:1, 3:4], gmax1[0:1, 0:1]).then_inc(sgp, 1)
                c.gp += 1
            rec.gpsimd(_)
            g_gp = c.gp
            def _(v):
                v.wait_ge(sgp, g_gp)
                v.tensor_scalar(cc_stage[0:1, 0:1], gmax1[0:1, 0:1],
                                1.0 / 127.0, EPS, ALU.mult, ALU.max) \
                 .then_inc(sve, 1)
                c.ve += 1
                v.wait_ge(sve, c.ve)
                v.tensor_copy(xsc_tmp[0:1, 1:2], cc_stage[0:1, 0:1]) \
                 .then_inc(sve, 1)
                c.ve += 1
            rec.vector(_)
            st_ve = c.ve
            def _(g):
                g.wait_ge(sve, st_ve)
                g.dma_start(out=d_cci, in_=cc_stage[:]).then_inc(sx, 16)
                c.x += 16
                g.wait_ge(sx, c.x)
                g.collective_compute(
                    "AllReduce", ALU.max, replica_groups=[list(range(8))],
                    ins=[d_cci.opt()], outs=[d_cco.opt()]).then_inc(scc, 1)
                c.cc += 1
                g.wait_ge(scc, c.cc)
                g.dma_start(out=xsc_tmp[0:1, 0:1], in_=d_cco[0:1, 0:1]) \
                 .then_inc(sx, 16)
                c.x += 16
                g.wait_ge(sx, c.x)
                g.dma_start(out=xs_col[:],
                            in_=xsc_tmp[0:1, 0:1].unsqueeze(1)
                            .broadcast_to((1, 128, 1))).then_inc(sx, 16)
                c.x += 16
            rec.gpsimd(_)
            xs_x = c.x
            def _(v):
                v.wait_ge(sx, xs_x)
                v.reciprocal(gmax1[0:1, 2:3], xsc_tmp[0:1, 0:1]).then_inc(sve, 1)
                c.ve += 1
                v.wait_ge(sve, c.ve)
                v.tensor_copy(xsc_tmp[0:1, 2:3], gmax1[0:1, 2:3]).then_inc(sve, 1)
                c.ve += 1
            rec.vector(_)
            rx_ve = c.ve
            def _(g):
                g.wait_ge(sve, rx_ve)
                g.dma_start(out=rxs_col[:],
                            in_=gmax1[0:1, 2:3].unsqueeze(1)
                            .broadcast_to((1, 128, 1))).then_inc(sx, 16)
                c.x += 16
            rec.gpsimd(_)
            return c.x

        def ffn(wait_x, w8_ready, wsbf_w):
            N = B * TC
            copy_after = {}
            mm_after = {}
            sqse_after = {}
            deq_after = {}
            out_after = {}
            r2ve_after = {}

            def head(i):
                bidx, tc = i // TC, i % TC
                par4 = i % 4
                lt = LN1[bidx][:, D * tc:D * tc + D]
                ptr = psB[:, 512 * par4:512 * par4 + 512].bitcast(BF16)
                pfa = psA[:, QH * (i % 2):QH * (i % 2) + QH]
                xq = xqt[par4]
                def _(sc):
                    sc.wait_ge(sx, wait_x)
                    sc.wait_ge(sw, wsbf_w)
                    sc.wait_ge(sse, c.se)
                    if i >= 4:
                        sc.wait_ge(ste, mm_after[i - 4])
                    sc.activation(qtmp[:], lt, AF.Identity, bias=magicc[:],
                                  scale=rxs_col[:]).then_inc(sse, 1)
                    c.se += 1
                    sc.wait_ge(sse, c.se)
                    sc.activation(xq[:], qtmp[:], AF.Identity,
                                  bias=nmagicc[:], scale=1.0).then_inc(sse, 1)
                    c.se += 1
                rec.scalar(_)
                q2_se = c.se
                def _(t):
                    t.wait_ge(sse, q2_se)
                    if i >= 4:
                        t.wait_ge(sve, copy_after[i - 4])
                    ins = None
                    for cc8 in range(NCH):
                        ins = t.transpose(ptr[:, 128 * cc8:128 * cc8 + 128],
                                          xq[:, 128 * cc8:128 * cc8 + 128],
                                          ident[:])
                    ins.then_inc(ste, 1)
                    c.te += 1
                rec.tensor(_)
                tr_done = c.te
                def _(v):
                    v.wait_ge(ste, tr_done)
                    v.tensor_copy(xq[:], ptr[:]).then_inc(sve, 1)
                    c.ve += 1
                rec.vector(_)
                copy_after[i] = c.ve
                def _(t):
                    t.wait_ge(sve, copy_after[i])
                    t.wait_ge(sx, w8_ready)
                    if i >= 2:
                        t.wait_ge(sve, deq_after[i - 2])
                    ins = None
                    for kc8 in range(NCH):
                        for nt in range(2):
                            ins = t.matmul(
                                pfa[:, 512 * nt:512 * nt + 512],
                                xq[:, 128 * kc8:128 * kc8 + 128],
                                w8full[:, D * kc8 + 512 * nt:
                                       D * kc8 + 512 * nt + 512],
                                start=(kc8 == 0), stop=(kc8 == NCH - 1))
                    ins.then_inc(ste, 1)
                    c.te += 1
                rec.tensor(_)
                mm_after[i] = c.te

            def tail(i):
                bidx, tc = i // TC, i % TC
                par = i % 2
                lt = LN1[bidx][:, D * tc:D * tc + D]
                pfa = psA[:, QH * par:QH * par + QH]
                si = 240 + 8 * par
                def _(v):
                    v.wait_ge(ste, mm_after[i])
                    if i >= 2:
                        v.wait_ge(sse, sqse_after[i - 2])
                        v.wait_ge(sw, out_after[i - 2])
                    v.scalar_tensor_tensor(d1t[:], pfa[:, 0:D], xs_col[:],
                                           wsc[:], ALU.mult, ALU.mult) \
                     .then_inc(sve, 1)
                    c.ve += 1
                    v.wait_ge(sve, c.ve)
                    v.tensor_tensor(d2t[par][:], d1t[:], bff[:], ALU.add) \
                     .then_inc(sve, 1)
                    c.ve += 1
                    v.wait_ge(sve, c.ve)
                    v.tensor_tensor(d2t[par][:], d2t[par][:], lt, ALU.add) \
                     .then_inc(sve, 1)
                    c.ve += 1
                    v.wait_ge(sve, c.ve)
                    v.reduce_sum(stats[:, si:si + 1], d2t[par][:], axis=AX.X) \
                     .then_inc(sve, 1)
                    c.ve += 1
                rec.vector(_)
                r2_ve = c.ve
                deq_after[i] = c.ve - 3
                r2ve_after[i] = c.ve
                def _(sc):
                    sc.wait_ge(sve, r2_ve - 1)
                    sc.wait_ge(sse, c.se)
                    sc.activation(pt0[:], d2t[par][:], AF.Square,
                                  accum_out=stats[:, si + 1:si + 2]) \
                     .then_inc(sse, 1)
                    c.se += 1
                    sq_se = c.se
                    sqse_after[i] = c.se
                rec.scalar(_)
                sq_se = sqse_after[i]
                def _(v):
                    v.wait_ge(sve, r2_ve)
                    v.wait_ge(sse, sq_se)
                    v.scalar_tensor_tensor(stats[:, si + 2:si + 3],
                                           stats[:, si:si + 1], 1.0 / D,
                                           stats[:, si:si + 1],
                                           ALU.mult, ALU.mult).then_inc(sve, 1)
                    c.ve += 1
                    v.tensor_scalar(stats[:, si + 3:si + 4],
                                    stats[:, si:si + 1], 1.0 / D, None,
                                    ALU.mult).then_inc(sve, 1)
                    c.ve += 1
                    v.wait_ge(sve, c.ve)
                    v.tensor_tensor(stats[:, si + 4:si + 5],
                                    stats[:, si + 1:si + 2],
                                    stats[:, si + 2:si + 3], ALU.subtract) \
                     .then_inc(sve, 1)
                    c.ve += 1
                    v.wait_ge(sve, c.ve)
                    v.tensor_copy(rcpt[:, 38:39],
                                  stats[:, si + 4:si + 5]).then_inc(sve, 1)
                    c.ve += 1
                rec.vector(_)
                u2_ve = c.ve
                def _(sc):
                    sc.wait_ge(sve, u2_ve)
                    sc.activation(stats[:, si + 5:si + 6],
                                  stats[:, si + 4:si + 5], AF.Ln,
                                  bias=epsc[:], scale=1.0 / D).then_inc(sse, 1)
                    c.se += 1
                    sc.wait_ge(sse, c.se)
                    sc.activation(rcpt[:, 34 + par:35 + par],
                                  stats[:, si + 5:si + 6], AF.Exp,
                                  scale=-0.5).then_inc(sse, 1)
                    c.se += 1
                    sc.wait_ge(sse, c.se)
                    sc.copy(rcpt[:, 36:37],
                            rcpt[:, 34 + par:35 + par]).then_inc(sse, 1)
                    c.se += 1
                rec.scalar(_)
                sig2_se = c.se
                def _(v):
                    v.wait_ge(sse, sig2_se)
                    v.tensor_scalar(ost[par][:], d2t[par][:],
                                    stats[:, si + 3:si + 4],
                                    rcpt[:, 34 + par:35 + par],
                                    ALU.subtract, ALU.mult).then_inc(sve, 1)
                    c.ve += 1
                rec.vector(_)
                n2_ve = c.ve
                def _(sc):
                    sc.wait_ge(sve, n2_ve)
                    sc.dma_start(out=d_out[bidx, 128 * tc:128 * tc + 128, :],
                                 in_=ost[par][:]).then_inc(sw, 16)
                    c.w += 16
                rec.sync(_)
                out_after[i] = c.w

            for i in range(N):
                head(i)
                if i >= 1:
                    tail(i - 1)
            tail(N - 1)
            return c.w

        x_rdy = load_xT(0, 0)
        w_cts = load_w_chunked(d_wqT, 0)
        ve_ = proj_qk(bq, qt, x_rdy, w_cts, consts_ve)
        w_cts = load_w_chunked(d_wkT, c.te)
        ve_ = proj_qk(bk, kt, x_rdy, w_cts, ve_)
        w_cts = load_w_chunked(d_wvT, c.te)
        ve_ = proj_v(x_rdy, w_cts, ve_)
        projV0_te = c.te

        if stop_after == "proj0":
            _finish(nc, rec, c, sw, sx, ste, sve, sse, sgp, sout,
                    d_dbg, DUMPABLE)
            rec.replay(block)
            return nc

        a_ve, a_gp = attention(at0, ve_)
        a_se = c.se
        if stop_after == "attn0":
            _finish(nc, rec, c, sw, sx, ste, sve, sse, sgp, sout,
                    d_dbg, DUMPABLE)
            rec.replay(block)
            return nc

        x_rdy = load_xT(1, projV0_te)
        w_cts = load_w_chunked(d_wqT, c.te, wait_ve=a_ve, wait_gp=a_gp)
        ve_ = proj_qk(bq, qt, x_rdy, w_cts, a_ve)
        w_cts = load_w_chunked(d_wkT, c.te)
        ve_ = proj_qk(bk, kt, x_rdy, w_cts, ve_)
        w_cts = load_w_chunked(d_wvT, c.te)
        ve_ = proj_v(x_rdy, w_cts, ve_)
        projV1_te = c.te
        a_ve, a_gp = attention(at1, ve_)
        a_se = c.se

        wo_rdy = load_w(d_woT, c.te, dst=wofull, wait_ve=a_ve, wait_gp=a_gp)
        ve_, se_ = wo_ln1(0, at0, a_ve, a_se, wo_rdy, projV1_te, a_gp)
        if stop_after == "wo0r":
            _finish(nc, rec, c, sw, sx, ste, sve, sse, sgp, sout,
                    d_dbg, DUMPABLE)
            rec.replay(block)
            return nc
        var0_ve = ln1_var(0, ve_, se_)
        if stop_after == "wo0":
            ln1_apply(0, var0_ve)
            _finish(nc, rec, c, sw, sx, ste, sve, sse, sgp, sout,
                    d_dbg, DUMPABLE)
            rec.replay(block)
            return nc
        ve_, se_ = wo_ln1(1, at1, a_ve, a_se, wo_rdy, projV1_te, a_gp)
        ln1_apply(0, var0_ve)
        var1_ve = ln1_var(1, ve_, se_)
        ve_ = ln1_apply(1, var1_ve)

        wo1_te = c.te
        xs_x = global_scale(ve_)
        w8_rdy = load_w(d_w8T, wo1_te, dst=w8full)
        def _(g):
            g.wait_ge(ste, wo1_te)
            g.dma_start(out=bff[:], in_=d_bf).then_inc(sx, 16)
            c.x += 16
        rec.gpsimd(_)
        bff_x = c.x
        def _(s):
            s.wait_ge(ste, wo1_te)
            s.dma_start(out=wsc[:], in_=d_ws).then_inc(sw, 16)
            c.w += 16
        rec.sync(_)
        wsc_w = c.w

        ffn(max(xs_x, bff_x), w8_rdy, wsc_w)

        _finish(nc, rec, c, sw, sx, ste, sve, sse, sgp, sout,
                d_dbg, DUMPABLE)
        rec.replay(block)
    return nc


def _finish(nc, rec, c, sw, sx, ste, sve, sse, sgp, sout, d_dbg, DUMPABLE):
    def _(s):
        s.wait_ge(sve, c.ve)
        s.wait_ge(sse, c.se)
        s.wait_ge(ste, c.te)
        for n, dap in d_dbg.items():
            s.dma_start(out=dap, in_=DUMPABLE[n][0][:]).then_inc(sw, 16)
            c.w += 16
        s.wait_ge(sx, c.x)
        s.wait_ge(sw, c.w)
        s.wait_ge(sgp, c.gp)
    rec.sync(_)
    def _(g):
        g.wait_ge(sw, c.w)
    rec.gpsimd(_)


def _prep_shared(inputs):
    f32 = np.float32
    ca = np.ascontiguousarray
    sh = {}
    sh["wqT"] = ca(np.asarray(inputs["L0_attn_Wq"], f32).T)
    sh["wkT"] = ca(np.asarray(inputs["L0_attn_Wk"], f32).T)
    sh["wvT"] = ca(np.asarray(inputs["L0_attn_Wv"], f32).T)
    sh["woT"] = ca(np.asarray(inputs["L0_attn_Wo"], f32).T)
    sh["w8T"] = ca(np.asarray(inputs["L0_ffn_weight_int8"]).T)
    sh["bq_pm"] = ca(np.asarray(inputs["L0_attn_bq"], f32).reshape(NCH, 128).T)
    sh["bk_pm"] = ca(np.asarray(inputs["L0_attn_bk"], f32).reshape(NCH, 128).T)
    sh["bv_bc"] = ca(np.broadcast_to(np.asarray(inputs["L0_attn_bv"], f32), (128, D)))
    sh["bo_bc"] = ca(np.broadcast_to(np.asarray(inputs["L0_attn_bo"], f32), (128, D)))
    sh["bf_bc"] = ca(np.broadcast_to(np.asarray(inputs["L0_ffn_bias"], f32), (128, D)))
    ws = np.asarray(inputs["L0_ffn_weight_scale"], f32)[:, 0]
    sh["ws_bc"] = ca(np.broadcast_to(ws, (128, D)))
    sh["ident"] = np.eye(128, dtype=ml_dtypes.bfloat16)
    sh["mask"] = np.triu(np.ones((128, 128), dtype=ml_dtypes.bfloat16))
    return sh


def kernel(**inputs):
    x = np.asarray(inputs["L0_attn_x"], np.float32)
    sh = _prep_shared(inputs)
    nc = build_program()
    in_maps = []
    for cid in range(8):
        shard = np.ascontiguousarray(x[B * cid:B * cid + B])
        shardT = np.ascontiguousarray(shard.transpose(0, 2, 1))
        m = dict(sh)
        m["x"] = shard
        m["xT"] = shardT
        in_maps.append(m)
    res = run_bass_kernel_spmd(nc, in_maps, core_ids=list(range(8)))
    return np.concatenate([r["out"] for r in res.results], axis=0)


# version 12
# speedup vs baseline: 1.0672x; 1.0643x over previous
import numpy as np
import ml_dtypes
import concourse.bass as bass
import concourse.mybir as mybir
from concourse.bass_utils import run_bass_kernel_spmd
from concourse.hw_specs import get_activation_tables

_ACT_SET_IDX = list(get_activation_tables("gen3").keys()).index(
    "natural_log_exp_and_others")

F32 = mybir.dt.float32
BF16 = mybir.dt.bfloat16
I8 = mybir.dt.int8
AF = mybir.ActivationFunctionType
ALU = mybir.AluOpType
AX = mybir.AxisListType

B = 2
T = 2048
D = 1024
H = 16
HD = 64
NCH = D // 128
KC = T // 128
TC = T // 128
QH = 1024
MAGIC = float(3 * 2 ** 22)
EPS = 1e-5
KB = 1024
VPAIR = 129
VCHUNK = VPAIR * (H // 2)
VW = VCHUNK * KC


def pieces(lo, hi=QH, step=512):
    out, p = [], lo
    while p < hi:
        q = min((p // step + 1) * step, hi)
        out.append((p, q))
        p = q
    return out


class Ctr:
    def __init__(self):
        self.w = 0; self.x = 0; self.te = 0; self.ve = 0
        self.se = 0; self.gp = 0; self.out = 0; self.cc = 0
        self.nb = 0


class _MockIns:
    __slots__ = ("entry",)
    def __init__(self, entry):
        self.entry = entry
    def then_inc(self, sem, n):
        self.entry[3] = (sem, n)
        return self


class _MockEng:
    __slots__ = ("q", "dma")
    def __init__(self, q, dma=None):
        self.q = q
        self.dma = dma
    def __getattr__(self, name):
        q = self.q
        if name == "dma_start" and self.dma is not None:
            sem, cnt = self.dma
            def call_dma(*a, **k):
                if cnt[0] > 0:
                    q.append(["wait_ge", (sem, 16 * cnt[0]), {}, None])
                cnt[0] += 1
                entry = [name, a, k, None]
                q.append(entry)
                return _MockIns(entry)
            return call_dma
        if name == "dma_start_noser" and self.dma is not None:
            sem, cnt = self.dma
            def call_dma_ns(*a, **k):
                cnt[0] += 1
                entry = ["dma_start", a, k, None]
                q.append(entry)
                return _MockIns(entry)
            return call_dma_ns
        if name == "dma_start_alien":
            def call_dma_al(*a, **k):
                entry = ["dma_start", a, k, None]
                q.append(entry)
                return _MockIns(entry)
            return call_dma_al
        def call(*a, **k):
            entry = [name, a, k, None]
            q.append(entry)
            return _MockIns(entry)
        return call


class _Rec:
    def __init__(self):
        self.q = {n: [] for n in ("tensor", "vector", "scalar", "gpsimd", "sync")}
        self.dma_cfg = {}
    def set_dma(self, engine, sem):
        self.dma_cfg[engine] = (sem, [0])
    def tensor(self, f): f(_MockEng(self.q["tensor"]))
    def vector(self, f): f(_MockEng(self.q["vector"]))
    def scalar(self, f): f(_MockEng(self.q["scalar"]))
    def gpsimd(self, f): f(_MockEng(self.q["gpsimd"], self.dma_cfg.get("gpsimd")))
    def sync(self, f): f(_MockEng(self.q["sync"], self.dma_cfg.get("sync")))
    def replay(self, block):
        def mk(q):
            def run(eng):
                for name, a, k, ti in q:
                    ins = getattr(eng, name)(*a, **k)
                    if ti is not None:
                        ins.then_inc(*ti)
            return run
        block.tensor(mk(self.q["tensor"]))
        block.vector(mk(self.q["vector"]))
        block.scalar(mk(self.q["scalar"]))
        block.gpsimd(mk(self.q["gpsimd"]))
        block.sync(mk(self.q["sync"]))


def build_program(stop_after="full", dump=()):
    nc = bass.Bass("TRN2", target_bir_lowering=False, debug=False)

    d_xT = nc.dram_tensor("xT", [B, D, T], F32, kind="ExternalInput").ap()
    d_x = nc.dram_tensor("x", [B, T, D], F32, kind="ExternalInput").ap()
    d_wqT = nc.dram_tensor("wqT", [D, D], F32, kind="ExternalInput").ap()
    d_wkT = nc.dram_tensor("wkT", [D, D], F32, kind="ExternalInput").ap()
    d_wvT = nc.dram_tensor("wvT", [D, D], F32, kind="ExternalInput").ap()
    d_woT = nc.dram_tensor("woT", [D, D], F32, kind="ExternalInput").ap()
    d_w8T = nc.dram_tensor("w8T", [D, D], I8, kind="ExternalInput").ap()
    d_bq = nc.dram_tensor("bq_pm", [128, NCH], F32, kind="ExternalInput").ap()
    d_bk = nc.dram_tensor("bk_pm", [128, NCH], F32, kind="ExternalInput").ap()
    d_bv = nc.dram_tensor("bv_bc", [128, D], F32, kind="ExternalInput").ap()
    d_bo = nc.dram_tensor("bo_bc", [128, D], F32, kind="ExternalInput").ap()
    d_bf = nc.dram_tensor("bf_bc", [128, D], F32, kind="ExternalInput").ap()
    d_ws = nc.dram_tensor("ws_bc", [128, D], F32, kind="ExternalInput").ap()
    d_id = nc.dram_tensor("ident", [128, 128], BF16, kind="ExternalInput").ap()
    d_mk = nc.dram_tensor("mask", [128, 128], BF16, kind="ExternalInput").ap()
    d_out = nc.dram_tensor("out", [B, T, D], F32, kind="ExternalOutput").ap()
    d_cci = nc.dram_tensor("cc_in", [1, 8], F32).ap()
    d_cco = nc.dram_tensor("cc_out", [1, 8], F32).ap()

    BASE = 16512
    at_ = nc.alloc_sbuf_tensor_at
    cur = [BASE]

    def place(name, shape, dtype, align=64):
        sz = int(np.prod(shape[1:])) * mybir.dt.size(dtype)
        cur[0] = (cur[0] + align - 1) // align * align
        h = at_(name, list(shape), dtype, offset=cur[0])
        cur[0] += sz
        return h

    ident = place("ident", [128, 128], BF16)
    mask = place("mask", [128, 128], BF16)
    bq = place("bq", [128, NCH], F32)
    bk = place("bk", [128, NCH], F32)
    bv = place("bv", [128, D], BF16)
    bo = place("bo", [128, D], BF16)
    epsc = place("epsc", [128, 1], F32)
    magicc = place("magicc", [128, 1], F32)
    nmagicc = place("nmagicc", [128, 1], F32)
    xs_col = place("xs_col", [128, 1], F32)
    rxs_col = place("rxs_col", [128, 1], F32)
    stats = place("stats", [128, 256], F32)
    gmaxc = place("gmaxc", [128, 1], F32)
    gmax1 = place("gmax1", [1, 4], F32)
    xsc_tmp = place("xsc_tmp", [1, 4], F32)
    cc_stage = place("cc_stage", [1, 8], F32)
    rcpt = place("rcpt", [128, 40], F32)
    pt0 = place("pt0", [128, QH], BF16)
    pt1 = place("pt1", [128, QH], BF16)
    PT23 = (cur[0] + 63) // 64 * 64
    pt2 = place("pt2", [128, QH], BF16)
    pt3 = place("pt3", [128, QH], BF16)
    tail_bc = at_("tail_bc", [128, QH], F32, offset=PT23)
    assert cur[0] <= BASE + 18 * KB, cur[0]

    UNION = BASE + 18 * KB
    bcastt = [at_("bcast0", [128, QH], F32, offset=UNION),
              at_("bcast1", [128, QH], F32, offset=UNION + 4 * KB)]
    ostg = [at_("ostg0", [128, QH], F32, offset=UNION + 8 * KB),
            at_("ostg1", [128, QH], F32, offset=UNION + 12 * KB),
            at_("ostg2", [128, QH], F32, offset=UNION + 16 * KB),
            at_("ostg3", [128, QH], F32, offset=UNION + 20 * KB)]
    wfull = at_("wfull", [128, NCH * D], BF16, offset=UNION)
    xstage = [at_("xstage0", [128, D], F32, offset=UNION),
              at_("xstage1", [128, D], F32, offset=UNION + 4 * KB)]
    wofull = at_("wofull", [128, NCH * D], BF16, offset=UNION + 8 * KB)
    qtmp = at_("qtmp", [128, D], F32, offset=UNION)
    d1t = at_("d1t", [128, D], F32, offset=UNION + 4 * KB)
    d2t = [at_("d2t0", [128, D], F32, offset=UNION + 8 * KB),
           at_("d2t1", [128, D], F32, offset=UNION + 12 * KB)]
    bff = at_("bff", [128, D], F32, offset=UNION + 16 * KB)
    wsc = at_("wsc", [128, D], F32, offset=UNION + 20 * KB)
    UNION_END = UNION + 24 * KB

    RXT = UNION_END
    xts = at_("xts", [128, NCH * T], BF16, offset=RXT)
    at1 = at_("at1", [128, NCH * T], BF16, offset=RXT)
    w8full = at_("w8full", [128, NCH * D], BF16, offset=RXT)
    xqt = [at_("xqt0", [128, D], BF16, offset=RXT + 16 * KB),
           at_("xqt1", [128, D], BF16, offset=RXT + 18 * KB),
           at_("xqt2", [128, D], BF16, offset=RXT + 20 * KB),
           at_("xqt3", [128, D], BF16, offset=RXT + 22 * KB)]
    ost = [at_("ost0", [128, D], F32, offset=RXT + 24 * KB),
           at_("ost1", [128, D], F32, offset=RXT + 28 * KB)]

    RQ = RXT + 32 * KB
    qt = at_("qt", [128, NCH * T], BF16, offset=RQ)
    RK = RQ + 32 * KB
    kt = at_("kt", [128, NCH * T], BF16, offset=RK)
    ln1_0 = at_("ln1_0", [128, TC * D], F32, offset=RQ)
    RV = RK + 32 * KB
    vaug = at_("vaug", [128, VW], BF16, offset=RV)
    tail_s = [at_("tail_s0", [128, QH], F32, offset=RV),
              at_("tail_s1", [128, QH], F32, offset=RV + 4 * KB)]
    RA0 = RV + 33 * KB
    at0 = at_("at0", [128, NCH * T], BF16, offset=RA0)
    ln1_1 = at_("ln1_1", [128, TC * D], F32, offset=RV)
    assert RA0 + 32 * KB <= 229344, RA0 + 32 * KB

    ATB = [at0, at1]
    LN1 = [ln1_0, ln1_1]
    DUMPABLE = {"qt": (qt, BF16, NCH * T), "kt": (kt, BF16, NCH * T),
                "vaug": (vaug, BF16, VW), "at0": (at0, BF16, NCH * T),
                "at1": (at1, BF16, NCH * T), "ln1_0": (ln1_0, F32, TC * D),
                "ln1_1": (ln1_1, F32, TC * D), "stats": (stats, F32, 256),
                "xs_col": (xs_col, F32, 1)}
    d_dbg = {n: nc.dram_tensor("dbg_" + n, [128, DUMPABLE[n][2]],
                               DUMPABLE[n][1], kind="ExternalOutput").ap()
             for n in dump}

    c = Ctr()

    with (
        nc.psum_tensor("psA", [128, 2048], F32) as psA,
        nc.psum_tensor("psB", [128, 2048], F32) as psB,
        nc.semaphore("sw") as sw,
        nc.semaphore("sx") as sx,
        nc.semaphore("ste") as ste,
        nc.semaphore("sve") as sve,
        nc.semaphore("sse") as sse,
        nc.semaphore("sgp") as sgp,
        nc.semaphore("sout") as sout,
        nc.semaphore("scc") as scc,
        nc.semaphore("snb") as snb,
        nc.Block() as block,
    ):
        rec = _Rec()
        rec.set_dma('sync', sw)
        rec.set_dma('gpsimd', sx)
        def _(s):
            s.add_instruction(mybir.InstLoadActFuncSet(
                name=nc.get_next_instruction_name(),
                act_func_set_id=_ACT_SET_IDX, ins=[], outs=[]))
        rec.scalar(_)

        def _(s):
            for dst, src in [(ident, d_id), (mask, d_mk), (bq, d_bq), (bk, d_bk)]:
                s.dma_start(out=dst[:], in_=src).then_inc(sw, 16)
                c.w += 16
        rec.sync(_)
        consts_w = c.w

        def _(g):
            g.dma_start(out=bv[:], in_=d_bv).then_inc(sx, 16); c.x += 16
            g.dma_start(out=bo[:], in_=d_bo).then_inc(sx, 16); c.x += 16
        rec.gpsimd(_)
        bias_x = c.x

        def _(v):
            v.memset(epsc[:], EPS)
            v.memset(magicc[:], MAGIC)
            v.memset(nmagicc[:], -MAGIC)
            v.memset(cc_stage[:], 0.0)
            v.memset(bass.AP(vaug, 64, [[VW, 128], [VCHUNK, KC],
                                        [VPAIR, H // 2]]), 1.0).then_inc(sve, 1)
            c.ve += 1
        rec.vector(_)
        consts_ve = c.ve

        def load_xT(b, wait_te):
            def _(g):
                g.wait_ge(ste, wait_te)
                for ch in range(NCH):
                    dma = g.dma_start if ch == 0 else g.dma_start_noser
                    dma(
                        out=xts[:, T * ch:T * ch + T],
                        in_=d_xT[b, 128 * ch:128 * ch + 128, :],
                    ).then_inc(sx, 16)
                    c.x += 16
            rec.gpsimd(_)
            return c.x

        def load_w(src, wait_te, dst=None, wait_ve=None, wait_gp=None):
            dst = wfull if dst is None else dst
            def _(g):
                g.wait_ge(ste, wait_te)
                if wait_ve is not None:
                    g.wait_ge(sve, wait_ve)
                if wait_gp is not None:
                    g.wait_ge(sgp, wait_gp)
                g.dma_start(
                    out=dst.ap().rearrange("p (ch n) -> p ch n", ch=NCH),
                    in_=src.rearrange("(ch p) n -> p ch n", p=128),
                ).then_inc(sx, 16)
                c.x += 16
            rec.gpsimd(_)
            return c.x

        def load_w_chunked(src, wait_te, wait_ve=None, wait_gp=None):
            counts = []
            def _(g):
                for ch in range(NCH):
                    if ch == 0:
                        g.wait_ge(ste, wait_te)
                        if wait_ve is not None:
                            g.wait_ge(sve, wait_ve)
                        if wait_gp is not None:
                            g.wait_ge(sgp, wait_gp)
                    g.dma_start(
                        out=wfull[:, D * ch:D * ch + D],
                        in_=src[128 * ch:128 * ch + 128, :],
                    ).then_inc(sx, 16)
                    c.x += 16
                    counts.append(c.x)
            rec.gpsimd(_)
            return counts

        def proj_qk(bias_col, dst, x_ready, w_counts, prev_ve):
            evac_after = {}
            entry_se = c.se
            for nch in range(NCH):
                arena = psA if nch % 2 == 0 else psB
                def _(t, nch=nch, arena=arena):
                    t.wait_ge(sx, x_ready)
                    if nch < 2:
                        t.wait_ge(sse, entry_se)
                    t.wait_ge(sve, evac_after.get(nch - 2, prev_ve))
                    ins = None
                    for kc in range(NCH):
                        if nch == 0:
                            t.wait_ge(sx, w_counts[kc])
                        for tt in range(4):
                            ins = t.matmul(
                                arena[:, 512 * tt:512 * tt + 512],
                                wfull[:, D * kc + 128 * nch:
                                      D * kc + 128 * nch + 128],
                                xts[:, T * kc + 512 * tt:T * kc + 512 * tt + 512],
                                start=(kc == 0), stop=(kc == NCH - 1))
                    ins.then_inc(ste, 1)
                    c.te += 1
                rec.tensor(_)
                te_done = c.te
                def _(v, nch=nch, arena=arena, te_done=te_done):
                    v.wait_ge(ste, te_done)
                    v.wait_ge(sw, consts_w)
                    ins = None
                    for tt in range(4):
                        ins = v.tensor_scalar(
                            dst[:, T * nch + 512 * tt:T * nch + 512 * tt + 512],
                            arena[:, 512 * tt:512 * tt + 512],
                            bias_col[:, nch:nch + 1], None, ALU.add)
                    ins.then_inc(sve, 1)
                    c.ve += 1
                rec.vector(_)
                evac_after[nch] = c.ve
            return c.ve

        def proj_v(x_ready, w_counts, prev_ve, tail_gp=0):
            evac_after = {}
            entry_se = c.se
            if tail_gp:
                def _(v):
                    v.wait_ge(sgp, tail_gp)
                    v.memset(bass.AP(vaug, 64, [[VW, 128], [VCHUNK, KC],
                                                [VPAIR, H // 2]]), 1.0) \
                     .then_inc(sve, 1)
                    c.ve += 1
                rec.vector(_)
            for tc in range(TC):
                arena = psA if tc % 2 == 0 else psB
                def _(t, tc=tc, arena=arena):
                    t.wait_ge(sx, x_ready)
                    if tc < 2:
                        t.wait_ge(sse, entry_se)
                    t.wait_ge(sve, evac_after.get(tc - 2, prev_ve))
                    ins = None
                    for kc in range(NCH):
                        if tc == 0:
                            t.wait_ge(sx, w_counts[kc])
                        for nt in range(2):
                            ins = t.matmul(
                                arena[:, 512 * nt:512 * nt + 512],
                                xts[:, T * kc + 128 * tc:T * kc + 128 * tc + 128],
                                wfull[:, D * kc + 512 * nt:D * kc + 512 * nt + 512],
                                start=(kc == 0), stop=(kc == NCH - 1))
                    ins.then_inc(ste, 1)
                    c.te += 1
                rec.tensor(_)
                te_done = c.te
                def _(v, tc=tc, arena=arena, te_done=te_done):
                    v.wait_ge(ste, te_done)
                    v.wait_ge(sx, bias_x)
                    ins = None
                    for nt in range(2):
                        base = VCHUNK * tc + VPAIR * 4 * nt
                        dst = bass.AP(vaug, base,
                                      [[VW, 128], [VPAIR, 4], [65, 2], [1, 64]])
                        src = bass.AP(arena.ap().tensor,
                                      arena.ap().offset + 512 * nt,
                                      [[2048, 128], [128, 4], [64, 2], [1, 64]])
                        b_ap = bass.AP(bv, 512 * nt,
                                       [[D, 128], [128, 4], [64, 2], [1, 64]])
                        ins = v.tensor_tensor(dst, src, b_ap, ALU.add)
                    ins.then_inc(sve, 1)
                    c.ve += 1
                rec.vector(_)
                evac_after[tc] = c.ve
            return c.ve

        def attention(at_dst, ready_ve):
            entry_se = c.se
            entry_gp = c.gp
            seg_i = [0]
            half_i = [0]
            exp_after = {}
            pv_after = {}
            norm_after = {}
            stage_after = {}

            class Stream:
                def __init__(self, h, hf):
                    self.h, self.hf = h, hf
                    self.q0h = QH * hf
                    self.js = [j for j in range(KC) if 128 * j < self.q0h + QH]
                    self.hidx = half_i[0]
                    half_i[0] += 1
                    self.osl = self.hidx % 2
                    self.obuf = psB[:, QH * self.osl:QH * self.osl + QH]
                    self.segs = []
                    self.ch = h // 2
                    self.po = 64 * (h % 2)
                    self.pblk = VPAIR * (h // 2 % (H // 2))

            def emit_s(st, j):
                qs = max(st.q0h, 128 * j)
                lo = qs - st.q0h
                si = seg_i[0]
                sp = si % 2
                sps = psA[:, QH * sp:QH * sp + QH]
                ptb = (pt0, pt1, pt2, pt3)[si % 4]
                def _(t):
                    t.wait_ge(sve, ready_ve)
                    if si >= 2:
                        t.wait_ge(sse, exp_after[si - 2])
                    else:
                        t.wait_ge(sse, entry_se)
                    ins = None
                    for (plo, phi) in pieces(lo):
                        ins = t.matmul(
                            sps[:, plo:phi],
                            kt[st.po:st.po + 64,
                               T * st.ch + 128 * j:T * st.ch + 128 * j + 128],
                            qt[st.po:st.po + 64,
                               T * st.ch + st.q0h + plo:T * st.ch + st.q0h + phi],
                            start=True, stop=True)
                    ins.then_inc(ste, 1)
                    c.te += 1
                rec.tensor(_)
                s_done = c.te
                def _(sc):
                    sc.wait_ge(ste, s_done)
                    if si >= 4:
                        sc.wait_ge(ste, pv_after[si - 4])
                    sc.activation(ptb[:, lo:QH], sps[:, lo:QH], AF.Exp,
                                  scale=0.125).then_inc(sse, 1)
                    c.se += 1
                rec.scalar(_)
                exp_after[si] = c.se
                e_done = c.se
                m_done = None
                if qs == 128 * j:
                    def _(v):
                        v.wait_ge(sse, e_done)
                        v.tensor_tensor(ptb[:, lo:lo + 128],
                                        ptb[:, lo:lo + 128], mask[:],
                                        ALU.mult).then_inc(sve, 1)
                        c.ve += 1
                    rec.vector(_)
                    m_done = c.ve
                st.segs.append((j, lo, ptb, e_done, m_done, si))
                seg_i[0] += 1

            def emit_pv(st, k):
                (j, lo, ptb, e_done, m_done, si) = st.segs[k]
                first = (k == 0)
                last_j = st.js[-1]
                def _(t):
                    if first:
                        if st.hidx >= 2:
                            t.wait_ge(sve, stage_after[st.hidx - 2])
                        else:
                            t.wait_ge(sve, ready_ve)
                    if m_done is not None:
                        t.wait_ge(sve, m_done)
                    else:
                        t.wait_ge(sse, e_done)
                    if st.h % 2 == 0:
                        lhs = vaug[:, VCHUNK * j + st.pblk:
                                   VCHUNK * j + st.pblk + 65]
                        nrow = 65
                    else:
                        lhs = vaug[:, VCHUNK * j + st.pblk + 1:
                                   VCHUNK * j + st.pblk + 129]
                        nrow = 128
                    ins = None
                    for (plo, phi) in pieces(lo):
                        bank_last = min((st.q0h + phi - 1) // 128, last_j)
                        ins = t.matmul(st.obuf[0:nrow, plo:phi], lhs,
                                       ptb[:, plo:phi],
                                       start=(j == 0), stop=(j == bank_last))
                    ins.then_inc(ste, 1)
                    c.te += 1
                rec.tensor(_)
                pv_after[si] = c.te

            def emit_norm_stage(st):
                pv_done = pv_after[st.segs[-1][5]]
                stg = tail_s[st.hidx - 2 * H + 2] if st.hidx >= 2 * H - 2 \
                    else ostg[st.hidx % 4]
                even = (st.h % 2 == 0)
                def _(v):
                    v.wait_ge(ste, pv_done)
                    v.wait_ge(sgp, norm_after.get(st.hidx - 4, entry_gp))
                    if even:
                        v.tensor_copy(stg[0:65, :], st.obuf[0:65, :]) \
                         .then_inc(sve, 1)
                    else:
                        v.tensor_copy(stg[32:64, :], st.obuf[32:64, :])
                        v.tensor_copy(stg[64:128, :], st.obuf[64:128, :]) \
                         .then_inc(sve, 1)
                    c.ve += 1
                    if not even:
                        v.wait_ge(sve, c.ve)
                        v.tensor_scalar(stg[32:64, :], stg[32:64, :], 1e-20,
                                        1e18, ALU.max, ALU.min).then_inc(sve, 1)
                        c.ve += 1
                rec.vector(_)
                stage_after[st.hidx] = c.ve
                return c.ve

            def emit_norm_finish(st, cp_done):
                h, ch, q0h = st.h, st.ch, st.q0h
                tail = st.hidx >= 2 * H - 2
                stg = tail_s[st.hidx - 2 * H + 2] if tail \
                    else ostg[st.hidx % 4]
                bcast = tail_bc if tail else bcastt[(st.hidx % 4) // 2]
                even = (h % 2 == 0)
                dr = stg[64:65, :] if even else stg[32:64, :]
                rr = stg[96:97, :] if even else stg[0:32, :]
                def _(sc):
                    sc.wait_ge(sve, cp_done)
                    sc.activation(rr, dr, AF.Ln).then_inc(sse, 1)
                    c.se += 1
                    sc.wait_ge(sse, c.se)
                    sc.activation(rr, rr, AF.Exp, scale=-1.0).then_inc(sse, 1)
                    c.se += 1
                rec.scalar(_)
                r_done = c.se
                r0, r1 = (0, 64) if even else (64, 128)
                if tail:
                    tail_te = c.te
                    def _(g):
                        g.wait_ge(sse, r_done)
                        g.wait_ge(ste, tail_te)
                        rsrc = stg[96:97, :] if even else stg[31:32, :]
                        g.dma_start_alien(
                            out=bcast[r0:r1, :],
                            in_=rsrc.unsqueeze(1)
                                .broadcast_to((1, r1 - r0, QH))) \
                         .then_inc(snb, 16)
                        c.nb += 16
                    rec.gpsimd(_)
                    b_done = c.nb
                else:
                    def _(sc):
                        sc.wait_ge(sse, r_done)
                        rsrc = stg[96:97, :] if even else stg[31:32, :]
                        sc.dma_start(
                            out=bcast[r0:r1, :],
                            in_=rsrc.unsqueeze(1)
                                .broadcast_to((1, r1 - r0, QH))) \
                         .then_inc(sw, 16)
                        c.w += 16
                    rec.sync(_)
                    b_done = c.w
                def _(g):
                    g.wait_ge(snb if tail else sw, b_done)
                    g.tensor_tensor(
                        at_dst[r0:r1, T * ch + q0h:T * ch + q0h + QH],
                        stg[0:64, :] if even else stg[64:128, :],
                        bcast[r0:r1, :], ALU.mult).then_inc(sgp, 1)
                    c.gp += 1
                rec.gpsimd(_)
                norm_after[st.hidx] = c.gp

            pending = []
            for hp in range(0, H, 2):
                for hf in range(2):
                    sa = Stream(hp, hf)
                    sb = Stream(hp + 1, hf)
                    n = len(sa.js)
                    for k in range(n):
                        emit_s(sa, sa.js[k])
                        emit_s(sb, sb.js[k])
                        if k == 2:
                            for st, cp in pending:
                                emit_norm_finish(st, cp)
                            pending = []
                        if k >= 1:
                            emit_pv(sa, k - 1)
                            emit_pv(sb, k - 1)
                    emit_pv(sa, n - 1)
                    emit_pv(sb, n - 1)
                    pending.append((sa, emit_norm_stage(sa)))
                    pending.append((sb, emit_norm_stage(sb)))
            thunks = [(lambda st=st, cp=cp: emit_norm_finish(st, cp))
                      for st, cp in pending]
            return c.ve, c.gp, thunks

        def wo_ln1(bidx, at_src, attn_ve, attn_se, wo_ready, first_xstage_te, attn_gp=0):
            ln1 = LN1[bidx]
            ev_after = {}
            entry_ve = c.ve
            for tc in range(TC):
                arena = psA if tc % 2 == 0 else psB
                xst = xstage[tc % 2]
                def _(s):
                    if tc < 2:
                        s.wait_ge(ste, first_xstage_te)
                        s.wait_ge(sve, entry_ve)
                        s.wait_ge(sgp, attn_gp)
                    else:
                        s.wait_ge(sve, ev_after[tc - 2])
                    s.dma_start(out=xst[:],
                                in_=d_x[bidx, 128 * tc:128 * tc + 128, :]) \
                        .then_inc(sw, 16)
                    c.w += 16
                rec.sync(_)
                x_done = c.w
                def _(t):
                    t.wait_ge(sve, attn_ve)
                    t.wait_ge(sse, attn_se)
                    t.wait_ge(sgp, attn_gp)
                    t.wait_ge(sx, wo_ready)
                    t.wait_ge(sve, ev_after.get(tc - 2, entry_ve))
                    ins = None
                    for dch in range(NCH):
                        for nt in range(2):
                            ins = t.matmul(
                                arena[:, 512 * nt:512 * nt + 512],
                                at_src[:, T * dch + 128 * tc:
                                       T * dch + 128 * tc + 128],
                                wofull[:, D * dch + 512 * nt:
                                       D * dch + 512 * nt + 512],
                                start=(dch == 0), stop=(dch == NCH - 1))
                    ins.then_inc(ste, 1)
                    c.te += 1
                rec.tensor(_)
                mm_done = c.te
                si = bidx * TC + tc
                lt = ln1[:, D * tc:D * tc + D]
                def _(v):
                    v.wait_ge(ste, mm_done)
                    v.wait_ge(sw, x_done)
                    v.tensor_tensor(lt, arena[:, 0:D], bo[:], ALU.add) \
                     .then_inc(sve, 1)
                    c.ve += 1
                    v.wait_ge(sve, c.ve)
                    v.tensor_tensor(lt, lt, xst[:], ALU.add).then_inc(sve, 1)
                    c.ve += 1
                    v.wait_ge(sve, c.ve)
                    v.reduce_sum(stats[:, si:si + 1], lt, axis=AX.X) \
                     .then_inc(sve, 1)
                    c.ve += 1
                rec.vector(_)
                r1_ve = c.ve
                def _(s):
                    s.wait_ge(sve, r1_ve - 1)
                    s.wait_ge(sse, c.se)
                    s.activation(pt0[:], lt, AF.Square,
                                 accum_out=stats[:, 64 + si:64 + si + 1]) \
                     .then_inc(sse, 1)
                    c.se += 1
                rec.scalar(_)
                ev_after[tc] = c.ve
            return c.ve, c.se

        def ln1_var(bidx, prev_ve, prev_se):
            s0 = bidx * TC
            sums = stats[:, s0:s0 + TC]
            sumsq = stats[:, 64 + s0:64 + s0 + TC]
            tcol = stats[:, 128 + s0:128 + s0 + TC]
            ucol = stats[:, 160 + s0:160 + s0 + TC]
            mean = stats[:, 192 + s0:192 + s0 + TC]
            sigc = stats[:, 96 + s0:96 + s0 + TC]
            def _(v):
                v.wait_ge(sve, prev_ve)
                v.wait_ge(sse, prev_se)
                v.scalar_tensor_tensor(tcol, sums, 1.0 / D, sums,
                                       ALU.mult, ALU.mult).then_inc(sve, 1)
                c.ve += 1
                v.tensor_scalar(mean, sums, 1.0 / D, None, ALU.mult) \
                 .then_inc(sve, 1)
                c.ve += 1
                v.wait_ge(sve, c.ve)
                v.tensor_tensor(ucol, sumsq, tcol, ALU.subtract).then_inc(sve, 1)
                c.ve += 1
                v.wait_ge(sve, c.ve)
                v.tensor_copy(rcpt[:, 38:39], ucol[:, 0:1]).then_inc(sve, 1)
                c.ve += 1
            rec.vector(_)
            u_ve = c.ve
            rstd = rcpt[:, 16 * bidx:16 * bidx + TC]
            def _(s):
                s.wait_ge(sve, u_ve)
                s.activation(sigc, ucol, AF.Ln, bias=epsc[:], scale=1.0 / D) \
                 .then_inc(sse, 1)
                c.se += 1
                s.wait_ge(sse, c.se)
                s.activation(rstd, sigc, AF.Exp, scale=-0.5).then_inc(sse, 1)
                c.se += 1
                s.wait_ge(sse, c.se)
                s.copy(rcpt[:, 39:40], rstd[:, 0:1]).then_inc(sse, 1)
                c.se += 1
            rec.scalar(_)
            sig_se = c.se
            nmrs = stats[:, 128 + s0:128 + s0 + TC]
            def _(v):
                v.wait_ge(sse, sig_se)
                v.scalar_tensor_tensor(nmrs, mean, -1.0, rstd,
                                       ALU.mult, ALU.mult).then_inc(sve, 1)
                c.ve += 1
                v.wait_ge(sve, c.ve)
                v.tensor_copy(rcpt[:, 38:39], nmrs[:, 0:1]).then_inc(sve, 1)
                c.ve += 1
            rec.vector(_)
            return c.ve

        def ln1_apply(bidx, prev_ve):
            ln1 = LN1[bidx]
            s0 = bidx * TC
            rstd = rcpt[:, 16 * bidx:16 * bidx + TC]
            nmrs = stats[:, 128 + s0:128 + s0 + TC]
            def _(s):
                s.wait_ge(sve, prev_ve)
                ins = None
                for tc in range(TC):
                    lt = ln1[:, D * tc:D * tc + D]
                    ins = s.activation(lt, lt, AF.Identity,
                                       bias=nmrs[:, tc:tc + 1],
                                       scale=rstd[:, tc:tc + 1])
                ins.then_inc(sse, 1)
                c.se += 1
            rec.scalar(_)
            nrm_se = c.se
            def _(v):
                v.wait_ge(sse, nrm_se)
                ins = None
                for tc in range(TC):
                    lt = ln1[:, D * tc:D * tc + D]
                    ins = v.reduce_max(stats[:, 224 + s0 + tc:224 + s0 + tc + 1],
                                       lt, axis=AX.X, apply_absolute_value=True)
                ins.then_inc(sve, 1)
                c.ve += 1
            rec.vector(_)
            return c.ve

        def global_scale(prev_ve):
            def _(v):
                v.wait_ge(sve, prev_ve)
                v.reduce_max(gmaxc[:], stats[:, 224:224 + 2 * TC], axis=AX.X) \
                 .then_inc(sve, 1)
                c.ve += 1
                v.wait_ge(sve, c.ve)
                v.tensor_copy(rcpt[:, 38:39], gmaxc[:]).then_inc(sve, 1)
                c.ve += 1
            rec.vector(_)
            g_ve = c.ve
            def _(g):
                g.wait_ge(sve, g_ve)
                g.reduce_max(gmax1[0:1, 0:1], gmaxc[:], axis=AX.XYZWC) \
                 .then_inc(sgp, 1)
                c.gp += 1
                g.wait_ge(sgp, c.gp)
                g.tensor_copy(gmax1[0:1, 3:4], gmax1[0:1, 0:1]).then_inc(sgp, 1)
                c.gp += 1
            rec.gpsimd(_)
            g_gp = c.gp
            def _(v):
                v.wait_ge(sgp, g_gp)
                v.tensor_scalar(cc_stage[0:1, 0:1], gmax1[0:1, 0:1],
                                1.0 / 127.0, EPS, ALU.mult, ALU.max) \
                 .then_inc(sve, 1)
                c.ve += 1
                v.wait_ge(sve, c.ve)
                v.tensor_copy(xsc_tmp[0:1, 1:2], cc_stage[0:1, 0:1]) \
                 .then_inc(sve, 1)
                c.ve += 1
            rec.vector(_)
            st_ve = c.ve
            def _(g):
                g.wait_ge(sve, st_ve)
                g.dma_start(out=d_cci, in_=cc_stage[:]).then_inc(sx, 16)
                c.x += 16
                g.wait_ge(sx, c.x)
                g.collective_compute(
                    "AllReduce", ALU.max, replica_groups=[list(range(8))],
                    ins=[d_cci.opt()], outs=[d_cco.opt()]).then_inc(scc, 1)
                c.cc += 1
                g.wait_ge(scc, c.cc)
                g.dma_start(out=xsc_tmp[0:1, 0:1], in_=d_cco[0:1, 0:1]) \
                 .then_inc(sx, 16)
                c.x += 16
                g.wait_ge(sx, c.x)
                g.dma_start(out=xs_col[:],
                            in_=xsc_tmp[0:1, 0:1].unsqueeze(1)
                            .broadcast_to((1, 128, 1))).then_inc(sx, 16)
                c.x += 16
            rec.gpsimd(_)
            xs_x = c.x
            def _(v):
                v.wait_ge(sx, xs_x)
                v.reciprocal(gmax1[0:1, 2:3], xsc_tmp[0:1, 0:1]).then_inc(sve, 1)
                c.ve += 1
                v.wait_ge(sve, c.ve)
                v.tensor_copy(xsc_tmp[0:1, 2:3], gmax1[0:1, 2:3]).then_inc(sve, 1)
                c.ve += 1
            rec.vector(_)
            rx_ve = c.ve
            def _(g):
                g.wait_ge(sve, rx_ve)
                g.dma_start(out=rxs_col[:],
                            in_=gmax1[0:1, 2:3].unsqueeze(1)
                            .broadcast_to((1, 128, 1))).then_inc(sx, 16)
                c.x += 16
            rec.gpsimd(_)
            return c.x

        def ffn(wait_x, w8_ready, wsbf_w):
            N = B * TC
            copy_after = {}
            mm_after = {}
            deq_after = {}
            dv_after = {}
            sq_after = {}
            restve_after = {}
            out_after = {}

            def head(i):
                bidx, tc = i // TC, i % TC
                par4 = i % 4
                lt = LN1[bidx][:, D * tc:D * tc + D]
                ptr = psB[:, 512 * par4:512 * par4 + 512].bitcast(BF16)
                pfa = psA[:, QH * (i % 2):QH * (i % 2) + QH]
                xq = xqt[par4]
                def _(sc):
                    sc.wait_ge(sx, wait_x)
                    sc.wait_ge(sw, wsbf_w)
                    sc.wait_ge(sse, c.se)
                    if i >= 4:
                        sc.wait_ge(ste, mm_after[i - 4])
                    sc.activation(qtmp[:], lt, AF.Identity, bias=magicc[:],
                                  scale=rxs_col[:]).then_inc(sse, 1)
                    c.se += 1
                    sc.wait_ge(sse, c.se)
                    sc.activation(xq[:], qtmp[:], AF.Identity,
                                  bias=nmagicc[:], scale=1.0).then_inc(sse, 1)
                    c.se += 1
                rec.scalar(_)
                q2_se = c.se
                def _(t):
                    t.wait_ge(sse, q2_se)
                    if i >= 4:
                        t.wait_ge(sve, copy_after[i - 4])
                    ins = None
                    for cc8 in range(NCH):
                        ins = t.transpose(ptr[:, 128 * cc8:128 * cc8 + 128],
                                          xq[:, 128 * cc8:128 * cc8 + 128],
                                          ident[:])
                    ins.then_inc(ste, 1)
                    c.te += 1
                rec.tensor(_)
                tr_done = c.te
                def _(v):
                    v.wait_ge(ste, tr_done)
                    v.tensor_copy(xq[:], ptr[:]).then_inc(sve, 1)
                    c.ve += 1
                rec.vector(_)
                copy_after[i] = c.ve
                def _(t):
                    t.wait_ge(sve, copy_after[i])
                    t.wait_ge(sx, w8_ready)
                    if i >= 2:
                        t.wait_ge(sve, deq_after[i - 2])
                    ins = None
                    for kc8 in range(NCH):
                        for nt in range(2):
                            ins = t.matmul(
                                pfa[:, 512 * nt:512 * nt + 512],
                                xq[:, 128 * kc8:128 * kc8 + 128],
                                w8full[:, D * kc8 + 512 * nt:
                                       D * kc8 + 512 * nt + 512],
                                start=(kc8 == 0), stop=(kc8 == NCH - 1))
                    ins.then_inc(ste, 1)
                    c.te += 1
                rec.tensor(_)
                mm_after[i] = c.te

            def deqblk(i):
                bidx, tc = i // TC, i % TC
                par = i % 2
                lt = LN1[bidx][:, D * tc:D * tc + D]
                pfa = psA[:, QH * par:QH * par + QH]
                si = 240 + 8 * par
                def _(v):
                    v.wait_ge(ste, mm_after[i])
                    if i >= 2:
                        v.wait_ge(sve, restve_after[i - 2])
                        v.wait_ge(sse, sq_after[i - 2])
                    v.scalar_tensor_tensor(d1t[:], pfa[:, 0:D], xs_col[:],
                                           wsc[:], ALU.mult, ALU.mult) \
                     .then_inc(sve, 1)
                    c.ve += 1
                    deq_after[i] = c.ve
                    v.wait_ge(sve, c.ve)
                    v.tensor_tensor(d2t[par][:], d1t[:], bff[:], ALU.add) \
                     .then_inc(sve, 1)
                    c.ve += 1
                    v.wait_ge(sve, c.ve)
                    v.tensor_tensor(d2t[par][:], d2t[par][:], lt, ALU.add) \
                     .then_inc(sve, 1)
                    c.ve += 1
                    v.wait_ge(sve, c.ve)
                    v.reduce_sum(stats[:, si:si + 1], d2t[par][:], axis=AX.X) \
                     .then_inc(sve, 1)
                    c.ve += 1
                rec.vector(_)
                dv_after[i] = c.ve

            def rest(i):
                bidx, tc = i // TC, i % TC
                par = i % 2
                si = 240 + 8 * par
                def _(sc):
                    sc.wait_ge(sve, dv_after[i] - 1)
                    sc.wait_ge(sse, c.se)
                    sc.activation(pt0[:], d2t[par][:], AF.Square,
                                  accum_out=stats[:, si + 1:si + 2]) \
                     .then_inc(sse, 1)
                    c.se += 1
                rec.scalar(_)
                sq_after[i] = c.se
                def _(v):
                    v.wait_ge(sve, dv_after[i])
                    v.wait_ge(sse, sq_after[i])
                    v.scalar_tensor_tensor(stats[:, si + 2:si + 3],
                                           stats[:, si:si + 1], 1.0 / D,
                                           stats[:, si:si + 1],
                                           ALU.mult, ALU.mult).then_inc(sve, 1)
                    c.ve += 1
                    v.tensor_scalar(stats[:, si + 3:si + 4],
                                    stats[:, si:si + 1], 1.0 / D, None,
                                    ALU.mult).then_inc(sve, 1)
                    c.ve += 1
                    v.wait_ge(sve, c.ve)
                    v.tensor_tensor(stats[:, si + 4:si + 5],
                                    stats[:, si + 1:si + 2],
                                    stats[:, si + 2:si + 3], ALU.subtract) \
                     .then_inc(sve, 1)
                    c.ve += 1
                    v.wait_ge(sve, c.ve)
                    v.tensor_copy(rcpt[:, 38:39],
                                  stats[:, si + 4:si + 5]).then_inc(sve, 1)
                    c.ve += 1
                rec.vector(_)
                u2_ve = c.ve
                def _(sc):
                    sc.wait_ge(sve, u2_ve)
                    sc.activation(stats[:, si + 5:si + 6],
                                  stats[:, si + 4:si + 5], AF.Ln,
                                  bias=epsc[:], scale=1.0 / D).then_inc(sse, 1)
                    c.se += 1
                    sc.wait_ge(sse, c.se)
                    sc.activation(rcpt[:, 34 + par:35 + par],
                                  stats[:, si + 5:si + 6], AF.Exp,
                                  scale=-0.5).then_inc(sse, 1)
                    c.se += 1
                    sc.wait_ge(sse, c.se)
                    sc.copy(rcpt[:, 36:37],
                            rcpt[:, 34 + par:35 + par]).then_inc(sse, 1)
                    c.se += 1
                rec.scalar(_)
                sig2_se = c.se
                def _(v):
                    v.wait_ge(sse, sig2_se)
                    if i >= 2:
                        v.wait_ge(sw, out_after[i - 2])
                    v.tensor_scalar(ost[par][:], d2t[par][:],
                                    stats[:, si + 3:si + 4],
                                    rcpt[:, 34 + par:35 + par],
                                    ALU.subtract, ALU.mult).then_inc(sve, 1)
                    c.ve += 1
                rec.vector(_)
                restve_after[i] = c.ve
                n2_ve = c.ve
                def _(sc):
                    sc.wait_ge(sve, n2_ve)
                    sc.dma_start(out=d_out[bidx, 128 * tc:128 * tc + 128, :],
                                 in_=ost[par][:]).then_inc(sw, 16)
                    c.w += 16
                rec.sync(_)
                out_after[i] = c.w

            for i in range(N):
                head(i)
                if i >= 1:
                    deqblk(i - 1)
                if i >= 2:
                    rest(i - 2)
            deqblk(N - 1)
            rest(N - 2)
            rest(N - 1)
            return c.w

        x_rdy = load_xT(0, 0)
        w_cts = load_w_chunked(d_wqT, 0)
        ve_ = proj_qk(bq, qt, x_rdy, w_cts, consts_ve)
        w_cts = load_w_chunked(d_wkT, c.te)
        ve_ = proj_qk(bk, kt, x_rdy, w_cts, ve_)
        w_cts = load_w_chunked(d_wvT, c.te)
        ve_ = proj_v(x_rdy, w_cts, ve_)
        projV0_te = c.te

        if stop_after == "proj0":
            _finish(nc, rec, c, sw, sx, ste, sve, sse, sgp, snb, sout,
                    d_dbg, DUMPABLE)
            rec.replay(block)
            return nc

        x_rdy1 = load_xT(1, projV0_te)
        a_ve, pre_gp, tails = attention(at0, ve_)
        if stop_after == "attn0":
            for f in tails:
                f()
            _finish(nc, rec, c, sw, sx, ste, sve, sse, sgp, snb, sout,
                    d_dbg, DUMPABLE)
            rec.replay(block)
            return nc

        w_cts = load_w_chunked(d_wqT, projV0_te, wait_gp=pre_gp)
        for f in tails:
            f()
        tail_gp0 = c.gp
        ve_ = proj_qk(bq, qt, x_rdy1, w_cts, a_ve)
        w_cts = load_w_chunked(d_wkT, c.te)
        ve_ = proj_qk(bk, kt, x_rdy1, w_cts, ve_)
        w_cts = load_w_chunked(d_wvT, c.te)
        ve_ = proj_v(x_rdy1, w_cts, ve_, tail_gp=tail_gp0)
        projV1_te = c.te
        a_ve, pre_gp, tails = attention(at1, ve_)

        wo_rdy = load_w(d_woT, projV1_te, dst=wofull, wait_gp=pre_gp)
        for f in tails:
            f()
        tail_gp1 = c.gp
        a_se = c.se
        ve_, se_ = wo_ln1(0, at0, a_ve, a_se, wo_rdy, projV1_te, pre_gp)
        if stop_after == "wo0r":
            _finish(nc, rec, c, sw, sx, ste, sve, sse, sgp, snb, sout,
                    d_dbg, DUMPABLE)
            rec.replay(block)
            return nc
        var0_ve = ln1_var(0, ve_, se_)
        if stop_after == "wo0":
            ln1_apply(0, var0_ve)
            _finish(nc, rec, c, sw, sx, ste, sve, sse, sgp, snb, sout,
                    d_dbg, DUMPABLE)
            rec.replay(block)
            return nc
        ve_, se_ = wo_ln1(1, at1, a_ve, a_se, wo_rdy, projV1_te, tail_gp1)
        ln1_apply(0, var0_ve)
        var1_ve = ln1_var(1, ve_, se_)
        ve_ = ln1_apply(1, var1_ve)

        wo1_te = c.te
        xs_x = global_scale(ve_)
        w8_rdy = load_w(d_w8T, wo1_te, dst=w8full)
        def _(g):
            g.wait_ge(ste, wo1_te)
            g.dma_start(out=bff[:], in_=d_bf).then_inc(sx, 16)
            c.x += 16
        rec.gpsimd(_)
        bff_x = c.x
        def _(s):
            s.wait_ge(ste, wo1_te)
            s.dma_start(out=wsc[:], in_=d_ws).then_inc(sw, 16)
            c.w += 16
        rec.sync(_)
        wsc_w = c.w

        ffn(max(xs_x, bff_x), w8_rdy, wsc_w)

        _finish(nc, rec, c, sw, sx, ste, sve, sse, sgp, snb, sout,
                d_dbg, DUMPABLE)
        rec.replay(block)
    return nc


def _finish(nc, rec, c, sw, sx, ste, sve, sse, sgp, snb, sout, d_dbg,
            DUMPABLE):
    def _(s):
        s.wait_ge(sve, c.ve)
        s.wait_ge(sse, c.se)
        s.wait_ge(ste, c.te)
        s.wait_ge(snb, c.nb)
        for n, dap in d_dbg.items():
            s.dma_start(out=dap, in_=DUMPABLE[n][0][:]).then_inc(sw, 16)
            c.w += 16
        s.wait_ge(sx, c.x)
        s.wait_ge(sw, c.w)
        s.wait_ge(sgp, c.gp)
    rec.sync(_)
    def _(g):
        g.wait_ge(sw, c.w)
    rec.gpsimd(_)


def _prep_shared(inputs):
    f32 = np.float32
    ca = np.ascontiguousarray
    sh = {}
    sh["wqT"] = ca(np.asarray(inputs["L0_attn_Wq"], f32).T)
    sh["wkT"] = ca(np.asarray(inputs["L0_attn_Wk"], f32).T)
    sh["wvT"] = ca(np.asarray(inputs["L0_attn_Wv"], f32).T)
    sh["woT"] = ca(np.asarray(inputs["L0_attn_Wo"], f32).T)
    sh["w8T"] = ca(np.asarray(inputs["L0_ffn_weight_int8"]).T)
    sh["bq_pm"] = ca(np.asarray(inputs["L0_attn_bq"], f32).reshape(NCH, 128).T)
    sh["bk_pm"] = ca(np.asarray(inputs["L0_attn_bk"], f32).reshape(NCH, 128).T)
    sh["bv_bc"] = ca(np.broadcast_to(np.asarray(inputs["L0_attn_bv"], f32), (128, D)))
    sh["bo_bc"] = ca(np.broadcast_to(np.asarray(inputs["L0_attn_bo"], f32), (128, D)))
    sh["bf_bc"] = ca(np.broadcast_to(np.asarray(inputs["L0_ffn_bias"], f32), (128, D)))
    ws = np.asarray(inputs["L0_ffn_weight_scale"], f32)[:, 0]
    sh["ws_bc"] = ca(np.broadcast_to(ws, (128, D)))
    sh["ident"] = np.eye(128, dtype=ml_dtypes.bfloat16)
    sh["mask"] = np.triu(np.ones((128, 128), dtype=ml_dtypes.bfloat16))
    return sh


def kernel(**inputs):
    x = np.asarray(inputs["L0_attn_x"], np.float32)
    sh = _prep_shared(inputs)
    nc = build_program()
    in_maps = []
    for cid in range(8):
        shard = np.ascontiguousarray(x[B * cid:B * cid + B])
        shardT = np.ascontiguousarray(shard.transpose(0, 2, 1))
        m = dict(sh)
        m["x"] = shard
        m["xT"] = shardT
        in_maps.append(m)
    res = run_bass_kernel_spmd(nc, in_maps, core_ids=list(range(8)))
    return np.concatenate([r["out"] for r in res.results], axis=0)


# version 13
# speedup vs baseline: 1.1220x; 1.0513x over previous
import numpy as np
import ml_dtypes
import concourse.bass as bass
import concourse.mybir as mybir
from concourse.bass_utils import run_bass_kernel_spmd
from concourse.hw_specs import get_activation_tables

_ACT_SET_IDX = list(get_activation_tables("gen3").keys()).index(
    "natural_log_exp_and_others")

F32 = mybir.dt.float32
BF16 = mybir.dt.bfloat16
I8 = mybir.dt.int8
AF = mybir.ActivationFunctionType
ALU = mybir.AluOpType
AX = mybir.AxisListType

B = 2
T = 2048
D = 1024
H = 16
HD = 64
NCH = D // 128
KC = T // 128
TC = T // 128
QH = 1024
MAGIC = float(3 * 2 ** 22)
EPS = 1e-5
KB = 1024
VPAIR = 129
VCHUNK = VPAIR * (H // 2)
VW = VCHUNK * KC


def pieces(lo, hi=QH, step=512):
    out, p = [], lo
    while p < hi:
        q = min((p // step + 1) * step, hi)
        out.append((p, q))
        p = q
    return out


class Ctr:
    def __init__(self):
        self.w = 0; self.x = 0; self.te = 0; self.ve = 0
        self.se = 0; self.gp = 0; self.out = 0; self.cc = 0
        self.nb = 0


class _MockIns:
    __slots__ = ("entry",)
    def __init__(self, entry):
        self.entry = entry
    def then_inc(self, sem, n):
        self.entry[3] = (sem, n)
        return self


class _MockEng:
    __slots__ = ("q", "dma")
    def __init__(self, q, dma=None):
        self.q = q
        self.dma = dma
    def __getattr__(self, name):
        q = self.q
        if name == "dma_start" and self.dma is not None:
            sem, cnt = self.dma
            def call_dma(*a, **k):
                if cnt[0] > 0:
                    q.append(["wait_ge", (sem, 16 * cnt[0]), {}, None])
                cnt[0] += 1
                entry = [name, a, k, None]
                q.append(entry)
                return _MockIns(entry)
            return call_dma
        if name == "dma_start_noser" and self.dma is not None:
            sem, cnt = self.dma
            def call_dma_ns(*a, **k):
                cnt[0] += 1
                entry = ["dma_start", a, k, None]
                q.append(entry)
                return _MockIns(entry)
            return call_dma_ns
        if name == "dma_start_alien":
            def call_dma_al(*a, **k):
                entry = ["dma_start", a, k, None]
                q.append(entry)
                return _MockIns(entry)
            return call_dma_al
        def call(*a, **k):
            entry = [name, a, k, None]
            q.append(entry)
            return _MockIns(entry)
        return call


class _Rec:
    def __init__(self):
        self.q = {n: [] for n in ("tensor", "vector", "scalar", "gpsimd", "sync")}
        self.dma_cfg = {}
    def set_dma(self, engine, sem):
        self.dma_cfg[engine] = (sem, [0])
    def tensor(self, f): f(_MockEng(self.q["tensor"]))
    def vector(self, f): f(_MockEng(self.q["vector"]))
    def scalar(self, f): f(_MockEng(self.q["scalar"]))
    def gpsimd(self, f): f(_MockEng(self.q["gpsimd"], self.dma_cfg.get("gpsimd")))
    def sync(self, f): f(_MockEng(self.q["sync"], self.dma_cfg.get("sync")))
    def replay(self, block):
        def mk(q):
            def run(eng):
                for name, a, k, ti in q:
                    ins = getattr(eng, name)(*a, **k)
                    if ti is not None:
                        ins.then_inc(*ti)
            return run
        block.tensor(mk(self.q["tensor"]))
        block.vector(mk(self.q["vector"]))
        block.scalar(mk(self.q["scalar"]))
        block.gpsimd(mk(self.q["gpsimd"]))
        block.sync(mk(self.q["sync"]))


def build_program(stop_after="full", dump=()):
    nc = bass.Bass("TRN2", target_bir_lowering=False, debug=False)

    d_xT = nc.dram_tensor("xT", [B, D, T], F32, kind="ExternalInput").ap()
    d_x = nc.dram_tensor("x", [B, T, D], F32, kind="ExternalInput").ap()
    d_wqT = nc.dram_tensor("wqT", [D, D], F32, kind="ExternalInput").ap()
    d_wkT = nc.dram_tensor("wkT", [D, D], F32, kind="ExternalInput").ap()
    d_wvT = nc.dram_tensor("wvT", [D, D], F32, kind="ExternalInput").ap()
    d_woT = nc.dram_tensor("woT", [D, D], F32, kind="ExternalInput").ap()
    d_w8T = nc.dram_tensor("w8T", [D, D], I8, kind="ExternalInput").ap()
    d_bq = nc.dram_tensor("bq_pm", [128, NCH], F32, kind="ExternalInput").ap()
    d_bk = nc.dram_tensor("bk_pm", [128, NCH], F32, kind="ExternalInput").ap()
    d_bv = nc.dram_tensor("bv_bc", [128, D], F32, kind="ExternalInput").ap()
    d_bo = nc.dram_tensor("bo_bc", [128, D], F32, kind="ExternalInput").ap()
    d_bf = nc.dram_tensor("bf_bc", [128, D], F32, kind="ExternalInput").ap()
    d_ws = nc.dram_tensor("ws_bc", [128, D], F32, kind="ExternalInput").ap()
    d_id = nc.dram_tensor("ident", [128, 128], BF16, kind="ExternalInput").ap()
    d_mk = nc.dram_tensor("mask", [128, 128], BF16, kind="ExternalInput").ap()
    d_out = nc.dram_tensor("out", [B, T, D], F32, kind="ExternalOutput").ap()
    d_cci = nc.dram_tensor("cc_in", [1, 8], F32).ap()
    d_cco = nc.dram_tensor("cc_out", [1, 8], F32).ap()

    BASE = 16512
    at_ = nc.alloc_sbuf_tensor_at
    cur = [BASE]

    def place(name, shape, dtype, align=64):
        sz = int(np.prod(shape[1:])) * mybir.dt.size(dtype)
        cur[0] = (cur[0] + align - 1) // align * align
        h = at_(name, list(shape), dtype, offset=cur[0])
        cur[0] += sz
        return h

    ident = place("ident", [128, 128], BF16)
    mask = place("mask", [128, 128], BF16)
    bq = place("bq", [128, NCH], F32)
    bk = place("bk", [128, NCH], F32)
    bv = place("bv", [128, D], BF16)
    bo = place("bo", [128, D], BF16)
    epsc = place("epsc", [128, 1], F32)
    magicc = place("magicc", [128, 1], F32)
    nmagicc = place("nmagicc", [128, 1], F32)
    xs_col = place("xs_col", [128, 1], F32)
    rxs_col = place("rxs_col", [128, 1], F32)
    stats = place("stats", [128, 256], F32)
    gmaxc = place("gmaxc", [128, 1], F32)
    gmax1 = place("gmax1", [1, 4], F32)
    xsc_tmp = place("xsc_tmp", [1, 4], F32)
    cc_stage = place("cc_stage", [1, 8], F32)
    rcpt = place("rcpt", [128, 40], F32)
    ones_row = place("ones_row", [1, 128], F32)
    pt0 = place("pt0", [128, QH], BF16)
    pt1 = place("pt1", [128, QH], BF16)
    PT23 = (cur[0] + 63) // 64 * 64
    pt2 = place("pt2", [128, QH], BF16)
    pt3 = place("pt3", [128, QH], BF16)
    tail_bc = at_("tail_bc", [128, QH], F32, offset=PT23)
    assert cur[0] <= BASE + 18 * KB, cur[0]

    UNION = BASE + 18 * KB
    bcastt = [at_("bcast0", [128, QH], F32, offset=UNION),
              at_("bcast1", [128, QH], F32, offset=UNION + 4 * KB)]
    ostg = [at_("ostg0", [128, QH], F32, offset=UNION + 8 * KB),
            at_("ostg1", [128, QH], F32, offset=UNION + 12 * KB),
            at_("ostg2", [128, QH], F32, offset=UNION + 16 * KB),
            at_("ostg3", [128, QH], F32, offset=UNION + 20 * KB)]
    wfull = at_("wfull", [128, NCH * D], BF16, offset=UNION)
    xstage = [at_("xstage0", [128, D], F32, offset=UNION),
              at_("xstage1", [128, D], F32, offset=UNION + 4 * KB)]
    wofull = at_("wofull", [128, NCH * D], BF16, offset=UNION + 8 * KB)
    qtmp = at_("qtmp", [128, D], F32, offset=UNION)
    d1t = at_("d1t", [128, D], F32, offset=UNION + 4 * KB)
    d2t = [at_("d2t0", [128, D], F32, offset=UNION + 8 * KB),
           at_("d2t1", [128, D], F32, offset=UNION + 12 * KB)]
    bff = at_("bff", [128, D], F32, offset=UNION + 16 * KB)
    wsc = at_("wsc", [128, D], F32, offset=UNION + 20 * KB)
    UNION_END = UNION + 24 * KB

    RXT = UNION_END
    xts = at_("xts", [128, NCH * T], BF16, offset=RXT)
    at1 = at_("at1", [128, NCH * T], BF16, offset=RXT)
    w8full = at_("w8full", [128, NCH * D], BF16, offset=RXT)
    xqt = [at_("xqt0", [128, D], BF16, offset=RXT + 16 * KB),
           at_("xqt1", [128, D], BF16, offset=RXT + 18 * KB),
           at_("xqt2", [128, D], BF16, offset=RXT + 20 * KB),
           at_("xqt3", [128, D], BF16, offset=RXT + 22 * KB)]
    ost = [at_("ost0", [128, D], F32, offset=RXT + 24 * KB),
           at_("ost1", [128, D], F32, offset=RXT + 28 * KB)]

    RQ = RXT + 32 * KB
    qt = at_("qt", [128, NCH * T], BF16, offset=RQ)
    RK = RQ + 32 * KB
    kt = at_("kt", [128, NCH * T], BF16, offset=RK)
    ln1_0 = at_("ln1_0", [128, TC * D], F32, offset=RQ)
    RV = RK + 32 * KB
    vaug = at_("vaug", [128, VW], BF16, offset=RV)
    tail_s = [at_("tail_s0", [128, QH], F32, offset=RV),
              at_("tail_s1", [128, QH], F32, offset=RV + 4 * KB)]
    RA0 = RV + 33 * KB
    at0 = at_("at0", [128, NCH * T], BF16, offset=RA0)
    ln1_1 = at_("ln1_1", [128, TC * D], F32, offset=RV)
    assert RA0 + 32 * KB <= 229344, RA0 + 32 * KB

    ATB = [at0, at1]
    LN1 = [ln1_0, ln1_1]
    DUMPABLE = {"qt": (qt, BF16, NCH * T), "kt": (kt, BF16, NCH * T),
                "vaug": (vaug, BF16, VW), "at0": (at0, BF16, NCH * T),
                "at1": (at1, BF16, NCH * T), "ln1_0": (ln1_0, F32, TC * D),
                "ln1_1": (ln1_1, F32, TC * D), "stats": (stats, F32, 256),
                "xs_col": (xs_col, F32, 1)}
    d_dbg = {n: nc.dram_tensor("dbg_" + n, [128, DUMPABLE[n][2]],
                               DUMPABLE[n][1], kind="ExternalOutput").ap()
             for n in dump}

    c = Ctr()

    with (
        nc.psum_tensor("psA", [128, 2048], F32) as psA,
        nc.psum_tensor("psB", [128, 2048], F32) as psB,
        nc.semaphore("sw") as sw,
        nc.semaphore("sx") as sx,
        nc.semaphore("ste") as ste,
        nc.semaphore("sve") as sve,
        nc.semaphore("sse") as sse,
        nc.semaphore("sgp") as sgp,
        nc.semaphore("sout") as sout,
        nc.semaphore("scc") as scc,
        nc.semaphore("snb") as snb,
        nc.Block() as block,
    ):
        rec = _Rec()
        rec.set_dma('sync', sw)
        rec.set_dma('gpsimd', sx)
        def _(s):
            s.add_instruction(mybir.InstLoadActFuncSet(
                name=nc.get_next_instruction_name(),
                act_func_set_id=_ACT_SET_IDX, ins=[], outs=[]))
        rec.scalar(_)

        def _(s):
            for dst, src in [(ident, d_id), (mask, d_mk), (bq, d_bq), (bk, d_bk)]:
                s.dma_start(out=dst[:], in_=src).then_inc(sw, 16)
                c.w += 16
        rec.sync(_)
        consts_w = c.w

        def _(g):
            g.dma_start(out=bv[:], in_=d_bv).then_inc(sx, 16); c.x += 16
            g.dma_start(out=bo[:], in_=d_bo).then_inc(sx, 16); c.x += 16
        rec.gpsimd(_)
        bias_x = c.x

        def _(v):
            v.memset(epsc[:], EPS)
            v.memset(magicc[:], MAGIC)
            v.memset(nmagicc[:], -MAGIC)
            v.memset(cc_stage[:], 0.0)
            v.memset(ones_row[:], 1.0)
            v.memset(bass.AP(vaug, 64, [[VW, 128], [VCHUNK, KC],
                                        [VPAIR, H // 2]]), 1.0).then_inc(sve, 1)
            c.ve += 1
        rec.vector(_)
        consts_ve = c.ve

        def load_xT(b, wait_te):
            def _(g):
                g.wait_ge(ste, wait_te)
                for ch in range(NCH):
                    dma = g.dma_start if ch == 0 else g.dma_start_noser
                    dma(
                        out=xts[:, T * ch:T * ch + T],
                        in_=d_xT[b, 128 * ch:128 * ch + 128, :],
                    ).then_inc(sx, 16)
                    c.x += 16
            rec.gpsimd(_)
            return c.x

        def load_w(src, wait_te, dst=None, wait_ve=None, wait_gp=None):
            dst = wfull if dst is None else dst
            def _(g):
                g.wait_ge(ste, wait_te)
                if wait_ve is not None:
                    g.wait_ge(sve, wait_ve)
                if wait_gp is not None:
                    g.wait_ge(sgp, wait_gp)
                g.dma_start(
                    out=dst.ap().rearrange("p (ch n) -> p ch n", ch=NCH),
                    in_=src.rearrange("(ch p) n -> p ch n", p=128),
                ).then_inc(sx, 16)
                c.x += 16
            rec.gpsimd(_)
            return c.x

        def load_w_chunked(src, wait_te, wait_ve=None, wait_gp=None):
            counts = []
            def _(g):
                for ch in range(NCH):
                    if ch == 0:
                        g.wait_ge(ste, wait_te)
                        if wait_ve is not None:
                            g.wait_ge(sve, wait_ve)
                        if wait_gp is not None:
                            g.wait_ge(sgp, wait_gp)
                    g.dma_start(
                        out=wfull[:, D * ch:D * ch + D],
                        in_=src[128 * ch:128 * ch + 128, :],
                    ).then_inc(sx, 16)
                    c.x += 16
                    counts.append(c.x)
            rec.gpsimd(_)
            return counts

        def proj_qk(bias_col, dst, x_ready, w_counts, prev_ve):
            evac_after = {}
            entry_se = c.se
            for nch in range(NCH):
                arena = psA if nch % 2 == 0 else psB
                def _(t, nch=nch, arena=arena):
                    t.wait_ge(sx, x_ready)
                    if nch < 2:
                        t.wait_ge(sse, entry_se)
                    t.wait_ge(sve, evac_after.get(nch - 2, prev_ve))
                    ins = None
                    for kc in range(NCH):
                        if nch == 0:
                            t.wait_ge(sx, w_counts[kc])
                        for tt in range(4):
                            ins = t.matmul(
                                arena[:, 512 * tt:512 * tt + 512],
                                wfull[:, D * kc + 128 * nch:
                                      D * kc + 128 * nch + 128],
                                xts[:, T * kc + 512 * tt:T * kc + 512 * tt + 512],
                                start=(kc == 0), stop=(kc == NCH - 1))
                    ins.then_inc(ste, 1)
                    c.te += 1
                rec.tensor(_)
                te_done = c.te
                def _(v, nch=nch, arena=arena, te_done=te_done):
                    v.wait_ge(ste, te_done)
                    v.wait_ge(sw, consts_w)
                    ins = None
                    for tt in range(4):
                        ins = v.tensor_scalar(
                            dst[:, T * nch + 512 * tt:T * nch + 512 * tt + 512],
                            arena[:, 512 * tt:512 * tt + 512],
                            bias_col[:, nch:nch + 1], None, ALU.add)
                    ins.then_inc(sve, 1)
                    c.ve += 1
                rec.vector(_)
                evac_after[nch] = c.ve
            return c.ve

        def proj_v(x_ready, w_counts, prev_ve, tail_gp=0):
            evac_after = {}
            entry_se = c.se
            if tail_gp:
                def _(v):
                    v.wait_ge(sgp, tail_gp)
                    v.memset(bass.AP(vaug, 64, [[VW, 128], [VCHUNK, KC],
                                                [VPAIR, H // 2]]), 1.0) \
                     .then_inc(sve, 1)
                    c.ve += 1
                rec.vector(_)
            for tc in range(TC):
                arena = psA if tc % 2 == 0 else psB
                def _(t, tc=tc, arena=arena):
                    t.wait_ge(sx, x_ready)
                    if tc < 2:
                        t.wait_ge(sse, entry_se)
                    t.wait_ge(sve, evac_after.get(tc - 2, prev_ve))
                    ins = None
                    for kc in range(NCH):
                        if tc == 0:
                            t.wait_ge(sx, w_counts[kc])
                        for nt in range(2):
                            ins = t.matmul(
                                arena[:, 512 * nt:512 * nt + 512],
                                xts[:, T * kc + 128 * tc:T * kc + 128 * tc + 128],
                                wfull[:, D * kc + 512 * nt:D * kc + 512 * nt + 512],
                                start=(kc == 0), stop=(kc == NCH - 1))
                    ins.then_inc(ste, 1)
                    c.te += 1
                rec.tensor(_)
                te_done = c.te
                def _(v, tc=tc, arena=arena, te_done=te_done):
                    v.wait_ge(ste, te_done)
                    v.wait_ge(sx, bias_x)
                    ins = None
                    for nt in range(2):
                        base = VCHUNK * tc + VPAIR * 4 * nt
                        dst = bass.AP(vaug, base,
                                      [[VW, 128], [VPAIR, 4], [65, 2], [1, 64]])
                        src = bass.AP(arena.ap().tensor,
                                      arena.ap().offset + 512 * nt,
                                      [[2048, 128], [128, 4], [64, 2], [1, 64]])
                        b_ap = bass.AP(bv, 512 * nt,
                                       [[D, 128], [128, 4], [64, 2], [1, 64]])
                        ins = v.tensor_tensor(dst, src, b_ap, ALU.add)
                    ins.then_inc(sve, 1)
                    c.ve += 1
                rec.vector(_)
                evac_after[tc] = c.ve
            return c.ve

        def attention(at_dst, ready_ve):
            entry_se = c.se
            entry_gp = c.gp
            seg_i = [0]
            half_i = [0]
            exp_after = {}
            pv_after = {}
            norm_after = {}
            stage_after = {}

            class Stream:
                def __init__(self, h, hf):
                    self.h, self.hf = h, hf
                    self.q0h = QH * hf
                    self.js = [j for j in range(KC) if 128 * j < self.q0h + QH]
                    self.hidx = half_i[0]
                    half_i[0] += 1
                    self.osl = self.hidx % 2
                    self.obuf = psB[:, QH * self.osl:QH * self.osl + QH]
                    self.segs = []
                    self.ch = h // 2
                    self.po = 64 * (h % 2)
                    self.pblk = VPAIR * (h // 2 % (H // 2))

            def emit_s(st, j):
                qs = max(st.q0h, 128 * j)
                lo = qs - st.q0h
                si = seg_i[0]
                sp = si % 2
                sps = psA[:, QH * sp:QH * sp + QH]
                ptb = (pt0, pt1, pt2, pt3)[si % 4]
                def _(t):
                    t.wait_ge(sve, ready_ve)
                    if si >= 2:
                        t.wait_ge(sse, exp_after[si - 2])
                    else:
                        t.wait_ge(sse, entry_se)
                    ins = None
                    for (plo, phi) in pieces(lo):
                        ins = t.matmul(
                            sps[:, plo:phi],
                            kt[st.po:st.po + 64,
                               T * st.ch + 128 * j:T * st.ch + 128 * j + 128],
                            qt[st.po:st.po + 64,
                               T * st.ch + st.q0h + plo:T * st.ch + st.q0h + phi],
                            start=True, stop=True)
                    ins.then_inc(ste, 1)
                    c.te += 1
                rec.tensor(_)
                s_done = c.te
                def _(sc):
                    sc.wait_ge(ste, s_done)
                    if si >= 4:
                        sc.wait_ge(ste, pv_after[si - 4])
                    sc.activation(ptb[:, lo:QH], sps[:, lo:QH], AF.Exp,
                                  scale=0.125).then_inc(sse, 1)
                    c.se += 1
                rec.scalar(_)
                exp_after[si] = c.se
                e_done = c.se
                m_done = None
                if qs == 128 * j:
                    def _(v):
                        v.wait_ge(sse, e_done)
                        v.tensor_tensor(ptb[:, lo:lo + 128],
                                        ptb[:, lo:lo + 128], mask[:],
                                        ALU.mult).then_inc(sve, 1)
                        c.ve += 1
                    rec.vector(_)
                    m_done = c.ve
                st.segs.append((j, lo, ptb, e_done, m_done, si))
                seg_i[0] += 1

            def emit_pv(st, k):
                (j, lo, ptb, e_done, m_done, si) = st.segs[k]
                first = (k == 0)
                last_j = st.js[-1]
                def _(t):
                    if first:
                        if st.hidx >= 2:
                            t.wait_ge(sve, stage_after[st.hidx - 2])
                        else:
                            t.wait_ge(sve, ready_ve)
                    if m_done is not None:
                        t.wait_ge(sve, m_done)
                    else:
                        t.wait_ge(sse, e_done)
                    if st.h % 2 == 0:
                        lhs = vaug[:, VCHUNK * j + st.pblk:
                                   VCHUNK * j + st.pblk + 65]
                        nrow = 65
                    else:
                        lhs = vaug[:, VCHUNK * j + st.pblk + 1:
                                   VCHUNK * j + st.pblk + 129]
                        nrow = 128
                    ins = None
                    for (plo, phi) in pieces(lo):
                        bank_last = min((st.q0h + phi - 1) // 128, last_j)
                        ins = t.matmul(st.obuf[0:nrow, plo:phi], lhs,
                                       ptb[:, plo:phi],
                                       start=(j == 0), stop=(j == bank_last))
                    ins.then_inc(ste, 1)
                    c.te += 1
                rec.tensor(_)
                pv_after[si] = c.te

            def emit_norm_stage(st):
                pv_done = pv_after[st.segs[-1][5]]
                stg = tail_s[st.hidx - 2 * H + 2] if st.hidx >= 2 * H - 2 \
                    else ostg[st.hidx % 4]
                even = (st.h % 2 == 0)
                def _(v):
                    v.wait_ge(ste, pv_done)
                    v.wait_ge(sgp, norm_after.get(st.hidx - 4, entry_gp))
                    if even:
                        v.tensor_copy(stg[0:65, :], st.obuf[0:65, :]) \
                         .then_inc(sve, 1)
                    else:
                        v.tensor_copy(stg[32:64, :], st.obuf[32:64, :])
                        v.tensor_copy(stg[64:128, :], st.obuf[64:128, :]) \
                         .then_inc(sve, 1)
                    c.ve += 1
                    if not even:
                        v.wait_ge(sve, c.ve)
                        v.tensor_scalar(stg[32:64, :], stg[32:64, :], 1e-20,
                                        1e18, ALU.max, ALU.min).then_inc(sve, 1)
                        c.ve += 1
                rec.vector(_)
                stage_after[st.hidx] = c.ve
                return c.ve

            def emit_norm_finish(st, cp_done):
                h, ch, q0h = st.h, st.ch, st.q0h
                tail = st.hidx >= 2 * H - 2
                stg = tail_s[st.hidx - 2 * H + 2] if tail \
                    else ostg[st.hidx % 4]
                bcast = tail_bc if tail else bcastt[(st.hidx % 4) // 2]
                even = (h % 2 == 0)
                dr = stg[64:65, :] if even else stg[32:64, :]
                rr = stg[96:97, :] if even else stg[0:32, :]
                def _(sc):
                    sc.wait_ge(sve, cp_done)
                    sc.activation(rr, dr, AF.Ln).then_inc(sse, 1)
                    c.se += 1
                    sc.wait_ge(sse, c.se)
                    sc.activation(rr, rr, AF.Exp, scale=-1.0).then_inc(sse, 1)
                    c.se += 1
                rec.scalar(_)
                r_done = c.se
                r0, r1 = (0, 64) if even else (64, 128)
                if tail:
                    tail_te = c.te
                    def _(g):
                        g.wait_ge(sse, r_done)
                        g.wait_ge(ste, tail_te)
                        rsrc = stg[96:97, :] if even else stg[31:32, :]
                        g.dma_start_alien(
                            out=bcast[r0:r1, :],
                            in_=rsrc.unsqueeze(1)
                                .broadcast_to((1, r1 - r0, QH))) \
                         .then_inc(snb, 16)
                        c.nb += 16
                    rec.gpsimd(_)
                    b_done = c.nb
                else:
                    def _(sc):
                        sc.wait_ge(sse, r_done)
                        rsrc = stg[96:97, :] if even else stg[31:32, :]
                        sc.dma_start(
                            out=bcast[r0:r1, :],
                            in_=rsrc.unsqueeze(1)
                                .broadcast_to((1, r1 - r0, QH))) \
                         .then_inc(sw, 16)
                        c.w += 16
                    rec.sync(_)
                    b_done = c.w
                def _(g):
                    g.wait_ge(snb if tail else sw, b_done)
                    g.tensor_tensor(
                        at_dst[r0:r1, T * ch + q0h:T * ch + q0h + QH],
                        stg[0:64, :] if even else stg[64:128, :],
                        bcast[r0:r1, :], ALU.mult).then_inc(sgp, 1)
                    c.gp += 1
                rec.gpsimd(_)
                norm_after[st.hidx] = c.gp

            pending = []
            for hp in range(0, H, 2):
                for hf in range(2):
                    sa = Stream(hp, hf)
                    sb = Stream(hp + 1, hf)
                    n = len(sa.js)
                    for k in range(n):
                        emit_s(sa, sa.js[k])
                        emit_s(sb, sb.js[k])
                        if k == 2:
                            for st, cp in pending:
                                emit_norm_finish(st, cp)
                            pending = []
                        if k >= 1:
                            emit_pv(sa, k - 1)
                            emit_pv(sb, k - 1)
                    emit_pv(sa, n - 1)
                    emit_pv(sb, n - 1)
                    pending.append((sa, emit_norm_stage(sa)))
                    pending.append((sb, emit_norm_stage(sb)))
            thunks = [(lambda st=st, cp=cp: emit_norm_finish(st, cp))
                      for st, cp in pending]
            return c.ve, c.gp, thunks

        def wo_ln1(bidx, at_src, attn_ve, attn_se, wo_ready, first_xstage_te, attn_gp=0):
            ln1 = LN1[bidx]
            ev_after = {}
            entry_ve = c.ve
            for tc in range(TC):
                arena = psA if tc % 2 == 0 else psB
                xst = xstage[tc % 2]
                def _(s):
                    if tc < 2:
                        s.wait_ge(ste, first_xstage_te)
                        s.wait_ge(sve, entry_ve)
                        s.wait_ge(sgp, attn_gp)
                    else:
                        s.wait_ge(sve, ev_after[tc - 2])
                    s.dma_start(out=xst[:],
                                in_=d_x[bidx, 128 * tc:128 * tc + 128, :]) \
                        .then_inc(sw, 16)
                    c.w += 16
                rec.sync(_)
                x_done = c.w
                def _(t):
                    t.wait_ge(sve, attn_ve)
                    t.wait_ge(sse, attn_se)
                    t.wait_ge(sgp, attn_gp)
                    t.wait_ge(sx, wo_ready)
                    t.wait_ge(sve, ev_after.get(tc - 2, entry_ve))
                    ins = None
                    for dch in range(NCH):
                        for nt in range(2):
                            ins = t.matmul(
                                arena[:, 512 * nt:512 * nt + 512],
                                at_src[:, T * dch + 128 * tc:
                                       T * dch + 128 * tc + 128],
                                wofull[:, D * dch + 512 * nt:
                                       D * dch + 512 * nt + 512],
                                start=(dch == 0), stop=(dch == NCH - 1))
                    ins.then_inc(ste, 1)
                    c.te += 1
                rec.tensor(_)
                mm_done = c.te
                si = bidx * TC + tc
                lt = ln1[:, D * tc:D * tc + D]
                def _(v):
                    v.wait_ge(ste, mm_done)
                    v.wait_ge(sw, x_done)
                    v.tensor_tensor(lt, arena[:, 0:D], bo[:], ALU.add) \
                     .then_inc(sve, 1)
                    c.ve += 1
                    v.wait_ge(sve, c.ve)
                    v.tensor_tensor(lt, lt, xst[:], ALU.add).then_inc(sve, 1)
                    c.ve += 1
                    v.wait_ge(sve, c.ve)
                    v.reduce_sum(stats[:, si:si + 1], lt, axis=AX.X) \
                     .then_inc(sve, 1)
                    c.ve += 1
                rec.vector(_)
                r1_ve = c.ve
                def _(s):
                    s.wait_ge(sve, r1_ve - 1)
                    s.wait_ge(sse, c.se)
                    s.activation(pt0[:], lt, AF.Square,
                                 accum_out=stats[:, 64 + si:64 + si + 1]) \
                     .then_inc(sse, 1)
                    c.se += 1
                rec.scalar(_)
                ev_after[tc] = c.ve
            return c.ve, c.se

        def ln1_var(bidx, prev_ve, prev_se):
            s0 = bidx * TC
            sums = stats[:, s0:s0 + TC]
            sumsq = stats[:, 64 + s0:64 + s0 + TC]
            tcol = stats[:, 128 + s0:128 + s0 + TC]
            ucol = stats[:, 160 + s0:160 + s0 + TC]
            mean = stats[:, 192 + s0:192 + s0 + TC]
            sigc = stats[:, 96 + s0:96 + s0 + TC]
            def _(v):
                v.wait_ge(sve, prev_ve)
                v.wait_ge(sse, prev_se)
                v.scalar_tensor_tensor(tcol, sums, 1.0 / D, sums,
                                       ALU.mult, ALU.mult).then_inc(sve, 1)
                c.ve += 1
                v.tensor_scalar(mean, sums, 1.0 / D, None, ALU.mult) \
                 .then_inc(sve, 1)
                c.ve += 1
                v.wait_ge(sve, c.ve)
                v.tensor_tensor(ucol, sumsq, tcol, ALU.subtract).then_inc(sve, 1)
                c.ve += 1
                v.wait_ge(sve, c.ve)
                v.tensor_copy(rcpt[:, 38:39], ucol[:, 0:1]).then_inc(sve, 1)
                c.ve += 1
            rec.vector(_)
            u_ve = c.ve
            rstd = rcpt[:, 16 * bidx:16 * bidx + TC]
            def _(s):
                s.wait_ge(sve, u_ve)
                s.activation(sigc, ucol, AF.Ln, bias=epsc[:], scale=1.0 / D) \
                 .then_inc(sse, 1)
                c.se += 1
                s.wait_ge(sse, c.se)
                s.activation(rstd, sigc, AF.Exp, scale=-0.5).then_inc(sse, 1)
                c.se += 1
                s.wait_ge(sse, c.se)
                s.copy(rcpt[:, 39:40], rstd[:, 0:1]).then_inc(sse, 1)
                c.se += 1
            rec.scalar(_)
            sig_se = c.se
            nmrs = stats[:, 128 + s0:128 + s0 + TC]
            def _(v):
                v.wait_ge(sse, sig_se)
                v.scalar_tensor_tensor(nmrs, mean, -1.0, rstd,
                                       ALU.mult, ALU.mult).then_inc(sve, 1)
                c.ve += 1
                v.wait_ge(sve, c.ve)
                v.tensor_copy(rcpt[:, 38:39], nmrs[:, 0:1]).then_inc(sve, 1)
                c.ve += 1
            rec.vector(_)
            return c.ve

        def ln1_apply(bidx, prev_ve):
            ln1 = LN1[bidx]
            s0 = bidx * TC
            rstd = rcpt[:, 16 * bidx:16 * bidx + TC]
            nmrs = stats[:, 128 + s0:128 + s0 + TC]
            def _(s):
                s.wait_ge(sve, prev_ve)
                ins = None
                for tc in range(TC):
                    lt = ln1[:, D * tc:D * tc + D]
                    ins = s.activation(lt, lt, AF.Identity,
                                       bias=nmrs[:, tc:tc + 1],
                                       scale=rstd[:, tc:tc + 1])
                ins.then_inc(sse, 1)
                c.se += 1
            rec.scalar(_)
            nrm_se = c.se
            def _(v):
                v.wait_ge(sse, nrm_se)
                ins = None
                for tc in range(TC):
                    lt = ln1[:, D * tc:D * tc + D]
                    ins = v.reduce_max(stats[:, 224 + s0 + tc:224 + s0 + tc + 1],
                                       lt, axis=AX.X, apply_absolute_value=True)
                ins.then_inc(sve, 1)
                c.ve += 1
            rec.vector(_)
            return c.ve

        def global_scale(prev_ve):
            def _(v):
                v.wait_ge(sve, prev_ve)
                v.reduce_max(gmaxc[:], stats[:, 224:224 + 2 * TC], axis=AX.X) \
                 .then_inc(sve, 1)
                c.ve += 1
                v.wait_ge(sve, c.ve)
                v.tensor_copy(rcpt[:, 38:39], gmaxc[:]).then_inc(sve, 1)
                c.ve += 1
            rec.vector(_)
            g_ve = c.ve
            def _(g):
                g.wait_ge(sve, g_ve)
                g.reduce_max(gmax1[0:1, 0:1], gmaxc[:], axis=AX.XYZWC) \
                 .then_inc(sgp, 1)
                c.gp += 1
                g.wait_ge(sgp, c.gp)
                g.tensor_copy(gmax1[0:1, 3:4], gmax1[0:1, 0:1]).then_inc(sgp, 1)
                c.gp += 1
            rec.gpsimd(_)
            g_gp = c.gp
            def _(v):
                v.wait_ge(sgp, g_gp)
                v.tensor_scalar(cc_stage[0:1, 0:1], gmax1[0:1, 0:1],
                                1.0 / 127.0, EPS, ALU.mult, ALU.max) \
                 .then_inc(sve, 1)
                c.ve += 1
                v.wait_ge(sve, c.ve)
                v.tensor_copy(xsc_tmp[0:1, 1:2], cc_stage[0:1, 0:1]) \
                 .then_inc(sve, 1)
                c.ve += 1
            rec.vector(_)
            st_ve = c.ve
            def _(g):
                g.wait_ge(sve, st_ve)
                g.dma_start(out=d_cci, in_=cc_stage[:]).then_inc(sx, 16)
                c.x += 16
                g.wait_ge(sx, c.x)
                g.collective_compute(
                    "AllReduce", ALU.max, replica_groups=[list(range(8))],
                    ins=[d_cci.opt()], outs=[d_cco.opt()]).then_inc(scc, 1)
                c.cc += 1
                g.wait_ge(scc, c.cc)
                g.dma_start(out=xsc_tmp[0:1, 0:1], in_=d_cco[0:1, 0:1]) \
                 .then_inc(sx, 16)
                c.x += 16
            rec.gpsimd(_)
            in_x = c.x
            def _(v):
                v.wait_ge(sx, in_x)
                v.reciprocal(gmax1[0:1, 2:3], xsc_tmp[0:1, 0:1]).then_inc(sve, 1)
                c.ve += 1
                v.wait_ge(sve, c.ve)
                v.tensor_copy(xsc_tmp[0:1, 1:2], gmax1[0:1, 2:3]) \
                 .then_inc(sve, 1)
                c.ve += 1
                v.wait_ge(sve, c.ve)
                v.tensor_copy(gmax1[0:1, 3:4], xsc_tmp[0:1, 1:2]) \
                 .then_inc(sve, 1)
                c.ve += 1
            rec.vector(_)
            rx_ve = c.ve
            def _(t):
                t.wait_ge(sve, rx_ve)
                t.matmul(psA[:, 0:2], ones_row[:], xsc_tmp[0:1, 0:2],
                         start=True, stop=True).then_inc(ste, 1)
                c.te += 1
            rec.tensor(_)
            bc_te = c.te
            def _(v):
                v.wait_ge(ste, bc_te)
                v.tensor_copy(xs_col[:], psA[:, 0:1])
                v.tensor_copy(rxs_col[:], psA[:, 1:2]).then_inc(sve, 1)
                c.ve += 1
            rec.vector(_)
            return c.ve

        def ffn(xs_ve, bff_x, w8_ready, wsbf_w):
            N = B * TC
            copy_after = {}
            mm_after = {}
            deq_after = {}
            dv_after = {}
            u2_after = {}
            sq_after = {}
            restve_after = {}
            out_after = {}

            def head(i):
                bidx, tc = i // TC, i % TC
                par4 = i % 4
                lt = LN1[bidx][:, D * tc:D * tc + D]
                ptr = psB[:, 512 * par4:512 * par4 + 512].bitcast(BF16)
                pfa = psA[:, QH * (i % 2):QH * (i % 2) + QH]
                xq = xqt[par4]
                def _(sc):
                    sc.wait_ge(sve, xs_ve)
                    sc.wait_ge(sx, bff_x)
                    sc.wait_ge(sw, wsbf_w)
                    sc.wait_ge(sse, c.se)
                    if i >= 4:
                        sc.wait_ge(ste, mm_after[i - 4])
                    sc.activation(qtmp[:], lt, AF.Identity, bias=magicc[:],
                                  scale=rxs_col[:]).then_inc(sse, 1)
                    c.se += 1
                    sc.wait_ge(sse, c.se)
                    sc.activation(xq[:], qtmp[:], AF.Identity,
                                  bias=nmagicc[:], scale=1.0).then_inc(sse, 1)
                    c.se += 1
                rec.scalar(_)
                q2_se = c.se
                def _(t):
                    t.wait_ge(sse, q2_se)
                    if i >= 4:
                        t.wait_ge(sve, copy_after[i - 4])
                    ins = None
                    for cc8 in range(NCH):
                        ins = t.transpose(ptr[:, 128 * cc8:128 * cc8 + 128],
                                          xq[:, 128 * cc8:128 * cc8 + 128],
                                          ident[:])
                    ins.then_inc(ste, 1)
                    c.te += 1
                rec.tensor(_)
                tr_done = c.te
                def _(v):
                    v.wait_ge(ste, tr_done)
                    v.tensor_copy(xq[:], ptr[:]).then_inc(sve, 1)
                    c.ve += 1
                rec.vector(_)
                copy_after[i] = c.ve
                def _(t):
                    t.wait_ge(sve, copy_after[i])
                    t.wait_ge(sx, w8_ready)
                    if i >= 2:
                        t.wait_ge(sve, deq_after[i - 2])
                    ins = None
                    for kc8 in range(NCH):
                        for nt in range(2):
                            ins = t.matmul(
                                pfa[:, 512 * nt:512 * nt + 512],
                                xq[:, 128 * kc8:128 * kc8 + 128],
                                w8full[:, D * kc8 + 512 * nt:
                                       D * kc8 + 512 * nt + 512],
                                start=(kc8 == 0), stop=(kc8 == NCH - 1))
                    ins.then_inc(ste, 1)
                    c.te += 1
                rec.tensor(_)
                mm_after[i] = c.te

            def deqblk(i):
                bidx, tc = i // TC, i % TC
                par = i % 2
                lt = LN1[bidx][:, D * tc:D * tc + D]
                pfa = psA[:, QH * par:QH * par + QH]
                si = 240 + 8 * par
                def _(v):
                    v.wait_ge(ste, mm_after[i])
                    if i >= 2:
                        v.wait_ge(sve, restve_after[i - 2])
                        v.wait_ge(sse, sq_after[i - 2])
                    v.scalar_tensor_tensor(d1t[:], pfa[:, 0:D], xs_col[:],
                                           wsc[:], ALU.mult, ALU.mult) \
                     .then_inc(sve, 1)
                    c.ve += 1
                    deq_after[i] = c.ve
                    v.wait_ge(sve, c.ve)
                    v.tensor_tensor(d2t[par][:], d1t[:], bff[:], ALU.add) \
                     .then_inc(sve, 1)
                    c.ve += 1
                    v.wait_ge(sve, c.ve)
                    v.tensor_tensor(d2t[par][:], d2t[par][:], lt, ALU.add) \
                     .then_inc(sve, 1)
                    c.ve += 1
                    v.wait_ge(sve, c.ve)
                    v.reduce_sum(stats[:, si:si + 1], d2t[par][:], axis=AX.X) \
                     .then_inc(sve, 1)
                    c.ve += 1
                rec.vector(_)
                dv_after[i] = c.ve

            def rest_a(i):
                bidx, tc = i // TC, i % TC
                par = i % 2
                si = 240 + 8 * par
                def _(sc):
                    sc.wait_ge(sve, dv_after[i] - 1)
                    sc.wait_ge(sse, c.se)
                    sc.activation(pt0[:], d2t[par][:], AF.Square,
                                  accum_out=stats[:, si + 1:si + 2]) \
                     .then_inc(sse, 1)
                    c.se += 1
                rec.scalar(_)
                sq_after[i] = c.se
                def _(v):
                    v.wait_ge(sve, dv_after[i])
                    v.wait_ge(sse, sq_after[i])
                    v.scalar_tensor_tensor(stats[:, si + 2:si + 3],
                                           stats[:, si:si + 1], 1.0 / D,
                                           stats[:, si:si + 1],
                                           ALU.mult, ALU.mult).then_inc(sve, 1)
                    c.ve += 1
                    v.tensor_scalar(stats[:, si + 3:si + 4],
                                    stats[:, si:si + 1], 1.0 / D, None,
                                    ALU.mult).then_inc(sve, 1)
                    c.ve += 1
                    v.wait_ge(sve, c.ve)
                    v.tensor_tensor(stats[:, si + 4:si + 5],
                                    stats[:, si + 1:si + 2],
                                    stats[:, si + 2:si + 3], ALU.subtract) \
                     .then_inc(sve, 1)
                    c.ve += 1
                    v.wait_ge(sve, c.ve)
                    v.tensor_copy(rcpt[:, 38:39],
                                  stats[:, si + 4:si + 5]).then_inc(sve, 1)
                    c.ve += 1
                rec.vector(_)
                u2_after[i] = c.ve

            def rest_b(i):
                bidx, tc = i // TC, i % TC
                par = i % 2
                si = 240 + 8 * par
                def _(sc):
                    sc.wait_ge(sve, u2_after[i])
                    sc.activation(stats[:, si + 5:si + 6],
                                  stats[:, si + 4:si + 5], AF.Ln,
                                  bias=epsc[:], scale=1.0 / D).then_inc(sse, 1)
                    c.se += 1
                    sc.wait_ge(sse, c.se)
                    sc.activation(rcpt[:, 34 + par:35 + par],
                                  stats[:, si + 5:si + 6], AF.Exp,
                                  scale=-0.5).then_inc(sse, 1)
                    c.se += 1
                    sc.wait_ge(sse, c.se)
                    sc.copy(rcpt[:, 36:37],
                            rcpt[:, 34 + par:35 + par]).then_inc(sse, 1)
                    c.se += 1
                rec.scalar(_)
                sig2_se = c.se
                def _(v):
                    v.wait_ge(sse, sig2_se)
                    if i >= 2:
                        v.wait_ge(sw, out_after[i - 2])
                    v.tensor_scalar(ost[par][:], d2t[par][:],
                                    stats[:, si + 3:si + 4],
                                    rcpt[:, 34 + par:35 + par],
                                    ALU.subtract, ALU.mult).then_inc(sve, 1)
                    c.ve += 1
                rec.vector(_)
                restve_after[i] = c.ve
                n2_ve = c.ve
                def _(sc):
                    sc.wait_ge(sve, n2_ve)
                    sc.dma_start(out=d_out[bidx, 128 * tc:128 * tc + 128, :],
                                 in_=ost[par][:]).then_inc(sw, 16)
                    c.w += 16
                rec.sync(_)
                out_after[i] = c.w

            for i in range(N):
                head(i)
                if i >= 3:
                    rest_b(i - 3)
                if i >= 1:
                    deqblk(i - 1)
                if i >= 2:
                    rest_a(i - 2)
            rest_b(N - 3)
            deqblk(N - 1)
            rest_a(N - 2)
            rest_b(N - 2)
            rest_a(N - 1)
            rest_b(N - 1)
            return c.w

        x_rdy = load_xT(0, 0)
        w_cts = load_w_chunked(d_wqT, 0)
        ve_ = proj_qk(bq, qt, x_rdy, w_cts, consts_ve)
        w_cts = load_w_chunked(d_wkT, c.te)
        ve_ = proj_qk(bk, kt, x_rdy, w_cts, ve_)
        w_cts = load_w_chunked(d_wvT, c.te)
        ve_ = proj_v(x_rdy, w_cts, ve_)
        projV0_te = c.te

        if stop_after == "proj0":
            _finish(nc, rec, c, sw, sx, ste, sve, sse, sgp, snb, sout,
                    d_dbg, DUMPABLE)
            rec.replay(block)
            return nc

        x_rdy1 = load_xT(1, projV0_te)
        a_ve, pre_gp, tails = attention(at0, ve_)
        if stop_after == "attn0":
            for f in tails:
                f()
            _finish(nc, rec, c, sw, sx, ste, sve, sse, sgp, snb, sout,
                    d_dbg, DUMPABLE)
            rec.replay(block)
            return nc

        w_cts = load_w_chunked(d_wqT, projV0_te, wait_gp=pre_gp)
        for f in tails:
            f()
        tail_gp0 = c.gp
        ve_ = proj_qk(bq, qt, x_rdy1, w_cts, a_ve)
        w_cts = load_w_chunked(d_wkT, c.te)
        ve_ = proj_qk(bk, kt, x_rdy1, w_cts, ve_)
        w_cts = load_w_chunked(d_wvT, c.te)
        ve_ = proj_v(x_rdy1, w_cts, ve_, tail_gp=tail_gp0)
        projV1_te = c.te
        a_ve, pre_gp, tails = attention(at1, ve_)

        wo_rdy = load_w(d_woT, projV1_te, dst=wofull, wait_gp=pre_gp)
        for f in tails:
            f()
        tail_gp1 = c.gp
        a_se = c.se
        ve_, se_ = wo_ln1(0, at0, a_ve, a_se, wo_rdy, projV1_te, pre_gp)
        if stop_after == "wo0r":
            _finish(nc, rec, c, sw, sx, ste, sve, sse, sgp, snb, sout,
                    d_dbg, DUMPABLE)
            rec.replay(block)
            return nc
        var0_ve = ln1_var(0, ve_, se_)
        if stop_after == "wo0":
            ln1_apply(0, var0_ve)
            _finish(nc, rec, c, sw, sx, ste, sve, sse, sgp, snb, sout,
                    d_dbg, DUMPABLE)
            rec.replay(block)
            return nc
        ve_, se_ = wo_ln1(1, at1, a_ve, a_se, wo_rdy, projV1_te, tail_gp1)
        ln1_apply(0, var0_ve)
        var1_ve = ln1_var(1, ve_, se_)
        ve_ = ln1_apply(1, var1_ve)

        wo1_te = c.te
        xs_ve = global_scale(ve_)
        w8_rdy = load_w(d_w8T, wo1_te, dst=w8full)
        def _(g):
            g.wait_ge(ste, wo1_te)
            g.dma_start(out=bff[:], in_=d_bf).then_inc(sx, 16)
            c.x += 16
        rec.gpsimd(_)
        bff_x = c.x
        def _(s):
            s.wait_ge(ste, wo1_te)
            s.dma_start(out=wsc[:], in_=d_ws).then_inc(sw, 16)
            c.w += 16
        rec.sync(_)
        wsc_w = c.w

        ffn(xs_ve, bff_x, w8_rdy, wsc_w)

        _finish(nc, rec, c, sw, sx, ste, sve, sse, sgp, snb, sout,
                d_dbg, DUMPABLE)
        rec.replay(block)
    return nc


def _finish(nc, rec, c, sw, sx, ste, sve, sse, sgp, snb, sout, d_dbg,
            DUMPABLE):
    def _(s):
        s.wait_ge(sve, c.ve)
        s.wait_ge(sse, c.se)
        s.wait_ge(ste, c.te)
        s.wait_ge(snb, c.nb)
        for n, dap in d_dbg.items():
            s.dma_start(out=dap, in_=DUMPABLE[n][0][:]).then_inc(sw, 16)
            c.w += 16
        s.wait_ge(sx, c.x)
        s.wait_ge(sw, c.w)
        s.wait_ge(sgp, c.gp)
    rec.sync(_)
    def _(g):
        g.wait_ge(sw, c.w)
    rec.gpsimd(_)


def _prep_shared(inputs):
    f32 = np.float32
    ca = np.ascontiguousarray
    sh = {}
    sh["wqT"] = ca(np.asarray(inputs["L0_attn_Wq"], f32).T)
    sh["wkT"] = ca(np.asarray(inputs["L0_attn_Wk"], f32).T)
    sh["wvT"] = ca(np.asarray(inputs["L0_attn_Wv"], f32).T)
    sh["woT"] = ca(np.asarray(inputs["L0_attn_Wo"], f32).T)
    sh["w8T"] = ca(np.asarray(inputs["L0_ffn_weight_int8"]).T)
    sh["bq_pm"] = ca(np.asarray(inputs["L0_attn_bq"], f32).reshape(NCH, 128).T)
    sh["bk_pm"] = ca(np.asarray(inputs["L0_attn_bk"], f32).reshape(NCH, 128).T)
    sh["bv_bc"] = ca(np.broadcast_to(np.asarray(inputs["L0_attn_bv"], f32), (128, D)))
    sh["bo_bc"] = ca(np.broadcast_to(np.asarray(inputs["L0_attn_bo"], f32), (128, D)))
    sh["bf_bc"] = ca(np.broadcast_to(np.asarray(inputs["L0_ffn_bias"], f32), (128, D)))
    ws = np.asarray(inputs["L0_ffn_weight_scale"], f32)[:, 0]
    sh["ws_bc"] = ca(np.broadcast_to(ws, (128, D)))
    sh["ident"] = np.eye(128, dtype=ml_dtypes.bfloat16)
    sh["mask"] = np.triu(np.ones((128, 128), dtype=ml_dtypes.bfloat16))
    return sh


def kernel(**inputs):
    x = np.asarray(inputs["L0_attn_x"], np.float32)
    sh = _prep_shared(inputs)
    nc = build_program()
    in_maps = []
    for cid in range(8):
        shard = np.ascontiguousarray(x[B * cid:B * cid + B])
        shardT = np.ascontiguousarray(shard.transpose(0, 2, 1))
        m = dict(sh)
        m["x"] = shard
        m["xT"] = shardT
        in_maps.append(m)
    res = run_bass_kernel_spmd(nc, in_maps, core_ids=list(range(8)))
    return np.concatenate([r["out"] for r in res.results], axis=0)


# version 16
# speedup vs baseline: 1.1416x; 1.0175x over previous
import numpy as np
import ml_dtypes
import concourse.bass as bass
import concourse.mybir as mybir
from concourse.bass_utils import run_bass_kernel_spmd
from concourse.hw_specs import get_activation_tables

_ACT_SET_IDX = list(get_activation_tables("gen3").keys()).index(
    "natural_log_exp_and_others")

F32 = mybir.dt.float32
BF16 = mybir.dt.bfloat16
I8 = mybir.dt.int8
AF = mybir.ActivationFunctionType
ALU = mybir.AluOpType
AX = mybir.AxisListType

B = 2
T = 2048
D = 1024
H = 16
HD = 64
NCH = D // 128
KC = T // 128
TC = T // 128
QH = 1024
MAGIC = float(3 * 2 ** 22)
EPS = 1e-5
KB = 1024
VPAIR = 129
VCHUNK = VPAIR * (H // 2)
VW = VCHUNK * KC


def pieces(lo, hi=QH, step=512):
    out, p = [], lo
    while p < hi:
        q = min((p // step + 1) * step, hi)
        out.append((p, q))
        p = q
    return out


class Ctr:
    def __init__(self):
        self.w = 0; self.x = 0; self.te = 0; self.ve = 0
        self.se = 0; self.gp = 0; self.out = 0; self.cc = 0
        self.nb = 0; self.xa = 0; self.xb = 0


class _MockIns:
    __slots__ = ("entry",)
    def __init__(self, entry):
        self.entry = entry
    def then_inc(self, sem, n):
        self.entry[3] = (sem, n)
        return self


class _MockEng:
    __slots__ = ("q", "dma")
    def __init__(self, q, dma=None):
        self.q = q
        self.dma = dma
    def __getattr__(self, name):
        q = self.q
        if name == "dma_start" and self.dma is not None:
            sem, cnt = self.dma
            def call_dma(*a, **k):
                if cnt[0] > 0:
                    q.append(["wait_ge", (sem, 16 * cnt[0]), {}, None])
                cnt[0] += 1
                entry = [name, a, k, None]
                q.append(entry)
                return _MockIns(entry)
            return call_dma
        if name == "dma_start_noser" and self.dma is not None:
            sem, cnt = self.dma
            def call_dma_ns(*a, **k):
                cnt[0] += 1
                entry = ["dma_start", a, k, None]
                q.append(entry)
                return _MockIns(entry)
            return call_dma_ns
        if name == "dma_start_alien":
            def call_dma_al(*a, **k):
                entry = ["dma_start", a, k, None]
                q.append(entry)
                return _MockIns(entry)
            return call_dma_al
        def call(*a, **k):
            entry = [name, a, k, None]
            q.append(entry)
            return _MockIns(entry)
        return call


class _Rec:
    def __init__(self):
        self.q = {n: [] for n in ("tensor", "vector", "scalar", "gpsimd", "sync")}
        self.dma_cfg = {}
    def set_dma(self, engine, sem):
        self.dma_cfg[engine] = (sem, [0])
    def tensor(self, f): f(_MockEng(self.q["tensor"]))
    def vector(self, f): f(_MockEng(self.q["vector"]))
    def scalar(self, f): f(_MockEng(self.q["scalar"]))
    def gpsimd(self, f): f(_MockEng(self.q["gpsimd"], self.dma_cfg.get("gpsimd")))
    def sync(self, f): f(_MockEng(self.q["sync"], self.dma_cfg.get("sync")))
    def replay(self, block):
        def mk(q):
            def run(eng):
                for name, a, k, ti in q:
                    ins = getattr(eng, name)(*a, **k)
                    if ti is not None:
                        ins.then_inc(*ti)
            return run
        block.tensor(mk(self.q["tensor"]))
        block.vector(mk(self.q["vector"]))
        block.scalar(mk(self.q["scalar"]))
        block.gpsimd(mk(self.q["gpsimd"]))
        block.sync(mk(self.q["sync"]))


def build_program(stop_after="full", dump=()):
    nc = bass.Bass("TRN2", target_bir_lowering=False, debug=False)

    d_xT = nc.dram_tensor("xT", [B, D, T], F32, kind="ExternalInput").ap()
    d_x = nc.dram_tensor("x", [B, T, D], F32, kind="ExternalInput").ap()
    d_wqT = nc.dram_tensor("wqT", [D, D], F32, kind="ExternalInput").ap()
    d_wkT = nc.dram_tensor("wkT", [D, D], F32, kind="ExternalInput").ap()
    d_wvT = nc.dram_tensor("wvT", [D, D], F32, kind="ExternalInput").ap()
    d_woT = nc.dram_tensor("woT", [D, D], F32, kind="ExternalInput").ap()
    d_w8T = nc.dram_tensor("w8T", [D, D], I8, kind="ExternalInput").ap()
    d_bq = nc.dram_tensor("bq_pm", [128, NCH], F32, kind="ExternalInput").ap()
    d_bk = nc.dram_tensor("bk_pm", [128, NCH], F32, kind="ExternalInput").ap()
    d_bv = nc.dram_tensor("bv_bc", [128, D], F32, kind="ExternalInput").ap()
    d_bo = nc.dram_tensor("bo_bc", [128, D], F32, kind="ExternalInput").ap()
    d_bf = nc.dram_tensor("bf_bc", [128, D], F32, kind="ExternalInput").ap()
    d_ws = nc.dram_tensor("ws_bc", [128, D], F32, kind="ExternalInput").ap()
    d_id = nc.dram_tensor("ident", [128, 128], BF16, kind="ExternalInput").ap()
    d_mk = nc.dram_tensor("mask", [128, 128], BF16, kind="ExternalInput").ap()
    d_out = nc.dram_tensor("out", [B, T, D], F32, kind="ExternalOutput").ap()
    d_cci = nc.dram_tensor("cc_in", [1, 8], F32).ap()
    d_cco = nc.dram_tensor("cc_out", [1, 8], F32).ap()

    BASE = 16512
    at_ = nc.alloc_sbuf_tensor_at
    cur = [BASE]

    def place(name, shape, dtype, align=64):
        sz = int(np.prod(shape[1:])) * mybir.dt.size(dtype)
        cur[0] = (cur[0] + align - 1) // align * align
        h = at_(name, list(shape), dtype, offset=cur[0])
        cur[0] += sz
        return h

    ident = place("ident", [128, 128], BF16)
    mask = place("mask", [128, 128], BF16)
    bq = place("bq", [128, NCH], F32)
    bk = place("bk", [128, NCH], F32)
    bv = place("bv", [128, D], BF16)
    bo = place("bo", [128, D], BF16)
    epsc = place("epsc", [128, 1], F32)
    magicc = place("magicc", [128, 1], F32)
    nmagicc = place("nmagicc", [128, 1], F32)
    xs_col = place("xs_col", [128, 1], F32)
    rxs_col = place("rxs_col", [128, 1], F32)
    stats = place("stats", [128, 256], F32)
    gmaxc = place("gmaxc", [128, 1], F32)
    gmax1 = place("gmax1", [1, 4], F32)
    xsc_tmp = place("xsc_tmp", [1, 4], F32)
    cc_stage = place("cc_stage", [1, 8], F32)
    rcpt = place("rcpt", [128, 40], F32)
    ones_row = place("ones_row", [1, 128], F32)
    pt0 = place("pt0", [128, QH], BF16)
    pt1 = place("pt1", [128, QH], BF16)
    PT23 = (cur[0] + 63) // 64 * 64
    pt2 = place("pt2", [128, QH], BF16)
    pt3 = place("pt3", [128, QH], BF16)
    tail_bc = at_("tail_bc", [128, QH], F32, offset=PT23)
    assert cur[0] <= BASE + 18 * KB, cur[0]

    UNION = BASE + 18 * KB
    bcastt = [at_("bcast0", [128, QH], F32, offset=UNION),
              at_("bcast1", [128, QH], F32, offset=UNION + 4 * KB)]
    ostg = [at_("ostg0", [128, QH], F32, offset=UNION + 8 * KB),
            at_("ostg1", [128, QH], F32, offset=UNION + 12 * KB),
            at_("ostg2", [128, QH], F32, offset=UNION + 16 * KB),
            at_("ostg3", [128, QH], F32, offset=UNION + 20 * KB)]
    wfull = at_("wfull", [128, NCH * D], BF16, offset=UNION)
    xstage = [at_("xstage0", [128, D], F32, offset=UNION),
              at_("xstage1", [128, D], F32, offset=UNION + 4 * KB)]
    wofull = at_("wofull", [128, NCH * D], BF16, offset=UNION + 8 * KB)
    qtmp = at_("qtmp", [128, D], F32, offset=UNION)
    d1t = at_("d1t", [128, D], F32, offset=UNION + 4 * KB)
    d2t = [at_("d2t0", [128, D], F32, offset=UNION + 8 * KB),
           at_("d2t1", [128, D], F32, offset=UNION + 12 * KB)]
    bff = at_("bff", [128, D], F32, offset=UNION + 16 * KB)
    wsc = at_("wsc", [128, D], F32, offset=UNION + 20 * KB)
    UNION_END = UNION + 24 * KB

    RXT = UNION_END
    xts = at_("xts", [128, NCH * T], BF16, offset=RXT)
    at1 = at_("at1", [128, NCH * T], BF16, offset=RXT)
    w8full = at_("w8full", [128, NCH * D], BF16, offset=RXT)
    xqt = [at_("xqt0", [128, D], BF16, offset=RXT + 16 * KB),
           at_("xqt1", [128, D], BF16, offset=RXT + 18 * KB),
           at_("xqt2", [128, D], BF16, offset=RXT + 20 * KB),
           at_("xqt3", [128, D], BF16, offset=RXT + 22 * KB)]
    ost = [at_("ost0", [128, D], F32, offset=RXT + 24 * KB),
           at_("ost1", [128, D], F32, offset=RXT + 28 * KB)]

    RQ = RXT + 32 * KB
    qt = at_("qt", [128, NCH * T], BF16, offset=RQ)
    RK = RQ + 32 * KB
    kt = at_("kt", [128, NCH * T], BF16, offset=RK)
    ln1_0 = at_("ln1_0", [128, TC * D], F32, offset=RQ)
    RV = RK + 32 * KB
    vaug = at_("vaug", [128, VW], BF16, offset=RV)
    tail_s = [at_("tail_s0", [128, QH], F32, offset=RV),
              at_("tail_s1", [128, QH], F32, offset=RV + 4 * KB)]
    RA0 = RV + 33 * KB
    at0 = at_("at0", [128, NCH * T], BF16, offset=RA0)
    ln1_1 = at_("ln1_1", [128, TC * D], F32, offset=RV)
    assert RA0 + 32 * KB <= 229344, RA0 + 32 * KB

    ATB = [at0, at1]
    LN1 = [ln1_0, ln1_1]
    DUMPABLE = {"qt": (qt, BF16, NCH * T), "kt": (kt, BF16, NCH * T),
                "vaug": (vaug, BF16, VW), "at0": (at0, BF16, NCH * T),
                "at1": (at1, BF16, NCH * T), "ln1_0": (ln1_0, F32, TC * D),
                "ln1_1": (ln1_1, F32, TC * D), "stats": (stats, F32, 256),
                "xs_col": (xs_col, F32, 1)}
    d_dbg = {n: nc.dram_tensor("dbg_" + n, [128, DUMPABLE[n][2]],
                               DUMPABLE[n][1], kind="ExternalOutput").ap()
             for n in dump}

    c = Ctr()

    with (
        nc.psum_tensor("psA", [128, 2048], F32) as psA,
        nc.psum_tensor("psB", [128, 2048], F32) as psB,
        nc.semaphore("sw") as sw,
        nc.semaphore("sx") as sx,
        nc.semaphore("ste") as ste,
        nc.semaphore("sve") as sve,
        nc.semaphore("sse") as sse,
        nc.semaphore("sgp") as sgp,
        nc.semaphore("sout") as sout,
        nc.semaphore("scc") as scc,
        nc.semaphore("snb") as snb,
        nc.semaphore("sxa") as sxa,
        nc.semaphore("sxb") as sxb,
        nc.Block() as block,
    ):
        rec = _Rec()
        rec.set_dma('sync', sw)
        rec.set_dma('gpsimd', sx)
        def _(s):
            s.add_instruction(mybir.InstLoadActFuncSet(
                name=nc.get_next_instruction_name(),
                act_func_set_id=_ACT_SET_IDX, ins=[], outs=[]))
        rec.scalar(_)

        def _(s):
            for dst, src in [(ident, d_id), (mask, d_mk), (bq, d_bq), (bk, d_bk)]:
                s.dma_start(out=dst[:], in_=src).then_inc(sw, 16)
                c.w += 16
        rec.sync(_)
        consts_w = c.w

        def _(g):
            g.dma_start(out=bv[:], in_=d_bv).then_inc(sx, 16); c.x += 16
            g.dma_start(out=bo[:], in_=d_bo).then_inc(sx, 16); c.x += 16
        rec.gpsimd(_)
        bias_x = c.x

        def _(v):
            v.memset(epsc[:], EPS)
            v.memset(magicc[:], MAGIC)
            v.memset(nmagicc[:], -MAGIC)
            v.memset(cc_stage[:], 0.0)
            v.memset(ones_row[:], 1.0)
            v.memset(bass.AP(vaug, 64, [[VW, 128], [VCHUNK, KC],
                                        [VPAIR, H // 2]]), 1.0).then_inc(sve, 1)
            c.ve += 1
        rec.vector(_)
        consts_ve = c.ve
        def _(v):
            v.wait_ge(sx, bias_x)
            v.reduce_sum(rcpt[:, 32:33], bo[:], axis=AX.X).then_inc(sve, 1)
            c.ve += 1
        rec.vector(_)

        def load_xT(b, wait_te):
            def _(g):
                g.wait_ge(ste, wait_te)
                g.wait_ge(sx, c.x)
                for ch in range(NCH):
                    half = sxa if ch < 4 else sxb
                    g.dma_start_alien(
                        out=xts[:, T * ch:T * ch + T],
                        in_=d_xT[b, 128 * ch:128 * ch + 128, :],
                    ).then_inc(half, 16)
                    if ch < 4:
                        c.xa += 16
                    else:
                        c.xb += 16
            rec.gpsimd(_)
            return c.xa, c.xb

        def load_w(src, wait_te, dst=None, wait_ve=None, wait_gp=None):
            dst = wfull if dst is None else dst
            def _(g):
                g.wait_ge(ste, wait_te)
                if wait_ve is not None:
                    g.wait_ge(sve, wait_ve)
                if wait_gp is not None:
                    g.wait_ge(sgp, wait_gp)
                g.dma_start(
                    out=dst.ap().rearrange("p (ch n) -> p ch n", ch=NCH),
                    in_=src.rearrange("(ch p) n -> p ch n", p=128),
                ).then_inc(sx, 16)
                c.x += 16
            rec.gpsimd(_)
            return c.x

        def load_w_chunked(src, wait_te, wait_ve=None, wait_gp=None):
            counts = []
            def _(g):
                for ch in range(NCH):
                    if ch == 0:
                        g.wait_ge(ste, wait_te)
                        if wait_ve is not None:
                            g.wait_ge(sve, wait_ve)
                        if wait_gp is not None:
                            g.wait_ge(sgp, wait_gp)
                    g.dma_start(
                        out=wfull[:, D * ch:D * ch + D],
                        in_=src[128 * ch:128 * ch + 128, :],
                    ).then_inc(sx, 16)
                    c.x += 16
                    counts.append(c.x)
            rec.gpsimd(_)
            return counts

        def proj_qk(bias_col, dst, x_ready, w_counts, prev_ve):
            xa_t, xb_t = x_ready
            evac_after = {}
            entry_se = c.se
            for nch in range(NCH):
                arena = psA if nch % 2 == 0 else psB
                def _(t, nch=nch, arena=arena):
                    if nch < 2:
                        t.wait_ge(sse, entry_se)
                    t.wait_ge(sve, evac_after.get(nch - 2, prev_ve))
                    ins = None
                    for kc in range(NCH):
                        if nch == 0:
                            t.wait_ge(sxa if kc < 4 else sxb,
                                      xa_t if kc < 4 else xb_t)
                            t.wait_ge(sx, w_counts[kc])
                        for tt in range(4):
                            ins = t.matmul(
                                arena[:, 512 * tt:512 * tt + 512],
                                wfull[:, D * kc + 128 * nch:
                                      D * kc + 128 * nch + 128],
                                xts[:, T * kc + 512 * tt:T * kc + 512 * tt + 512],
                                start=(kc == 0), stop=(kc == NCH - 1))
                    ins.then_inc(ste, 1)
                    c.te += 1
                rec.tensor(_)
                te_done = c.te
                def _(v, nch=nch, arena=arena, te_done=te_done):
                    v.wait_ge(ste, te_done)
                    v.wait_ge(sw, consts_w)
                    ins = None
                    for tt in range(4):
                        ins = v.tensor_scalar(
                            dst[:, T * nch + 512 * tt:T * nch + 512 * tt + 512],
                            arena[:, 512 * tt:512 * tt + 512],
                            bias_col[:, nch:nch + 1], None, ALU.add)
                    ins.then_inc(sve, 1)
                    c.ve += 1
                rec.vector(_)
                evac_after[nch] = c.ve
            return c.ve

        def proj_v(x_ready, w_counts, prev_ve, tail_gp=0):
            evac_after = {}
            entry_se = c.se
            if tail_gp:
                def _(v):
                    v.wait_ge(sgp, tail_gp)
                    v.memset(bass.AP(vaug, 64, [[VW, 128], [VCHUNK, KC],
                                                [VPAIR, H // 2]]), 1.0) \
                     .then_inc(sve, 1)
                    c.ve += 1
                rec.vector(_)
            xa_t, xb_t = x_ready
            for tc in range(TC):
                arena = psA if tc % 2 == 0 else psB
                def _(t, tc=tc, arena=arena):
                    if tc < 2:
                        t.wait_ge(sse, entry_se)
                    t.wait_ge(sve, evac_after.get(tc - 2, prev_ve))
                    ins = None
                    for kc in range(NCH):
                        if tc == 0:
                            t.wait_ge(sxa if kc < 4 else sxb,
                                      xa_t if kc < 4 else xb_t)
                            t.wait_ge(sx, w_counts[kc])
                        for nt in range(2):
                            ins = t.matmul(
                                arena[:, 512 * nt:512 * nt + 512],
                                xts[:, T * kc + 128 * tc:T * kc + 128 * tc + 128],
                                wfull[:, D * kc + 512 * nt:D * kc + 512 * nt + 512],
                                start=(kc == 0), stop=(kc == NCH - 1))
                    ins.then_inc(ste, 1)
                    c.te += 1
                rec.tensor(_)
                te_done = c.te
                def _(v, tc=tc, arena=arena, te_done=te_done):
                    v.wait_ge(ste, te_done)
                    v.wait_ge(sx, bias_x)
                    ins = None
                    for nt in range(2):
                        base = VCHUNK * tc + VPAIR * 4 * nt
                        dst = bass.AP(vaug, base,
                                      [[VW, 128], [VPAIR, 4], [65, 2], [1, 64]])
                        src = bass.AP(arena.ap().tensor,
                                      arena.ap().offset + 512 * nt,
                                      [[2048, 128], [128, 4], [64, 2], [1, 64]])
                        b_ap = bass.AP(bv, 512 * nt,
                                       [[D, 128], [128, 4], [64, 2], [1, 64]])
                        ins = v.tensor_tensor(dst, src, b_ap, ALU.add)
                    ins.then_inc(sve, 1)
                    c.ve += 1
                rec.vector(_)
                evac_after[tc] = c.ve
            return c.ve

        def attention(at_dst, ready_ve):
            entry_se = c.se
            entry_gp = c.gp
            seg_i = [0]
            half_i = [0]
            exp_after = {}
            pv_after = {}
            norm_after = {}
            stage_after = {}

            class Stream:
                def __init__(self, h, hf):
                    self.h, self.hf = h, hf
                    self.q0h = QH * hf
                    self.js = [j for j in range(KC) if 128 * j < self.q0h + QH]
                    self.hidx = half_i[0]
                    half_i[0] += 1
                    self.osl = self.hidx % 2
                    self.obuf = psB[:, QH * self.osl:QH * self.osl + QH]
                    self.segs = []
                    self.ch = h // 2
                    self.po = 64 * (h % 2)
                    self.pblk = VPAIR * (h // 2 % (H // 2))

            def emit_s(st, j):
                qs = max(st.q0h, 128 * j)
                lo = qs - st.q0h
                si = seg_i[0]
                sp = si % 2
                sps = psA[:, QH * sp:QH * sp + QH]
                ptb = (pt0, pt1, pt2, pt3)[si % 4]
                def _(t):
                    t.wait_ge(sve, ready_ve)
                    if si >= 2:
                        t.wait_ge(sse, exp_after[si - 2])
                    else:
                        t.wait_ge(sse, entry_se)
                    ins = None
                    for (plo, phi) in pieces(lo):
                        ins = t.matmul(
                            sps[:, plo:phi],
                            kt[st.po:st.po + 64,
                               T * st.ch + 128 * j:T * st.ch + 128 * j + 128],
                            qt[st.po:st.po + 64,
                               T * st.ch + st.q0h + plo:T * st.ch + st.q0h + phi],
                            start=True, stop=True)
                    ins.then_inc(ste, 1)
                    c.te += 1
                rec.tensor(_)
                s_done = c.te
                def _(sc):
                    sc.wait_ge(ste, s_done)
                    if si >= 4:
                        sc.wait_ge(ste, pv_after[si - 4])
                    sc.activation(ptb[:, lo:QH], sps[:, lo:QH], AF.Exp,
                                  scale=0.125).then_inc(sse, 1)
                    c.se += 1
                rec.scalar(_)
                exp_after[si] = c.se
                e_done = c.se
                m_done = None
                if qs == 128 * j:
                    def _(v):
                        v.wait_ge(sse, e_done)
                        v.tensor_tensor(ptb[:, lo:lo + 128],
                                        ptb[:, lo:lo + 128], mask[:],
                                        ALU.mult).then_inc(sve, 1)
                        c.ve += 1
                    rec.vector(_)
                    m_done = c.ve
                st.segs.append((j, lo, ptb, e_done, m_done, si))
                seg_i[0] += 1

            def emit_pv(st, k):
                (j, lo, ptb, e_done, m_done, si) = st.segs[k]
                first = (k == 0)
                last_j = st.js[-1]
                def _(t):
                    if first:
                        if st.hidx >= 2:
                            t.wait_ge(sve, stage_after[st.hidx - 2])
                        else:
                            t.wait_ge(sve, ready_ve)
                    if m_done is not None:
                        t.wait_ge(sve, m_done)
                    else:
                        t.wait_ge(sse, e_done)
                    if st.h % 2 == 0:
                        lhs = vaug[:, VCHUNK * j + st.pblk:
                                   VCHUNK * j + st.pblk + 65]
                        nrow = 65
                    else:
                        lhs = vaug[:, VCHUNK * j + st.pblk + 1:
                                   VCHUNK * j + st.pblk + 129]
                        nrow = 128
                    ins = None
                    for (plo, phi) in pieces(lo):
                        bank_last = min((st.q0h + phi - 1) // 128, last_j)
                        ins = t.matmul(st.obuf[0:nrow, plo:phi], lhs,
                                       ptb[:, plo:phi],
                                       start=(j == 0), stop=(j == bank_last))
                    ins.then_inc(ste, 1)
                    c.te += 1
                rec.tensor(_)
                pv_after[si] = c.te

            def emit_norm_stage(st):
                pv_done = pv_after[st.segs[-1][5]]
                stg = tail_s[st.hidx - 2 * H + 2] if st.hidx >= 2 * H - 2 \
                    else ostg[st.hidx % 4]
                even = (st.h % 2 == 0)
                def _(v):
                    v.wait_ge(ste, pv_done)
                    v.wait_ge(sgp, norm_after.get(st.hidx - 4, entry_gp))
                    if even:
                        v.tensor_copy(stg[0:65, :], st.obuf[0:65, :]) \
                         .then_inc(sve, 1)
                    else:
                        v.tensor_copy(stg[32:64, :], st.obuf[32:64, :])
                        v.tensor_copy(stg[64:128, :], st.obuf[64:128, :]) \
                         .then_inc(sve, 1)
                    c.ve += 1
                    if not even:
                        v.wait_ge(sve, c.ve)
                        v.tensor_scalar(stg[32:64, :], stg[32:64, :], 1e-20,
                                        1e18, ALU.max, ALU.min).then_inc(sve, 1)
                        c.ve += 1
                rec.vector(_)
                stage_after[st.hidx] = c.ve
                return c.ve

            def emit_norm_finish(st, cp_done):
                h, ch, q0h = st.h, st.ch, st.q0h
                tail = st.hidx >= 2 * H - 2
                stg = tail_s[st.hidx - 2 * H + 2] if tail \
                    else ostg[st.hidx % 4]
                bcast = tail_bc if tail else bcastt[(st.hidx % 4) // 2]
                even = (h % 2 == 0)
                dr = stg[64:65, :] if even else stg[32:64, :]
                rr = stg[96:97, :] if even else stg[0:32, :]
                def _(sc):
                    sc.wait_ge(sve, cp_done)
                    sc.activation(rr, dr, AF.Ln).then_inc(sse, 1)
                    c.se += 1
                    sc.wait_ge(sse, c.se)
                    sc.activation(rr, rr, AF.Exp, scale=-1.0).then_inc(sse, 1)
                    c.se += 1
                rec.scalar(_)
                r_done = c.se
                r0, r1 = (0, 64) if even else (64, 128)
                if tail:
                    tail_te = c.te
                    def _(g):
                        g.wait_ge(sse, r_done)
                        g.wait_ge(ste, tail_te)
                        rsrc = stg[96:97, :] if even else stg[31:32, :]
                        g.dma_start_alien(
                            out=bcast[r0:r1, :],
                            in_=rsrc.unsqueeze(1)
                                .broadcast_to((1, r1 - r0, QH))) \
                         .then_inc(snb, 16)
                        c.nb += 16
                    rec.gpsimd(_)
                    b_done = c.nb
                else:
                    def _(sc):
                        sc.wait_ge(sse, r_done)
                        rsrc = stg[96:97, :] if even else stg[31:32, :]
                        sc.dma_start(
                            out=bcast[r0:r1, :],
                            in_=rsrc.unsqueeze(1)
                                .broadcast_to((1, r1 - r0, QH))) \
                         .then_inc(sw, 16)
                        c.w += 16
                    rec.sync(_)
                    b_done = c.w
                def _(g):
                    g.wait_ge(snb if tail else sw, b_done)
                    g.tensor_tensor(
                        at_dst[r0:r1, T * ch + q0h:T * ch + q0h + QH],
                        stg[0:64, :] if even else stg[64:128, :],
                        bcast[r0:r1, :], ALU.mult).then_inc(sgp, 1)
                    c.gp += 1
                rec.gpsimd(_)
                norm_after[st.hidx] = c.gp

            pending = []
            for hp in range(0, H, 2):
                for hf in range(2):
                    sa = Stream(hp, hf)
                    sb = Stream(hp + 1, hf)
                    n = len(sa.js)
                    for k in range(n):
                        emit_s(sa, sa.js[k])
                        emit_s(sb, sb.js[k])
                        if k in (2, 5) and pending:
                            emit_norm_finish(*pending.pop(0))
                        if k >= 1:
                            emit_pv(sa, k - 1)
                            emit_pv(sb, k - 1)
                    emit_pv(sa, n - 1)
                    emit_pv(sb, n - 1)
                    pending.append((sa, emit_norm_stage(sa)))
                    pending.append((sb, emit_norm_stage(sb)))
            thunks = [(lambda st=st, cp=cp: emit_norm_finish(st, cp))
                      for st, cp in pending]
            return c.ve, c.gp, thunks

        def wo_ln1(bidx, at_src, attn_ve, attn_se, wo_ready, first_xstage_te,
                   attn_gp=0, mid_hook=None, gp_bias=False, extra_ve=None):
            ln1 = LN1[bidx]
            ev_after = {}
            entry_ve = c.ve
            extra_ve = list(extra_ve) if extra_ve else []
            for tc in range(TC):
                if mid_hook is not None and tc in (8, 12):
                    mid_hook(tc)
                arena = psA if tc % 2 == 0 else psB
                xst = xstage[tc % 2]
                def _(s):
                    if tc < 2:
                        s.wait_ge(ste, first_xstage_te)
                        s.wait_ge(sve, entry_ve)
                        s.wait_ge(sgp, attn_gp)
                    else:
                        s.wait_ge(sve, ev_after[tc - 2])
                    s.dma_start(out=xst[:],
                                in_=d_x[bidx, 128 * tc:128 * tc + 128, :]) \
                        .then_inc(sw, 16)
                    c.w += 16
                rec.sync(_)
                x_done = c.w
                def _(t):
                    t.wait_ge(sve, attn_ve)
                    t.wait_ge(sse, attn_se)
                    t.wait_ge(sgp, attn_gp)
                    t.wait_ge(sx, wo_ready)
                    t.wait_ge(sve, ev_after.get(tc - 2, entry_ve))
                    ins = None
                    for dch in range(NCH):
                        for nt in range(2):
                            ins = t.matmul(
                                arena[:, 512 * nt:512 * nt + 512],
                                at_src[:, T * dch + 128 * tc:
                                       T * dch + 128 * tc + 128],
                                wofull[:, D * dch + 512 * nt:
                                       D * dch + 512 * nt + 512],
                                start=(dch == 0), stop=(dch == NCH - 1))
                    ins.then_inc(ste, 1)
                    c.te += 1
                rec.tensor(_)
                mm_done = c.te
                si = bidx * TC + tc
                lt = ln1[:, D * tc:D * tc + D]
                gp_done = None
                if gp_bias:
                    def _(v):
                        v.wait_ge(ste, mm_done)
                        v.wait_ge(sw, x_done)
                        v.tensor_tensor(lt, arena[:, 0:D], xst[:], ALU.add) \
                         .then_inc(sve, 1)
                        c.ve += 1
                        v.wait_ge(sve, c.ve)
                        v.reduce_sum(stats[:, si:si + 1], lt, axis=AX.X) \
                         .then_inc(sve, 1)
                        c.ve += 1
                        if tc >= 2 and extra_ve:
                            extra_ve.pop(0)(v)
                    rec.vector(_)
                    ve_mid = c.ve
                    def _(g):
                        g.wait_ge(sve, ve_mid)
                        g.wait_ge(sx, bias_x)
                        g.tensor_tensor(lt, lt, bo[:], ALU.add).then_inc(sgp, 1)
                        c.gp += 1
                    rec.gpsimd(_)
                    gp_done = c.gp
                elif True:
                    def _(v):
                        v.wait_ge(ste, mm_done)
                        v.wait_ge(sw, x_done)
                        v.tensor_tensor(lt, arena[:, 0:D], bo[:], ALU.add) \
                         .then_inc(sve, 1)
                        c.ve += 1
                        v.wait_ge(sve, c.ve)
                        v.tensor_tensor(lt, lt, xst[:], ALU.add).then_inc(sve, 1)
                        c.ve += 1
                        v.wait_ge(sve, c.ve)
                        v.reduce_sum(stats[:, si:si + 1], lt, axis=AX.X) \
                         .then_inc(sve, 1)
                        c.ve += 1
                    rec.vector(_)
                r1_ve = c.ve
                def _(s):
                    if gp_done is not None:
                        s.wait_ge(sgp, gp_done)
                    else:
                        s.wait_ge(sve, r1_ve - 1)
                    s.wait_ge(sse, c.se)
                    s.activation(pt0[:], lt, AF.Square,
                                 accum_out=stats[:, 64 + si:64 + si + 1]) \
                     .then_inc(sse, 1)
                    c.se += 1
                rec.scalar(_)
                ev_after[tc] = c.ve
            return c.ve, c.se

        def ln1_var(bidx, prev_ve, prev_se, t0=0, t1=TC, bias_col=None):
            s0 = bidx * TC + t0
            n = t1 - t0
            sums = stats[:, s0:s0 + n]
            sumsq = stats[:, 64 + s0:64 + s0 + n]
            tcol = stats[:, 128 + s0:128 + s0 + n]
            ucol = stats[:, 160 + s0:160 + s0 + n]
            mean = stats[:, 192 + s0:192 + s0 + n]
            sigc = stats[:, 96 + s0:96 + s0 + n]
            def _(v):
                v.wait_ge(sve, prev_ve)
                v.wait_ge(sse, prev_se)
                if bias_col is not None:
                    v.tensor_scalar(sums, sums, bias_col, None, ALU.add) \
                     .then_inc(sve, 1)
                    c.ve += 1
                    v.wait_ge(sve, c.ve)
                v.scalar_tensor_tensor(tcol, sums, 1.0 / D, sums,
                                       ALU.mult, ALU.mult).then_inc(sve, 1)
                c.ve += 1
                v.tensor_scalar(mean, sums, 1.0 / D, None, ALU.mult) \
                 .then_inc(sve, 1)
                c.ve += 1
                v.wait_ge(sve, c.ve)
                v.tensor_tensor(ucol, sumsq, tcol, ALU.subtract).then_inc(sve, 1)
                c.ve += 1
                v.wait_ge(sve, c.ve)
                v.tensor_copy(rcpt[:, 38:39], ucol[:, 0:1]).then_inc(sve, 1)
                c.ve += 1
            rec.vector(_)
            u_ve = c.ve
            rstd = rcpt[:, 16 * bidx + t0:16 * bidx + t1]
            def _(s):
                s.wait_ge(sve, u_ve)
                s.activation(sigc, ucol, AF.Ln, bias=epsc[:], scale=1.0 / D) \
                 .then_inc(sse, 1)
                c.se += 1
                s.wait_ge(sse, c.se)
                s.activation(rstd, sigc, AF.Exp, scale=-0.5).then_inc(sse, 1)
                c.se += 1
                s.wait_ge(sse, c.se)
                s.copy(rcpt[:, 39:40], rstd[:, 0:1]).then_inc(sse, 1)
                c.se += 1
            rec.scalar(_)
            sig_se = c.se
            nmrs = stats[:, 128 + s0:128 + s0 + n]
            def _(v):
                v.wait_ge(sse, sig_se)
                v.scalar_tensor_tensor(nmrs, mean, -1.0, rstd,
                                       ALU.mult, ALU.mult).then_inc(sve, 1)
                c.ve += 1
                v.wait_ge(sve, c.ve)
                v.tensor_copy(rcpt[:, 38:39], nmrs[:, 0:1]).then_inc(sve, 1)
                c.ve += 1
            rec.vector(_)
            return c.ve

        def ln1_apply(bidx, prev_ve, t0=0, t1=TC, chunk=None):
            ln1 = LN1[bidx]
            rstd = rcpt[:, 16 * bidx:16 * bidx + TC]
            nmrs = stats[:, 128 + bidx * TC:128 + bidx * TC + TC]
            chunk = (t1 - t0) if chunk is None else chunk
            se_by_tile = {}
            def _(s):
                s.wait_ge(sve, prev_ve)
                for c0 in range(t0, t1, chunk):
                    ins = None
                    for tc in range(c0, min(c0 + chunk, t1)):
                        lt = ln1[:, D * tc:D * tc + D]
                        ins = s.activation(lt, lt, AF.Identity,
                                           bias=nmrs[:, tc:tc + 1],
                                           scale=rstd[:, tc:tc + 1])
                    ins.then_inc(sse, 1)
                    c.se += 1
                    for tc in range(c0, min(c0 + chunk, t1)):
                        se_by_tile[tc] = c.se
            rec.scalar(_)
            return se_by_tile

        def ln1_max(bidx, apply_se, t0=0, t1=TC):
            ln1 = LN1[bidx]
            s0 = bidx * TC
            def _(v):
                v.wait_ge(sse, apply_se)
                ins = None
                for tc in range(t0, t1):
                    lt = ln1[:, D * tc:D * tc + D]
                    ins = v.reduce_max(stats[:, 224 + s0 + tc:224 + s0 + tc + 1],
                                       lt, axis=AX.X, apply_absolute_value=True)
                ins.then_inc(sve, 1)
                c.ve += 1
            rec.vector(_)
            return c.ve

        def global_scale(prev_ve):
            def _(v):
                v.wait_ge(sve, prev_ve)
                v.reduce_max(gmaxc[:], stats[:, 224:224 + 2 * TC], axis=AX.X) \
                 .then_inc(sve, 1)
                c.ve += 1
                v.wait_ge(sve, c.ve)
                v.tensor_copy(rcpt[:, 38:39], gmaxc[:]).then_inc(sve, 1)
                c.ve += 1
            rec.vector(_)
            g_ve = c.ve
            def _(g):
                g.wait_ge(sve, g_ve)
                g.reduce_max(gmax1[0:1, 0:1], gmaxc[:], axis=AX.XYZWC) \
                 .then_inc(sgp, 1)
                c.gp += 1
                g.wait_ge(sgp, c.gp)
                g.tensor_copy(gmax1[0:1, 3:4], gmax1[0:1, 0:1]).then_inc(sgp, 1)
                c.gp += 1
            rec.gpsimd(_)
            g_gp = c.gp
            def _(v):
                v.wait_ge(sgp, g_gp)
                v.tensor_scalar(cc_stage[0:1, 0:1], gmax1[0:1, 0:1],
                                1.0 / 127.0, EPS, ALU.mult, ALU.max) \
                 .then_inc(sve, 1)
                c.ve += 1
                v.wait_ge(sve, c.ve)
                v.tensor_copy(xsc_tmp[0:1, 1:2], cc_stage[0:1, 0:1]) \
                 .then_inc(sve, 1)
                c.ve += 1
            rec.vector(_)
            st_ve = c.ve
            def _(g):
                g.wait_ge(sve, st_ve)
                g.dma_start(out=d_cci, in_=cc_stage[:]).then_inc(sx, 16)
                c.x += 16
                g.wait_ge(sx, c.x)
                g.collective_compute(
                    "AllReduce", ALU.max, replica_groups=[list(range(8))],
                    ins=[d_cci.opt()], outs=[d_cco.opt()]).then_inc(scc, 1)
                c.cc += 1
                g.wait_ge(scc, c.cc)
                g.dma_start(out=xsc_tmp[0:1, 0:1], in_=d_cco[0:1, 0:1]) \
                 .then_inc(sx, 16)
                c.x += 16
            rec.gpsimd(_)
            in_x = c.x
            def _(v):
                v.wait_ge(sx, in_x)
                v.reciprocal(gmax1[0:1, 2:3], xsc_tmp[0:1, 0:1]).then_inc(sve, 1)
                c.ve += 1
                v.wait_ge(sve, c.ve)
                v.tensor_copy(xsc_tmp[0:1, 1:2], gmax1[0:1, 2:3]) \
                 .then_inc(sve, 1)
                c.ve += 1
                v.wait_ge(sve, c.ve)
                v.tensor_copy(gmax1[0:1, 3:4], xsc_tmp[0:1, 1:2]) \
                 .then_inc(sve, 1)
                c.ve += 1
            rec.vector(_)
            rx_ve = c.ve
            def _(t):
                t.wait_ge(sve, rx_ve)
                t.matmul(psA[:, 0:2], ones_row[:], xsc_tmp[0:1, 0:2],
                         start=True, stop=True).then_inc(ste, 1)
                c.te += 1
            rec.tensor(_)
            bc_te = c.te
            def _(sc):
                sc.wait_ge(ste, bc_te)
                sc.activation(xs_col[:], psA[:, 0:1], AF.Identity)
                sc.activation(rxs_col[:], psA[:, 1:2], AF.Identity) \
                 .then_inc(sse, 1)
                c.se += 1
            rec.scalar(_)
            return c.se

        def ffn(xs_ve, bff_x, w8_ready, wsbf_w):
            N = B * TC
            copy_after = {}
            mm_after = {}
            deq_after = {}
            dv_after = {}
            u2_after = {}
            sq_after = {}
            restve_after = {}
            out_after = {}

            def head(i):
                bidx, tc = i // TC, i % TC
                par4 = i % 4
                lt = LN1[bidx][:, D * tc:D * tc + D]
                ptr = psB[:, 512 * par4:512 * par4 + 512].bitcast(BF16)
                pfa = psA[:, QH * (i % 2):QH * (i % 2) + QH]
                xq = xqt[par4]
                def _(sc):
                    sc.wait_ge(sse, xs_ve)
                    sc.wait_ge(sx, bff_x)
                    sc.wait_ge(sw, wsbf_w)
                    sc.wait_ge(sse, c.se)
                    if i >= 4:
                        sc.wait_ge(ste, mm_after[i - 4])
                    sc.activation(qtmp[:], lt, AF.Identity, bias=magicc[:],
                                  scale=rxs_col[:]).then_inc(sse, 1)
                    c.se += 1
                    sc.wait_ge(sse, c.se)
                    sc.activation(xq[:], qtmp[:], AF.Identity,
                                  bias=nmagicc[:], scale=1.0).then_inc(sse, 1)
                    c.se += 1
                rec.scalar(_)
                q2_se = c.se
                def _(t):
                    t.wait_ge(sse, q2_se)
                    if i >= 4:
                        t.wait_ge(sve, copy_after[i - 4])
                    ins = None
                    for cc8 in range(NCH):
                        ins = t.transpose(ptr[:, 128 * cc8:128 * cc8 + 128],
                                          xq[:, 128 * cc8:128 * cc8 + 128],
                                          ident[:])
                    ins.then_inc(ste, 1)
                    c.te += 1
                rec.tensor(_)
                tr_done = c.te
                def _(v):
                    v.wait_ge(ste, tr_done)
                    v.tensor_copy(xq[:], ptr[:]).then_inc(sve, 1)
                    c.ve += 1
                rec.vector(_)
                copy_after[i] = c.ve
                def _(t):
                    t.wait_ge(sve, copy_after[i])
                    t.wait_ge(sx, w8_ready)
                    if i >= 2:
                        t.wait_ge(sve, deq_after[i - 2])
                    ins = None
                    for kc8 in range(NCH):
                        for nt in range(2):
                            ins = t.matmul(
                                pfa[:, 512 * nt:512 * nt + 512],
                                xq[:, 128 * kc8:128 * kc8 + 128],
                                w8full[:, D * kc8 + 512 * nt:
                                       D * kc8 + 512 * nt + 512],
                                start=(kc8 == 0), stop=(kc8 == NCH - 1))
                    ins.then_inc(ste, 1)
                    c.te += 1
                rec.tensor(_)
                mm_after[i] = c.te

            def deqblk(i):
                bidx, tc = i // TC, i % TC
                par = i % 2
                lt = LN1[bidx][:, D * tc:D * tc + D]
                pfa = psA[:, QH * par:QH * par + QH]
                si = 240 + 8 * par
                def _(v):
                    v.wait_ge(ste, mm_after[i])
                    if i >= 2:
                        v.wait_ge(sve, restve_after[i - 2])
                        v.wait_ge(sse, sq_after[i - 2])
                    v.scalar_tensor_tensor(d1t[:], pfa[:, 0:D], xs_col[:],
                                           wsc[:], ALU.mult, ALU.mult) \
                     .then_inc(sve, 1)
                    c.ve += 1
                    deq_after[i] = c.ve
                    v.wait_ge(sve, c.ve)
                    v.tensor_tensor(d2t[par][:], d1t[:], bff[:], ALU.add) \
                     .then_inc(sve, 1)
                    c.ve += 1
                    v.wait_ge(sve, c.ve)
                    v.tensor_tensor(d2t[par][:], d2t[par][:], lt, ALU.add) \
                     .then_inc(sve, 1)
                    c.ve += 1
                    v.wait_ge(sve, c.ve)
                    v.reduce_sum(stats[:, si:si + 1], d2t[par][:], axis=AX.X) \
                     .then_inc(sve, 1)
                    c.ve += 1
                rec.vector(_)
                dv_after[i] = c.ve

            def rest_a(i):
                bidx, tc = i // TC, i % TC
                par = i % 2
                si = 240 + 8 * par
                def _(sc):
                    sc.wait_ge(sve, dv_after[i] - 1)
                    sc.wait_ge(sse, c.se)
                    sc.activation(pt0[:], d2t[par][:], AF.Square,
                                  accum_out=stats[:, si + 1:si + 2]) \
                     .then_inc(sse, 1)
                    c.se += 1
                rec.scalar(_)
                sq_after[i] = c.se
                def _(v):
                    v.wait_ge(sve, dv_after[i])
                    v.wait_ge(sse, sq_after[i])
                    v.scalar_tensor_tensor(stats[:, si + 2:si + 3],
                                           stats[:, si:si + 1], 1.0 / D,
                                           stats[:, si:si + 1],
                                           ALU.mult, ALU.mult).then_inc(sve, 1)
                    c.ve += 1
                    v.tensor_scalar(stats[:, si + 3:si + 4],
                                    stats[:, si:si + 1], 1.0 / D, None,
                                    ALU.mult).then_inc(sve, 1)
                    c.ve += 1
                    v.wait_ge(sve, c.ve)
                    v.tensor_tensor(stats[:, si + 4:si + 5],
                                    stats[:, si + 1:si + 2],
                                    stats[:, si + 2:si + 3], ALU.subtract) \
                     .then_inc(sve, 1)
                    c.ve += 1
                    v.wait_ge(sve, c.ve)
                    v.tensor_copy(rcpt[:, 38:39],
                                  stats[:, si + 4:si + 5]).then_inc(sve, 1)
                    c.ve += 1
                rec.vector(_)
                u2_after[i] = c.ve

            def rest_b(i):
                bidx, tc = i // TC, i % TC
                par = i % 2
                si = 240 + 8 * par
                def _(sc):
                    sc.wait_ge(sve, u2_after[i])
                    sc.activation(stats[:, si + 5:si + 6],
                                  stats[:, si + 4:si + 5], AF.Ln,
                                  bias=epsc[:], scale=1.0 / D).then_inc(sse, 1)
                    c.se += 1
                    sc.wait_ge(sse, c.se)
                    sc.activation(rcpt[:, 34 + par:35 + par],
                                  stats[:, si + 5:si + 6], AF.Exp,
                                  scale=-0.5).then_inc(sse, 1)
                    c.se += 1
                    sc.wait_ge(sse, c.se)
                    sc.copy(rcpt[:, 36:37],
                            rcpt[:, 34 + par:35 + par]).then_inc(sse, 1)
                    c.se += 1
                rec.scalar(_)
                sig2_se = c.se
                def _(v):
                    v.wait_ge(sse, sig2_se)
                    if i >= 2:
                        v.wait_ge(sw, out_after[i - 2])
                    v.tensor_scalar(ost[par][:], d2t[par][:],
                                    stats[:, si + 3:si + 4],
                                    rcpt[:, 34 + par:35 + par],
                                    ALU.subtract, ALU.mult).then_inc(sve, 1)
                    c.ve += 1
                rec.vector(_)
                restve_after[i] = c.ve
                n2_ve = c.ve
                def _(sc):
                    sc.wait_ge(sve, n2_ve)
                    sc.dma_start(out=d_out[bidx, 128 * tc:128 * tc + 128, :],
                                 in_=ost[par][:]).then_inc(sw, 16)
                    c.w += 16
                rec.sync(_)
                out_after[i] = c.w

            for i in range(N):
                head(i)
                if i >= 3:
                    rest_b(i - 3)
                if i >= 1:
                    deqblk(i - 1)
                if i >= 2:
                    rest_a(i - 2)
            rest_b(N - 3)
            deqblk(N - 1)
            rest_a(N - 2)
            rest_b(N - 2)
            rest_a(N - 1)
            rest_b(N - 1)
            return c.w

        x_rdy = load_xT(0, 0)
        w_cts = load_w_chunked(d_wqT, 0)
        ve_ = proj_qk(bq, qt, x_rdy, w_cts, consts_ve)
        w_cts = load_w_chunked(d_wkT, c.te)
        ve_ = proj_qk(bk, kt, x_rdy, w_cts, ve_)
        w_cts = load_w_chunked(d_wvT, c.te)
        ve_ = proj_v(x_rdy, w_cts, ve_)
        projV0_te = c.te

        if stop_after == "proj0":
            _finish(nc, rec, c, sw, sx, ste, sve, sse, sgp, snb, sout,
                    d_dbg, DUMPABLE)
            rec.replay(block)
            return nc

        x_rdy1 = load_xT(1, projV0_te)
        a_ve, pre_gp, tails = attention(at0, ve_)
        if stop_after == "attn0":
            for f in tails:
                f()
            _finish(nc, rec, c, sw, sx, ste, sve, sse, sgp, snb, sout,
                    d_dbg, DUMPABLE)
            rec.replay(block)
            return nc

        w_cts = load_w_chunked(d_wqT, projV0_te, wait_gp=pre_gp)
        for f in tails:
            f()
        tail_gp0 = c.gp
        ve_ = proj_qk(bq, qt, x_rdy1, w_cts, a_ve)
        w_cts = load_w_chunked(d_wkT, c.te)
        ve_ = proj_qk(bk, kt, x_rdy1, w_cts, ve_)
        w_cts = load_w_chunked(d_wvT, c.te)
        ve_ = proj_v(x_rdy1, w_cts, ve_, tail_gp=tail_gp0)
        projV1_te = c.te
        a_ve, pre_gp, tails = attention(at1, ve_)

        wo_rdy = load_w(d_woT, projV1_te, dst=wofull, wait_gp=pre_gp)
        for f in tails:
            f()
        tail_gp1 = c.gp
        a_se = c.se
        ve_, se_ = wo_ln1(0, at0, a_ve, a_se, wo_rdy, projV1_te, pre_gp)
        if stop_after == "wo0r":
            _finish(nc, rec, c, sw, sx, ste, sve, sse, sgp, snb, sout,
                    d_dbg, DUMPABLE)
            rec.replay(block)
            return nc
        var0_ve = ln1_var(0, ve_, se_)
        if stop_after == "wo0":
            ln1_apply(0, var0_ve)
            _finish(nc, rec, c, sw, sx, ste, sve, sse, sgp, snb, sout,
                    d_dbg, DUMPABLE)
            rec.replay(block)
            return nc
        ap0 = ln1_apply(0, var0_ve, chunk=4)
        h1se = {}
        def _mid1(tc):
            t0, t1 = (0, 8) if tc == 8 else (8, 12)
            v_ = ln1_var(1, c.ve, c.se, t0, t1, bias_col=rcpt[:, 32:33])
            h1se.update(ln1_apply(1, v_, t0, t1))
        extra = []
        for tcx in range(TC):
            def _mx(v, tcx=tcx):
                v.wait_ge(sse, ap0[tcx])
                v.reduce_max(stats[:, 224 + tcx:224 + tcx + 1],
                             ln1_0[:, D * tcx:D * tcx + D],
                             axis=AX.X, apply_absolute_value=True) \
                 .then_inc(sve, 1)
                c.ve += 1
            extra.append(_mx)
        ve_, se_ = wo_ln1(1, at1, a_ve, a_se, wo_rdy, projV1_te, tail_gp1,
                          mid_hook=_mid1, gp_bias=True, extra_ve=extra)
        for f in extra[14:]:
            rec.vector(f)
        var1_ve = ln1_var(1, ve_, se_, 12, TC, bias_col=rcpt[:, 32:33])
        ln1_max(1, max(h1se.values()), 0, 12)
        ap1 = ln1_apply(1, var1_ve, 12, TC)
        ve_ = ln1_max(1, max(ap1.values()), 12, TC)

        wo1_te = c.te
        xs_ve = global_scale(ve_)
        w8_rdy = load_w(d_w8T, wo1_te, dst=w8full)
        def _(g):
            g.wait_ge(ste, wo1_te)
            g.dma_start(out=bff[:], in_=d_bf).then_inc(sx, 16)
            c.x += 16
        rec.gpsimd(_)
        bff_x = c.x
        def _(s):
            s.wait_ge(ste, wo1_te)
            s.dma_start(out=wsc[:], in_=d_ws).then_inc(sw, 16)
            c.w += 16
        rec.sync(_)
        wsc_w = c.w

        ffn(xs_ve, bff_x, w8_rdy, wsc_w)

        _finish(nc, rec, c, sw, sx, ste, sve, sse, sgp, snb, sout,
                d_dbg, DUMPABLE)
        rec.replay(block)
    return nc


def _finish(nc, rec, c, sw, sx, ste, sve, sse, sgp, snb, sout, d_dbg,
            DUMPABLE):
    def _(s):
        s.wait_ge(sve, c.ve)
        s.wait_ge(sse, c.se)
        s.wait_ge(ste, c.te)
        s.wait_ge(snb, c.nb)
        for n, dap in d_dbg.items():
            s.dma_start(out=dap, in_=DUMPABLE[n][0][:]).then_inc(sw, 16)
            c.w += 16
        s.wait_ge(sx, c.x)
        s.wait_ge(sw, c.w)
        s.wait_ge(sgp, c.gp)
    rec.sync(_)
    def _(g):
        g.wait_ge(sw, c.w)
    rec.gpsimd(_)


def _prep_shared(inputs):
    f32 = np.float32
    ca = np.ascontiguousarray
    sh = {}
    sh["wqT"] = ca(np.asarray(inputs["L0_attn_Wq"], f32).T)
    sh["wkT"] = ca(np.asarray(inputs["L0_attn_Wk"], f32).T)
    sh["wvT"] = ca(np.asarray(inputs["L0_attn_Wv"], f32).T)
    sh["woT"] = ca(np.asarray(inputs["L0_attn_Wo"], f32).T)
    sh["w8T"] = ca(np.asarray(inputs["L0_ffn_weight_int8"]).T)
    sh["bq_pm"] = ca(np.asarray(inputs["L0_attn_bq"], f32).reshape(NCH, 128).T)
    sh["bk_pm"] = ca(np.asarray(inputs["L0_attn_bk"], f32).reshape(NCH, 128).T)
    sh["bv_bc"] = ca(np.broadcast_to(np.asarray(inputs["L0_attn_bv"], f32), (128, D)))
    sh["bo_bc"] = ca(np.broadcast_to(np.asarray(inputs["L0_attn_bo"], f32), (128, D)))
    sh["bf_bc"] = ca(np.broadcast_to(np.asarray(inputs["L0_ffn_bias"], f32), (128, D)))
    ws = np.asarray(inputs["L0_ffn_weight_scale"], f32)[:, 0]
    sh["ws_bc"] = ca(np.broadcast_to(ws, (128, D)))
    sh["ident"] = np.eye(128, dtype=ml_dtypes.bfloat16)
    sh["mask"] = np.triu(np.ones((128, 128), dtype=ml_dtypes.bfloat16))
    return sh


def kernel(**inputs):
    x = np.asarray(inputs["L0_attn_x"], np.float32)
    sh = _prep_shared(inputs)
    nc = build_program()
    in_maps = []
    for cid in range(8):
        shard = np.ascontiguousarray(x[B * cid:B * cid + B])
        shardT = np.ascontiguousarray(shard.transpose(0, 2, 1))
        m = dict(sh)
        m["x"] = shard
        m["xT"] = shardT
        in_maps.append(m)
    res = run_bass_kernel_spmd(nc, in_maps, core_ids=list(range(8)))
    return np.concatenate([r["out"] for r in res.results], axis=0)


# version 17
# speedup vs baseline: 1.1425x; 1.0008x over previous
import numpy as np
import ml_dtypes
import concourse.bass as bass
import concourse.mybir as mybir
from concourse.bass_utils import run_bass_kernel_spmd
from concourse.hw_specs import get_activation_tables

_ACT_SET_IDX = list(get_activation_tables("gen3").keys()).index(
    "natural_log_exp_and_others")

F32 = mybir.dt.float32
BF16 = mybir.dt.bfloat16
I8 = mybir.dt.int8
AF = mybir.ActivationFunctionType
ALU = mybir.AluOpType
AX = mybir.AxisListType

B = 2
T = 2048
D = 1024
H = 16
HD = 64
NCH = D // 128
KC = T // 128
TC = T // 128
QH = 1024
MAGIC = float(3 * 2 ** 22)
EPS = 1e-5
KB = 1024
VPAIR = 129
VCHUNK = VPAIR * (H // 2)
VW = VCHUNK * KC


def pieces(lo, hi=QH, step=512):
    out, p = [], lo
    while p < hi:
        q = min((p // step + 1) * step, hi)
        out.append((p, q))
        p = q
    return out


class Ctr:
    def __init__(self):
        self.w = 0; self.x = 0; self.te = 0; self.ve = 0
        self.se = 0; self.gp = 0; self.out = 0; self.cc = 0
        self.nb = 0; self.xa = 0; self.xb = 0


class _MockIns:
    __slots__ = ("entry",)
    def __init__(self, entry):
        self.entry = entry
    def then_inc(self, sem, n):
        self.entry[3] = (sem, n)
        return self


class _MockEng:
    __slots__ = ("q", "dma")
    def __init__(self, q, dma=None):
        self.q = q
        self.dma = dma
    def __getattr__(self, name):
        q = self.q
        if name == "dma_start" and self.dma is not None:
            sem, cnt = self.dma
            def call_dma(*a, **k):
                if cnt[0] > 0:
                    q.append(["wait_ge", (sem, 16 * cnt[0]), {}, None])
                cnt[0] += 1
                entry = [name, a, k, None]
                q.append(entry)
                return _MockIns(entry)
            return call_dma
        if name == "dma_start_noser" and self.dma is not None:
            sem, cnt = self.dma
            def call_dma_ns(*a, **k):
                cnt[0] += 1
                entry = ["dma_start", a, k, None]
                q.append(entry)
                return _MockIns(entry)
            return call_dma_ns
        if name == "dma_start_alien":
            def call_dma_al(*a, **k):
                entry = ["dma_start", a, k, None]
                q.append(entry)
                return _MockIns(entry)
            return call_dma_al
        def call(*a, **k):
            entry = [name, a, k, None]
            q.append(entry)
            return _MockIns(entry)
        return call


class _Rec:
    def __init__(self):
        self.q = {n: [] for n in ("tensor", "vector", "scalar", "gpsimd", "sync")}
        self.dma_cfg = {}
    def set_dma(self, engine, sem):
        self.dma_cfg[engine] = (sem, [0])
    def tensor(self, f): f(_MockEng(self.q["tensor"]))
    def vector(self, f): f(_MockEng(self.q["vector"]))
    def scalar(self, f): f(_MockEng(self.q["scalar"]))
    def gpsimd(self, f): f(_MockEng(self.q["gpsimd"], self.dma_cfg.get("gpsimd")))
    def sync(self, f): f(_MockEng(self.q["sync"], self.dma_cfg.get("sync")))
    def replay(self, block):
        def mk(q):
            def run(eng):
                for name, a, k, ti in q:
                    ins = getattr(eng, name)(*a, **k)
                    if ti is not None:
                        ins.then_inc(*ti)
            return run
        block.tensor(mk(self.q["tensor"]))
        block.vector(mk(self.q["vector"]))
        block.scalar(mk(self.q["scalar"]))
        block.gpsimd(mk(self.q["gpsimd"]))
        block.sync(mk(self.q["sync"]))


def build_program(stop_after="full", dump=()):
    nc = bass.Bass("TRN2", target_bir_lowering=False, debug=False)

    d_xT = nc.dram_tensor("xT", [B, D, T], F32, kind="ExternalInput").ap()
    d_x = nc.dram_tensor("x", [B, T, D], F32, kind="ExternalInput").ap()
    d_wqT = nc.dram_tensor("wqT", [D, D], F32, kind="ExternalInput").ap()
    d_wkT = nc.dram_tensor("wkT", [D, D], F32, kind="ExternalInput").ap()
    d_wvT = nc.dram_tensor("wvT", [D, D], F32, kind="ExternalInput").ap()
    d_woT = nc.dram_tensor("woT", [D, D], F32, kind="ExternalInput").ap()
    d_w8T = nc.dram_tensor("w8T", [D, D], I8, kind="ExternalInput").ap()
    d_bq = nc.dram_tensor("bq_pm", [128, NCH], F32, kind="ExternalInput").ap()
    d_bk = nc.dram_tensor("bk_pm", [128, NCH], F32, kind="ExternalInput").ap()
    d_bv = nc.dram_tensor("bv_bc", [128, D], F32, kind="ExternalInput").ap()
    d_bo = nc.dram_tensor("bo_bc", [128, D], F32, kind="ExternalInput").ap()
    d_bf = nc.dram_tensor("bf_bc", [128, D], F32, kind="ExternalInput").ap()
    d_ws = nc.dram_tensor("ws_bc", [128, D], F32, kind="ExternalInput").ap()
    d_id = nc.dram_tensor("ident", [128, 128], BF16, kind="ExternalInput").ap()
    d_mk = nc.dram_tensor("mask", [128, 128], BF16, kind="ExternalInput").ap()
    d_out = nc.dram_tensor("out", [B, T, D], F32, kind="ExternalOutput").ap()
    d_cci = nc.dram_tensor("cc_in", [1, 8], F32).ap()
    d_cco = nc.dram_tensor("cc_out", [1, 8], F32).ap()

    BASE = 16512
    at_ = nc.alloc_sbuf_tensor_at
    cur = [BASE]

    def place(name, shape, dtype, align=64):
        sz = int(np.prod(shape[1:])) * mybir.dt.size(dtype)
        cur[0] = (cur[0] + align - 1) // align * align
        h = at_(name, list(shape), dtype, offset=cur[0])
        cur[0] += sz
        return h

    ident = place("ident", [128, 128], BF16)
    mask = place("mask", [128, 128], BF16)
    bq = place("bq", [128, NCH], F32)
    bk = place("bk", [128, NCH], F32)
    bv = place("bv", [128, D], BF16)
    bo = place("bo", [128, D], BF16)
    epsc = place("epsc", [128, 1], F32)
    magicc = place("magicc", [128, 1], F32)
    nmagicc = place("nmagicc", [128, 1], F32)
    xs_col = place("xs_col", [128, 1], F32)
    rxs_col = place("rxs_col", [128, 1], F32)
    stats = place("stats", [128, 256], F32)
    gmaxc = place("gmaxc", [128, 1], F32)
    gmax1 = place("gmax1", [1, 4], F32)
    xsc_tmp = place("xsc_tmp", [1, 4], F32)
    cc_stage = place("cc_stage", [1, 8], F32)
    rcpt = place("rcpt", [128, 40], F32)
    ones_row = place("ones_row", [1, 128], F32)
    pt0 = place("pt0", [128, QH], BF16)
    pt1 = place("pt1", [128, QH], BF16)
    PT23 = (cur[0] + 63) // 64 * 64
    pt2 = place("pt2", [128, QH], BF16)
    pt3 = place("pt3", [128, QH], BF16)
    tail_bc = at_("tail_bc", [128, QH], F32, offset=PT23)
    assert cur[0] <= BASE + 18 * KB, cur[0]

    UNION = BASE + 18 * KB
    bcastt = [at_("bcast0", [128, QH], F32, offset=UNION),
              at_("bcast1", [128, QH], F32, offset=UNION + 4 * KB)]
    ostg = [at_("ostg0", [128, QH], F32, offset=UNION + 8 * KB),
            at_("ostg1", [128, QH], F32, offset=UNION + 12 * KB),
            at_("ostg2", [128, QH], F32, offset=UNION + 16 * KB),
            at_("ostg3", [128, QH], F32, offset=UNION + 20 * KB)]
    wfull = at_("wfull", [128, NCH * D], BF16, offset=UNION)
    xstage = [at_("xstage0", [128, D], F32, offset=UNION),
              at_("xstage1", [128, D], F32, offset=UNION + 4 * KB)]
    wofull = at_("wofull", [128, NCH * D], BF16, offset=UNION + 8 * KB)
    qtmp = at_("qtmp", [128, D], F32, offset=UNION)
    d1t = at_("d1t", [128, D], F32, offset=UNION + 4 * KB)
    d2t = [at_("d2t0", [128, D], F32, offset=UNION + 8 * KB),
           at_("d2t1", [128, D], F32, offset=UNION + 12 * KB)]
    bff = at_("bff", [128, D], F32, offset=UNION + 16 * KB)
    wsc = at_("wsc", [128, D], F32, offset=UNION + 20 * KB)
    UNION_END = UNION + 24 * KB

    RXT = UNION_END
    xts = at_("xts", [128, NCH * T], BF16, offset=RXT)
    at1 = at_("at1", [128, NCH * T], BF16, offset=RXT)
    w8full = at_("w8full", [128, NCH * D], BF16, offset=RXT)
    xqt = [at_("xqt0", [128, D], BF16, offset=RXT + 16 * KB),
           at_("xqt1", [128, D], BF16, offset=RXT + 18 * KB),
           at_("xqt2", [128, D], BF16, offset=RXT + 20 * KB),
           at_("xqt3", [128, D], BF16, offset=RXT + 22 * KB)]
    ost = [at_("ost0", [128, D], F32, offset=RXT + 24 * KB),
           at_("ost1", [128, D], F32, offset=RXT + 28 * KB)]

    RQ = RXT + 32 * KB
    qt = at_("qt", [128, NCH * T], BF16, offset=RQ)
    RK = RQ + 32 * KB
    kt = at_("kt", [128, NCH * T], BF16, offset=RK)
    ln1_0 = at_("ln1_0", [128, TC * D], F32, offset=RQ)
    RV = RK + 32 * KB
    vaug = at_("vaug", [128, VW], BF16, offset=RV)
    tail_s = [at_("tail_s0", [128, QH], F32, offset=RV),
              at_("tail_s1", [128, QH], F32, offset=RV + 4 * KB)]
    RA0 = RV + 33 * KB
    at0 = at_("at0", [128, NCH * T], BF16, offset=RA0)
    ln1_1 = at_("ln1_1", [128, TC * D], F32, offset=RV)
    assert RA0 + 32 * KB <= 229344, RA0 + 32 * KB

    ATB = [at0, at1]
    LN1 = [ln1_0, ln1_1]
    DUMPABLE = {"qt": (qt, BF16, NCH * T), "kt": (kt, BF16, NCH * T),
                "vaug": (vaug, BF16, VW), "at0": (at0, BF16, NCH * T),
                "at1": (at1, BF16, NCH * T), "ln1_0": (ln1_0, F32, TC * D),
                "ln1_1": (ln1_1, F32, TC * D), "stats": (stats, F32, 256),
                "xs_col": (xs_col, F32, 1)}
    d_dbg = {n: nc.dram_tensor("dbg_" + n, [128, DUMPABLE[n][2]],
                               DUMPABLE[n][1], kind="ExternalOutput").ap()
             for n in dump}

    c = Ctr()

    with (
        nc.psum_tensor("psA", [128, 2048], F32) as psA,
        nc.psum_tensor("psB", [128, 2048], F32) as psB,
        nc.semaphore("sw") as sw,
        nc.semaphore("sx") as sx,
        nc.semaphore("ste") as ste,
        nc.semaphore("sve") as sve,
        nc.semaphore("sse") as sse,
        nc.semaphore("sgp") as sgp,
        nc.semaphore("sout") as sout,
        nc.semaphore("scc") as scc,
        nc.semaphore("snb") as snb,
        nc.semaphore("sxa") as sxa,
        nc.semaphore("sxb") as sxb,
        nc.Block() as block,
    ):
        rec = _Rec()
        rec.set_dma('sync', sw)
        rec.set_dma('gpsimd', sx)
        def _(s):
            s.add_instruction(mybir.InstLoadActFuncSet(
                name=nc.get_next_instruction_name(),
                act_func_set_id=_ACT_SET_IDX, ins=[], outs=[]))
        rec.scalar(_)

        def _(s):
            for dst, src in [(ident, d_id), (mask, d_mk), (bq, d_bq), (bk, d_bk)]:
                s.dma_start(out=dst[:], in_=src).then_inc(sw, 16)
                c.w += 16
        rec.sync(_)
        consts_w = c.w

        def _(g):
            g.dma_start(out=bv[:], in_=d_bv).then_inc(sx, 16); c.x += 16
            g.dma_start(out=bo[:], in_=d_bo).then_inc(sx, 16); c.x += 16
        rec.gpsimd(_)
        bias_x = c.x

        def _(v):
            v.memset(epsc[:], EPS)
            v.memset(magicc[:], MAGIC)
            v.memset(nmagicc[:], -MAGIC)
            v.memset(cc_stage[:], 0.0)
            v.memset(ones_row[:], 1.0)
            v.memset(bass.AP(vaug, 64, [[VW, 128], [VCHUNK, KC],
                                        [VPAIR, H // 2]]), 1.0).then_inc(sve, 1)
            c.ve += 1
        rec.vector(_)
        consts_ve = c.ve
        def _(v):
            v.wait_ge(sx, bias_x)
            v.reduce_sum(rcpt[:, 32:33], bo[:], axis=AX.X).then_inc(sve, 1)
            c.ve += 1
        rec.vector(_)

        def load_xT(b, wait_te):
            def _(g):
                g.wait_ge(ste, wait_te)
                g.wait_ge(sx, c.x)
                for ch in range(NCH):
                    half = sxa if ch < 4 else sxb
                    g.dma_start_alien(
                        out=xts[:, T * ch:T * ch + T],
                        in_=d_xT[b, 128 * ch:128 * ch + 128, :],
                    ).then_inc(half, 16)
                    if ch < 4:
                        c.xa += 16
                    else:
                        c.xb += 16
            rec.gpsimd(_)
            return c.xa, c.xb

        def load_w(src, wait_te, dst=None, wait_ve=None, wait_gp=None):
            dst = wfull if dst is None else dst
            def _(g):
                g.wait_ge(ste, wait_te)
                if wait_ve is not None:
                    g.wait_ge(sve, wait_ve)
                if wait_gp is not None:
                    g.wait_ge(sgp, wait_gp)
                g.dma_start(
                    out=dst.ap().rearrange("p (ch n) -> p ch n", ch=NCH),
                    in_=src.rearrange("(ch p) n -> p ch n", p=128),
                ).then_inc(sx, 16)
                c.x += 16
            rec.gpsimd(_)
            return c.x

        def load_w_chunked(src, wait_te, wait_ve=None, wait_gp=None):
            counts = []
            def _(g):
                for ch in range(NCH):
                    if ch == 0:
                        g.wait_ge(ste, wait_te)
                        if wait_ve is not None:
                            g.wait_ge(sve, wait_ve)
                        if wait_gp is not None:
                            g.wait_ge(sgp, wait_gp)
                    g.dma_start(
                        out=wfull[:, D * ch:D * ch + D],
                        in_=src[128 * ch:128 * ch + 128, :],
                    ).then_inc(sx, 16)
                    c.x += 16
                    counts.append(c.x)
            rec.gpsimd(_)
            return counts

        def proj_qk(bias_col, dst, x_ready, w_counts, prev_ve):
            xa_t, xb_t = x_ready
            evac_after = {}
            entry_se = c.se
            for nch in range(NCH):
                arena = psA if nch % 2 == 0 else psB
                def _(t, nch=nch, arena=arena):
                    if nch < 2:
                        t.wait_ge(sse, entry_se)
                    t.wait_ge(sve, evac_after.get(nch - 2, prev_ve))
                    ins = None
                    for kc in range(NCH):
                        if nch == 0:
                            t.wait_ge(sxa if kc < 4 else sxb,
                                      xa_t if kc < 4 else xb_t)
                            t.wait_ge(sx, w_counts[kc])
                        for tt in range(4):
                            ins = t.matmul(
                                arena[:, 512 * tt:512 * tt + 512],
                                wfull[:, D * kc + 128 * nch:
                                      D * kc + 128 * nch + 128],
                                xts[:, T * kc + 512 * tt:T * kc + 512 * tt + 512],
                                start=(kc == 0), stop=(kc == NCH - 1))
                    ins.then_inc(ste, 1)
                    c.te += 1
                rec.tensor(_)
                te_done = c.te
                def _(v, nch=nch, arena=arena, te_done=te_done):
                    v.wait_ge(ste, te_done)
                    v.wait_ge(sw, consts_w)
                    ins = None
                    for tt in range(4):
                        ins = v.tensor_scalar(
                            dst[:, T * nch + 512 * tt:T * nch + 512 * tt + 512],
                            arena[:, 512 * tt:512 * tt + 512],
                            bias_col[:, nch:nch + 1], None, ALU.add)
                    ins.then_inc(sve, 1)
                    c.ve += 1
                rec.vector(_)
                evac_after[nch] = c.ve
            return c.ve

        def proj_v(x_ready, w_counts, prev_ve, tail_gp=0):
            evac_after = {}
            entry_se = c.se
            if tail_gp:
                def _(v):
                    v.wait_ge(sgp, tail_gp)
                    v.memset(bass.AP(vaug, 64, [[VW, 128], [VCHUNK, KC],
                                                [VPAIR, H // 2]]), 1.0) \
                     .then_inc(sve, 1)
                    c.ve += 1
                rec.vector(_)
            xa_t, xb_t = x_ready
            for tc in range(TC):
                arena = psA if tc % 2 == 0 else psB
                def _(t, tc=tc, arena=arena):
                    if tc < 2:
                        t.wait_ge(sse, entry_se)
                    t.wait_ge(sve, evac_after.get(tc - 2, prev_ve))
                    ins = None
                    for kc in range(NCH):
                        if tc == 0:
                            t.wait_ge(sxa if kc < 4 else sxb,
                                      xa_t if kc < 4 else xb_t)
                            t.wait_ge(sx, w_counts[kc])
                        for nt in range(2):
                            ins = t.matmul(
                                arena[:, 512 * nt:512 * nt + 512],
                                xts[:, T * kc + 128 * tc:T * kc + 128 * tc + 128],
                                wfull[:, D * kc + 512 * nt:D * kc + 512 * nt + 512],
                                start=(kc == 0), stop=(kc == NCH - 1))
                    ins.then_inc(ste, 1)
                    c.te += 1
                rec.tensor(_)
                te_done = c.te
                def _(v, tc=tc, arena=arena, te_done=te_done):
                    v.wait_ge(ste, te_done)
                    v.wait_ge(sx, bias_x)
                    ins = None
                    for nt in range(2):
                        base = VCHUNK * tc + VPAIR * 4 * nt
                        dst = bass.AP(vaug, base,
                                      [[VW, 128], [VPAIR, 4], [65, 2], [1, 64]])
                        src = bass.AP(arena.ap().tensor,
                                      arena.ap().offset + 512 * nt,
                                      [[2048, 128], [128, 4], [64, 2], [1, 64]])
                        b_ap = bass.AP(bv, 512 * nt,
                                       [[D, 128], [128, 4], [64, 2], [1, 64]])
                        ins = v.tensor_tensor(dst, src, b_ap, ALU.add)
                    ins.then_inc(sve, 1)
                    c.ve += 1
                rec.vector(_)
                evac_after[tc] = c.ve
            return c.ve

        def attention(at_dst, ready_ve):
            entry_se = c.se
            entry_gp = c.gp
            seg_i = [0]
            half_i = [0]
            exp_after = {}
            pv_after = {}
            norm_after = {}
            stage_after = {}

            class Stream:
                def __init__(self, h, hf):
                    self.h, self.hf = h, hf
                    self.q0h = QH * hf
                    self.js = [j for j in range(KC) if 128 * j < self.q0h + QH]
                    self.hidx = half_i[0]
                    half_i[0] += 1
                    self.osl = self.hidx % 2
                    self.obuf = psB[:, QH * self.osl:QH * self.osl + QH]
                    self.segs = []
                    self.ch = h // 2
                    self.po = 64 * (h % 2)
                    self.pblk = VPAIR * (h // 2 % (H // 2))

            def emit_s(st, j):
                qs = max(st.q0h, 128 * j)
                lo = qs - st.q0h
                si = seg_i[0]
                sp = si % 2
                sps = psA[:, QH * sp:QH * sp + QH]
                ptb = (pt0, pt1, pt2, pt3)[si % 4]
                def _(t):
                    t.wait_ge(sve, ready_ve)
                    if si >= 2:
                        t.wait_ge(sse, exp_after[si - 2])
                    else:
                        t.wait_ge(sse, entry_se)
                    ins = None
                    for (plo, phi) in pieces(lo):
                        ins = t.matmul(
                            sps[:, plo:phi],
                            kt[st.po:st.po + 64,
                               T * st.ch + 128 * j:T * st.ch + 128 * j + 128],
                            qt[st.po:st.po + 64,
                               T * st.ch + st.q0h + plo:T * st.ch + st.q0h + phi],
                            start=True, stop=True)
                    ins.then_inc(ste, 1)
                    c.te += 1
                rec.tensor(_)
                s_done = c.te
                def _(sc):
                    sc.wait_ge(ste, s_done)
                    if si >= 4:
                        sc.wait_ge(ste, pv_after[si - 4])
                    sc.activation(ptb[:, lo:QH], sps[:, lo:QH], AF.Exp,
                                  scale=0.125).then_inc(sse, 1)
                    c.se += 1
                rec.scalar(_)
                exp_after[si] = c.se
                e_done = c.se
                m_done = None
                if qs == 128 * j:
                    def _(v):
                        v.wait_ge(sse, e_done)
                        v.tensor_tensor(ptb[:, lo:lo + 128],
                                        ptb[:, lo:lo + 128], mask[:],
                                        ALU.mult).then_inc(sve, 1)
                        c.ve += 1
                    rec.vector(_)
                    m_done = c.ve
                st.segs.append((j, lo, ptb, e_done, m_done, si))
                seg_i[0] += 1

            def emit_pv(st, k):
                (j, lo, ptb, e_done, m_done, si) = st.segs[k]
                first = (k == 0)
                last_j = st.js[-1]
                def _(t):
                    if first:
                        if st.hidx >= 2:
                            t.wait_ge(sve, stage_after[st.hidx - 2])
                        else:
                            t.wait_ge(sve, ready_ve)
                    if m_done is not None:
                        t.wait_ge(sve, m_done)
                    else:
                        t.wait_ge(sse, e_done)
                    if st.h % 2 == 0:
                        lhs = vaug[:, VCHUNK * j + st.pblk:
                                   VCHUNK * j + st.pblk + 65]
                        nrow = 65
                    else:
                        lhs = vaug[:, VCHUNK * j + st.pblk + 1:
                                   VCHUNK * j + st.pblk + 129]
                        nrow = 128
                    ins = None
                    for (plo, phi) in pieces(lo):
                        bank_last = min((st.q0h + phi - 1) // 128, last_j)
                        ins = t.matmul(st.obuf[0:nrow, plo:phi], lhs,
                                       ptb[:, plo:phi],
                                       start=(j == 0), stop=(j == bank_last))
                    ins.then_inc(ste, 1)
                    c.te += 1
                rec.tensor(_)
                pv_after[si] = c.te

            def emit_norm_stage(st):
                pv_done = pv_after[st.segs[-1][5]]
                stg = tail_s[st.hidx - 2 * H + 2] if st.hidx >= 2 * H - 2 \
                    else ostg[st.hidx % 4]
                even = (st.h % 2 == 0)
                def _(v):
                    v.wait_ge(ste, pv_done)
                    v.wait_ge(sgp, norm_after.get(st.hidx - 4, entry_gp))
                    if even:
                        v.tensor_copy(stg[0:65, :], st.obuf[0:65, :]) \
                         .then_inc(sve, 1)
                    else:
                        v.tensor_copy(stg[32:64, :], st.obuf[32:64, :])
                        v.tensor_copy(stg[64:128, :], st.obuf[64:128, :]) \
                         .then_inc(sve, 1)
                    c.ve += 1
                    if not even:
                        v.wait_ge(sve, c.ve)
                        v.tensor_scalar(stg[32:64, :], stg[32:64, :], 1e-20,
                                        1e18, ALU.max, ALU.min).then_inc(sve, 1)
                        c.ve += 1
                rec.vector(_)
                stage_after[st.hidx] = c.ve
                return c.ve

            def emit_norm_finish(st, cp_done):
                h, ch, q0h = st.h, st.ch, st.q0h
                tail = st.hidx >= 2 * H - 2
                stg = tail_s[st.hidx - 2 * H + 2] if tail \
                    else ostg[st.hidx % 4]
                bcast = tail_bc if tail else bcastt[(st.hidx % 4) // 2]
                even = (h % 2 == 0)
                dr = stg[64:65, :] if even else stg[32:64, :]
                rr = stg[96:97, :] if even else stg[0:32, :]
                def _(sc):
                    sc.wait_ge(sve, cp_done)
                    sc.activation(rr, dr, AF.Ln).then_inc(sse, 1)
                    c.se += 1
                    sc.wait_ge(sse, c.se)
                    sc.activation(rr, rr, AF.Exp, scale=-1.0).then_inc(sse, 1)
                    c.se += 1
                rec.scalar(_)
                r_done = c.se
                r0, r1 = (0, 64) if even else (64, 128)
                if tail:
                    tail_te = c.te
                    def _(g):
                        g.wait_ge(sse, r_done)
                        g.wait_ge(ste, tail_te)
                        rsrc = stg[96:97, :] if even else stg[31:32, :]
                        g.dma_start_alien(
                            out=bcast[r0:r1, :],
                            in_=rsrc.unsqueeze(1)
                                .broadcast_to((1, r1 - r0, QH))) \
                         .then_inc(snb, 16)
                        c.nb += 16
                    rec.gpsimd(_)
                    b_done = c.nb
                else:
                    def _(sc):
                        sc.wait_ge(sse, r_done)
                        rsrc = stg[96:97, :] if even else stg[31:32, :]
                        sc.dma_start(
                            out=bcast[r0:r1, :],
                            in_=rsrc.unsqueeze(1)
                                .broadcast_to((1, r1 - r0, QH))) \
                         .then_inc(sw, 16)
                        c.w += 16
                    rec.sync(_)
                    b_done = c.w
                def _(g):
                    g.wait_ge(snb if tail else sw, b_done)
                    g.tensor_tensor(
                        at_dst[r0:r1, T * ch + q0h:T * ch + q0h + QH],
                        stg[0:64, :] if even else stg[64:128, :],
                        bcast[r0:r1, :], ALU.mult).then_inc(sgp, 1)
                    c.gp += 1
                rec.gpsimd(_)
                norm_after[st.hidx] = c.gp

            pending = []
            for hp in range(0, H, 2):
                for hf in range(2):
                    sa = Stream(hp, hf)
                    sb = Stream(hp + 1, hf)
                    n = len(sa.js)
                    for k in range(n):
                        emit_s(sa, sa.js[k])
                        emit_s(sb, sb.js[k])
                        if k == 2:
                            for st, cp in pending:
                                emit_norm_finish(st, cp)
                            pending = []
                        if k >= 1:
                            emit_pv(sa, k - 1)
                            emit_pv(sb, k - 1)
                    emit_pv(sa, n - 1)
                    emit_pv(sb, n - 1)
                    pending.append((sa, emit_norm_stage(sa)))
                    pending.append((sb, emit_norm_stage(sb)))
            thunks = [(lambda st=st, cp=cp: emit_norm_finish(st, cp))
                      for st, cp in pending]
            return c.ve, c.gp, thunks

        def wo_ln1(bidx, at_src, attn_ve, attn_se, wo_ready, first_xstage_te,
                   attn_gp=0, mid_hook=None, gp_bias=False, extra_ve=None):
            ln1 = LN1[bidx]
            ev_after = {}
            entry_ve = c.ve
            extra_ve = list(extra_ve) if extra_ve else []
            for tc in range(TC):
                if mid_hook is not None and tc in (8, 12):
                    mid_hook(tc)
                arena = psA if tc % 2 == 0 else psB
                xst = xstage[tc % 2]
                def _(s):
                    if tc < 2:
                        s.wait_ge(ste, first_xstage_te)
                        s.wait_ge(sve, entry_ve)
                        s.wait_ge(sgp, attn_gp)
                    else:
                        s.wait_ge(sve, ev_after[tc - 2])
                    s.dma_start(out=xst[:],
                                in_=d_x[bidx, 128 * tc:128 * tc + 128, :]) \
                        .then_inc(sw, 16)
                    c.w += 16
                rec.sync(_)
                x_done = c.w
                def _(t):
                    t.wait_ge(sve, attn_ve)
                    t.wait_ge(sse, attn_se)
                    t.wait_ge(sgp, attn_gp)
                    t.wait_ge(sx, wo_ready)
                    t.wait_ge(sve, ev_after.get(tc - 2, entry_ve))
                    ins = None
                    for dch in range(NCH):
                        for nt in range(2):
                            ins = t.matmul(
                                arena[:, 512 * nt:512 * nt + 512],
                                at_src[:, T * dch + 128 * tc:
                                       T * dch + 128 * tc + 128],
                                wofull[:, D * dch + 512 * nt:
                                       D * dch + 512 * nt + 512],
                                start=(dch == 0), stop=(dch == NCH - 1))
                    ins.then_inc(ste, 1)
                    c.te += 1
                rec.tensor(_)
                mm_done = c.te
                si = bidx * TC + tc
                lt = ln1[:, D * tc:D * tc + D]
                gp_done = None
                if gp_bias:
                    def _(v):
                        v.wait_ge(ste, mm_done)
                        v.wait_ge(sw, x_done)
                        v.tensor_tensor(lt, arena[:, 0:D], xst[:], ALU.add) \
                         .then_inc(sve, 1)
                        c.ve += 1
                        v.wait_ge(sve, c.ve)
                        v.reduce_sum(stats[:, si:si + 1], lt, axis=AX.X) \
                         .then_inc(sve, 1)
                        c.ve += 1
                        if tc >= 2 and extra_ve:
                            extra_ve.pop(0)(v)
                    rec.vector(_)
                    ve_mid = c.ve
                    def _(g):
                        g.wait_ge(sve, ve_mid)
                        g.wait_ge(sx, bias_x)
                        g.tensor_tensor(lt, lt, bo[:], ALU.add).then_inc(sgp, 1)
                        c.gp += 1
                    rec.gpsimd(_)
                    gp_done = c.gp
                elif True:
                    def _(v):
                        v.wait_ge(ste, mm_done)
                        v.wait_ge(sw, x_done)
                        v.tensor_tensor(lt, arena[:, 0:D], bo[:], ALU.add) \
                         .then_inc(sve, 1)
                        c.ve += 1
                        v.wait_ge(sve, c.ve)
                        v.tensor_tensor(lt, lt, xst[:], ALU.add).then_inc(sve, 1)
                        c.ve += 1
                        v.wait_ge(sve, c.ve)
                        v.reduce_sum(stats[:, si:si + 1], lt, axis=AX.X) \
                         .then_inc(sve, 1)
                        c.ve += 1
                    rec.vector(_)
                r1_ve = c.ve
                def _(s):
                    if gp_done is not None:
                        s.wait_ge(sgp, gp_done)
                    else:
                        s.wait_ge(sve, r1_ve - 1)
                    s.wait_ge(sse, c.se)
                    s.activation(pt0[:], lt, AF.Square,
                                 accum_out=stats[:, 64 + si:64 + si + 1]) \
                     .then_inc(sse, 1)
                    c.se += 1
                rec.scalar(_)
                ev_after[tc] = c.ve
            return c.ve, c.se

        def ln1_var(bidx, prev_ve, prev_se, t0=0, t1=TC, bias_col=None):
            s0 = bidx * TC + t0
            n = t1 - t0
            sums = stats[:, s0:s0 + n]
            sumsq = stats[:, 64 + s0:64 + s0 + n]
            tcol = stats[:, 128 + s0:128 + s0 + n]
            ucol = stats[:, 160 + s0:160 + s0 + n]
            mean = stats[:, 192 + s0:192 + s0 + n]
            sigc = stats[:, 96 + s0:96 + s0 + n]
            def _(v):
                v.wait_ge(sve, prev_ve)
                v.wait_ge(sse, prev_se)
                if bias_col is not None:
                    v.tensor_scalar(sums, sums, bias_col, None, ALU.add) \
                     .then_inc(sve, 1)
                    c.ve += 1
                    v.wait_ge(sve, c.ve)
                v.scalar_tensor_tensor(tcol, sums, 1.0 / D, sums,
                                       ALU.mult, ALU.mult).then_inc(sve, 1)
                c.ve += 1
                v.tensor_scalar(mean, sums, 1.0 / D, None, ALU.mult) \
                 .then_inc(sve, 1)
                c.ve += 1
                v.wait_ge(sve, c.ve)
                v.tensor_tensor(ucol, sumsq, tcol, ALU.subtract).then_inc(sve, 1)
                c.ve += 1
                v.wait_ge(sve, c.ve)
                v.tensor_copy(rcpt[:, 38:39], ucol[:, 0:1]).then_inc(sve, 1)
                c.ve += 1
            rec.vector(_)
            u_ve = c.ve
            rstd = rcpt[:, 16 * bidx + t0:16 * bidx + t1]
            def _(s):
                s.wait_ge(sve, u_ve)
                s.activation(sigc, ucol, AF.Ln, bias=epsc[:], scale=1.0 / D) \
                 .then_inc(sse, 1)
                c.se += 1
                s.wait_ge(sse, c.se)
                s.activation(rstd, sigc, AF.Exp, scale=-0.5).then_inc(sse, 1)
                c.se += 1
                s.wait_ge(sse, c.se)
                s.copy(rcpt[:, 39:40], rstd[:, 0:1]).then_inc(sse, 1)
                c.se += 1
            rec.scalar(_)
            sig_se = c.se
            nmrs = stats[:, 128 + s0:128 + s0 + n]
            def _(v):
                v.wait_ge(sse, sig_se)
                v.scalar_tensor_tensor(nmrs, mean, -1.0, rstd,
                                       ALU.mult, ALU.mult).then_inc(sve, 1)
                c.ve += 1
                v.wait_ge(sve, c.ve)
                v.tensor_copy(rcpt[:, 38:39], nmrs[:, 0:1]).then_inc(sve, 1)
                c.ve += 1
            rec.vector(_)
            return c.ve

        def ln1_apply(bidx, prev_ve, t0=0, t1=TC, chunk=None):
            ln1 = LN1[bidx]
            rstd = rcpt[:, 16 * bidx:16 * bidx + TC]
            nmrs = stats[:, 128 + bidx * TC:128 + bidx * TC + TC]
            chunk = (t1 - t0) if chunk is None else chunk
            se_by_tile = {}
            def _(s):
                s.wait_ge(sve, prev_ve)
                for c0 in range(t0, t1, chunk):
                    ins = None
                    for tc in range(c0, min(c0 + chunk, t1)):
                        lt = ln1[:, D * tc:D * tc + D]
                        ins = s.activation(lt, lt, AF.Identity,
                                           bias=nmrs[:, tc:tc + 1],
                                           scale=rstd[:, tc:tc + 1])
                    ins.then_inc(sse, 1)
                    c.se += 1
                    for tc in range(c0, min(c0 + chunk, t1)):
                        se_by_tile[tc] = c.se
            rec.scalar(_)
            return se_by_tile

        def ln1_max(bidx, apply_se, t0=0, t1=TC):
            ln1 = LN1[bidx]
            s0 = bidx * TC
            def _(v):
                v.wait_ge(sse, apply_se)
                ins = None
                for tc in range(t0, t1):
                    lt = ln1[:, D * tc:D * tc + D]
                    ins = v.reduce_max(stats[:, 224 + s0 + tc:224 + s0 + tc + 1],
                                       lt, axis=AX.X, apply_absolute_value=True)
                ins.then_inc(sve, 1)
                c.ve += 1
            rec.vector(_)
            return c.ve

        def global_scale(prev_ve):
            def _(v):
                v.wait_ge(sve, prev_ve)
                v.reduce_max(gmaxc[:], stats[:, 224:224 + 2 * TC], axis=AX.X) \
                 .then_inc(sve, 1)
                c.ve += 1
                v.wait_ge(sve, c.ve)
                v.tensor_copy(rcpt[:, 38:39], gmaxc[:]).then_inc(sve, 1)
                c.ve += 1
            rec.vector(_)
            g_ve = c.ve
            def _(g):
                g.wait_ge(sve, g_ve)
                g.reduce_max(gmax1[0:1, 0:1], gmaxc[:], axis=AX.XYZWC) \
                 .then_inc(sgp, 1)
                c.gp += 1
                g.wait_ge(sgp, c.gp)
                g.tensor_copy(gmax1[0:1, 3:4], gmax1[0:1, 0:1]).then_inc(sgp, 1)
                c.gp += 1
            rec.gpsimd(_)
            g_gp = c.gp
            def _(v):
                v.wait_ge(sgp, g_gp)
                v.tensor_scalar(cc_stage[0:1, 0:1], gmax1[0:1, 0:1],
                                1.0 / 127.0, EPS, ALU.mult, ALU.max) \
                 .then_inc(sve, 1)
                c.ve += 1
                v.wait_ge(sve, c.ve)
                v.tensor_copy(xsc_tmp[0:1, 1:2], cc_stage[0:1, 0:1]) \
                 .then_inc(sve, 1)
                c.ve += 1
            rec.vector(_)
            st_ve = c.ve
            def _(g):
                g.wait_ge(sve, st_ve)
                g.dma_start(out=d_cci, in_=cc_stage[:]).then_inc(sx, 16)
                c.x += 16
                g.wait_ge(sx, c.x)
                g.collective_compute(
                    "AllReduce", ALU.max, replica_groups=[list(range(8))],
                    ins=[d_cci.opt()], outs=[d_cco.opt()]).then_inc(scc, 1)
                c.cc += 1
                g.wait_ge(scc, c.cc)
                g.dma_start(out=xsc_tmp[0:1, 0:1], in_=d_cco[0:1, 0:1]) \
                 .then_inc(sx, 16)
                c.x += 16
            rec.gpsimd(_)
            in_x = c.x
            def _(v):
                v.wait_ge(sx, in_x)
                v.reciprocal(gmax1[0:1, 2:3], xsc_tmp[0:1, 0:1]).then_inc(sve, 1)
                c.ve += 1
                v.wait_ge(sve, c.ve)
                v.tensor_copy(xsc_tmp[0:1, 1:2], gmax1[0:1, 2:3]) \
                 .then_inc(sve, 1)
                c.ve += 1
                v.wait_ge(sve, c.ve)
                v.tensor_copy(gmax1[0:1, 3:4], xsc_tmp[0:1, 1:2]) \
                 .then_inc(sve, 1)
                c.ve += 1
            rec.vector(_)
            rx_ve = c.ve
            def _(t):
                t.wait_ge(sve, rx_ve)
                t.matmul(psA[:, 0:2], ones_row[:], xsc_tmp[0:1, 0:2],
                         start=True, stop=True).then_inc(ste, 1)
                c.te += 1
            rec.tensor(_)
            bc_te = c.te
            def _(sc):
                sc.wait_ge(ste, bc_te)
                sc.activation(xs_col[:], psA[:, 0:1], AF.Identity)
                sc.activation(rxs_col[:], psA[:, 1:2], AF.Identity) \
                 .then_inc(sse, 1)
                c.se += 1
            rec.scalar(_)
            return c.se

        def ffn(xs_ve, bff_x, w8_ready, wsbf_w):
            N = B * TC
            copy_after = {}
            mm_after = {}
            deq_after = {}
            dv_after = {}
            u2_after = {}
            sq_after = {}
            restve_after = {}
            out_after = {}

            def head(i):
                bidx, tc = i // TC, i % TC
                par4 = i % 4
                lt = LN1[bidx][:, D * tc:D * tc + D]
                ptr = psB[:, 512 * par4:512 * par4 + 512].bitcast(BF16)
                pfa = psA[:, QH * (i % 2):QH * (i % 2) + QH]
                xq = xqt[par4]
                def _(sc):
                    sc.wait_ge(sse, xs_ve)
                    sc.wait_ge(sx, bff_x)
                    sc.wait_ge(sw, wsbf_w)
                    sc.wait_ge(sse, c.se)
                    if i >= 4:
                        sc.wait_ge(ste, mm_after[i - 4])
                    sc.activation(qtmp[:], lt, AF.Identity, bias=magicc[:],
                                  scale=rxs_col[:]).then_inc(sse, 1)
                    c.se += 1
                    sc.wait_ge(sse, c.se)
                    sc.activation(xq[:], qtmp[:], AF.Identity,
                                  bias=nmagicc[:], scale=1.0).then_inc(sse, 1)
                    c.se += 1
                rec.scalar(_)
                q2_se = c.se
                def _(t):
                    t.wait_ge(sse, q2_se)
                    if i >= 4:
                        t.wait_ge(sve, copy_after[i - 4])
                    ins = None
                    for cc8 in range(NCH):
                        ins = t.transpose(ptr[:, 128 * cc8:128 * cc8 + 128],
                                          xq[:, 128 * cc8:128 * cc8 + 128],
                                          ident[:])
                    ins.then_inc(ste, 1)
                    c.te += 1
                rec.tensor(_)
                tr_done = c.te
                def _(v):
                    v.wait_ge(ste, tr_done)
                    v.tensor_copy(xq[:], ptr[:]).then_inc(sve, 1)
                    c.ve += 1
                rec.vector(_)
                copy_after[i] = c.ve
                def _(t):
                    t.wait_ge(sve, copy_after[i])
                    t.wait_ge(sx, w8_ready)
                    if i >= 2:
                        t.wait_ge(sve, deq_after[i - 2])
                    ins = None
                    for kc8 in range(NCH):
                        for nt in range(2):
                            ins = t.matmul(
                                pfa[:, 512 * nt:512 * nt + 512],
                                xq[:, 128 * kc8:128 * kc8 + 128],
                                w8full[:, D * kc8 + 512 * nt:
                                       D * kc8 + 512 * nt + 512],
                                start=(kc8 == 0), stop=(kc8 == NCH - 1))
                    ins.then_inc(ste, 1)
                    c.te += 1
                rec.tensor(_)
                mm_after[i] = c.te

            def deqblk(i):
                bidx, tc = i // TC, i % TC
                par = i % 2
                lt = LN1[bidx][:, D * tc:D * tc + D]
                pfa = psA[:, QH * par:QH * par + QH]
                si = 240 + 8 * par
                def _(v):
                    v.wait_ge(ste, mm_after[i])
                    if i >= 2:
                        v.wait_ge(sve, restve_after[i - 2])
                        v.wait_ge(sse, sq_after[i - 2])
                    v.scalar_tensor_tensor(d1t[:], pfa[:, 0:D], xs_col[:],
                                           wsc[:], ALU.mult, ALU.mult) \
                     .then_inc(sve, 1)
                    c.ve += 1
                    deq_after[i] = c.ve
                    v.wait_ge(sve, c.ve)
                    v.tensor_tensor(d2t[par][:], d1t[:], bff[:], ALU.add) \
                     .then_inc(sve, 1)
                    c.ve += 1
                    v.wait_ge(sve, c.ve)
                    v.tensor_tensor(d2t[par][:], d2t[par][:], lt, ALU.add) \
                     .then_inc(sve, 1)
                    c.ve += 1
                    v.wait_ge(sve, c.ve)
                    v.reduce_sum(stats[:, si:si + 1], d2t[par][:], axis=AX.X) \
                     .then_inc(sve, 1)
                    c.ve += 1
                rec.vector(_)
                dv_after[i] = c.ve

            def rest_a(i):
                bidx, tc = i // TC, i % TC
                par = i % 2
                si = 240 + 8 * par
                def _(sc):
                    sc.wait_ge(sve, dv_after[i] - 1)
                    sc.wait_ge(sse, c.se)
                    sc.activation(pt0[:], d2t[par][:], AF.Square,
                                  accum_out=stats[:, si + 1:si + 2]) \
                     .then_inc(sse, 1)
                    c.se += 1
                rec.scalar(_)
                sq_after[i] = c.se
                def _(v):
                    v.wait_ge(sve, dv_after[i])
                    v.wait_ge(sse, sq_after[i])
                    v.scalar_tensor_tensor(stats[:, si + 2:si + 3],
                                           stats[:, si:si + 1], 1.0 / D,
                                           stats[:, si:si + 1],
                                           ALU.mult, ALU.mult).then_inc(sve, 1)
                    c.ve += 1
                    v.tensor_scalar(stats[:, si + 3:si + 4],
                                    stats[:, si:si + 1], 1.0 / D, None,
                                    ALU.mult).then_inc(sve, 1)
                    c.ve += 1
                    v.wait_ge(sve, c.ve)
                    v.tensor_tensor(stats[:, si + 4:si + 5],
                                    stats[:, si + 1:si + 2],
                                    stats[:, si + 2:si + 3], ALU.subtract) \
                     .then_inc(sve, 1)
                    c.ve += 1
                    v.wait_ge(sve, c.ve)
                    v.tensor_copy(rcpt[:, 38:39],
                                  stats[:, si + 4:si + 5]).then_inc(sve, 1)
                    c.ve += 1
                rec.vector(_)
                u2_after[i] = c.ve

            def rest_b(i):
                bidx, tc = i // TC, i % TC
                par = i % 2
                si = 240 + 8 * par
                def _(sc):
                    sc.wait_ge(sve, u2_after[i])
                    sc.activation(stats[:, si + 5:si + 6],
                                  stats[:, si + 4:si + 5], AF.Ln,
                                  bias=epsc[:], scale=1.0 / D).then_inc(sse, 1)
                    c.se += 1
                    sc.wait_ge(sse, c.se)
                    sc.activation(rcpt[:, 34 + par:35 + par],
                                  stats[:, si + 5:si + 6], AF.Exp,
                                  scale=-0.5).then_inc(sse, 1)
                    c.se += 1
                    sc.wait_ge(sse, c.se)
                    sc.copy(rcpt[:, 36:37],
                            rcpt[:, 34 + par:35 + par]).then_inc(sse, 1)
                    c.se += 1
                rec.scalar(_)
                sig2_se = c.se
                def _(v):
                    v.wait_ge(sse, sig2_se)
                    if i >= 2:
                        v.wait_ge(sw, out_after[i - 2])
                    v.tensor_scalar(ost[par][:], d2t[par][:],
                                    stats[:, si + 3:si + 4],
                                    rcpt[:, 34 + par:35 + par],
                                    ALU.subtract, ALU.mult).then_inc(sve, 1)
                    c.ve += 1
                rec.vector(_)
                restve_after[i] = c.ve
                n2_ve = c.ve
                def _(sc):
                    sc.wait_ge(sve, n2_ve)
                    sc.dma_start(out=d_out[bidx, 128 * tc:128 * tc + 128, :],
                                 in_=ost[par][:]).then_inc(sw, 16)
                    c.w += 16
                rec.sync(_)
                out_after[i] = c.w

            for i in range(N):
                head(i)
                if i >= 3:
                    rest_b(i - 3)
                if i >= 1:
                    deqblk(i - 1)
                if i >= 2:
                    rest_a(i - 2)
            rest_b(N - 3)
            deqblk(N - 1)
            rest_a(N - 2)
            rest_b(N - 2)
            rest_a(N - 1)
            rest_b(N - 1)
            return c.w

        x_rdy = load_xT(0, 0)
        w_cts = load_w_chunked(d_wqT, 0)
        ve_ = proj_qk(bq, qt, x_rdy, w_cts, consts_ve)
        w_cts = load_w_chunked(d_wkT, c.te)
        ve_ = proj_qk(bk, kt, x_rdy, w_cts, ve_)
        w_cts = load_w_chunked(d_wvT, c.te)
        ve_ = proj_v(x_rdy, w_cts, ve_)
        projV0_te = c.te

        if stop_after == "proj0":
            _finish(nc, rec, c, sw, sx, ste, sve, sse, sgp, snb, sout,
                    d_dbg, DUMPABLE)
            rec.replay(block)
            return nc

        x_rdy1 = load_xT(1, projV0_te)
        a_ve, pre_gp, tails = attention(at0, ve_)
        if stop_after == "attn0":
            for f in tails:
                f()
            _finish(nc, rec, c, sw, sx, ste, sve, sse, sgp, snb, sout,
                    d_dbg, DUMPABLE)
            rec.replay(block)
            return nc

        w_cts = load_w_chunked(d_wqT, projV0_te, wait_gp=pre_gp)
        for f in tails:
            f()
        tail_gp0 = c.gp
        ve_ = proj_qk(bq, qt, x_rdy1, w_cts, a_ve)
        w_cts = load_w_chunked(d_wkT, c.te)
        ve_ = proj_qk(bk, kt, x_rdy1, w_cts, ve_)
        w_cts = load_w_chunked(d_wvT, c.te)
        ve_ = proj_v(x_rdy1, w_cts, ve_, tail_gp=tail_gp0)
        projV1_te = c.te
        a_ve, pre_gp, tails = attention(at1, ve_)

        wo_rdy = load_w(d_woT, projV1_te, dst=wofull, wait_gp=pre_gp)
        for f in tails:
            f()
        tail_gp1 = c.gp
        a_se = c.se
        ve_, se_ = wo_ln1(0, at0, a_ve, a_se, wo_rdy, projV1_te, pre_gp)
        if stop_after == "wo0r":
            _finish(nc, rec, c, sw, sx, ste, sve, sse, sgp, snb, sout,
                    d_dbg, DUMPABLE)
            rec.replay(block)
            return nc
        var0_ve = ln1_var(0, ve_, se_)
        if stop_after == "wo0":
            ln1_apply(0, var0_ve)
            _finish(nc, rec, c, sw, sx, ste, sve, sse, sgp, snb, sout,
                    d_dbg, DUMPABLE)
            rec.replay(block)
            return nc
        ap0 = ln1_apply(0, var0_ve, chunk=4)
        h1se = {}
        def _mid1(tc):
            t0, t1 = (0, 8) if tc == 8 else (8, 12)
            v_ = ln1_var(1, c.ve, c.se, t0, t1, bias_col=rcpt[:, 32:33])
            h1se.update(ln1_apply(1, v_, t0, t1))
        extra = []
        for tcx in range(TC):
            def _mx(v, tcx=tcx):
                v.wait_ge(sse, ap0[tcx])
                v.reduce_max(stats[:, 224 + tcx:224 + tcx + 1],
                             ln1_0[:, D * tcx:D * tcx + D],
                             axis=AX.X, apply_absolute_value=True) \
                 .then_inc(sve, 1)
                c.ve += 1
            extra.append(_mx)
        ve_, se_ = wo_ln1(1, at1, a_ve, a_se, wo_rdy, projV1_te, tail_gp1,
                          mid_hook=_mid1, gp_bias=True, extra_ve=extra)
        for f in extra[14:]:
            rec.vector(f)
        var1_ve = ln1_var(1, ve_, se_, 12, TC, bias_col=rcpt[:, 32:33])
        ln1_max(1, max(h1se.values()), 0, 12)
        ap1 = ln1_apply(1, var1_ve, 12, TC)
        ve_ = ln1_max(1, max(ap1.values()), 12, TC)

        wo1_te = c.te
        xs_ve = global_scale(ve_)
        w8_rdy = load_w(d_w8T, wo1_te, dst=w8full)
        def _(g):
            g.wait_ge(ste, wo1_te)
            g.dma_start(out=bff[:], in_=d_bf).then_inc(sx, 16)
            c.x += 16
        rec.gpsimd(_)
        bff_x = c.x
        def _(s):
            s.wait_ge(ste, wo1_te)
            s.dma_start(out=wsc[:], in_=d_ws).then_inc(sw, 16)
            c.w += 16
        rec.sync(_)
        wsc_w = c.w

        ffn(xs_ve, bff_x, w8_rdy, wsc_w)

        _finish(nc, rec, c, sw, sx, ste, sve, sse, sgp, snb, sout,
                d_dbg, DUMPABLE)
        rec.replay(block)
    return nc


def _finish(nc, rec, c, sw, sx, ste, sve, sse, sgp, snb, sout, d_dbg,
            DUMPABLE):
    def _(s):
        s.wait_ge(sve, c.ve)
        s.wait_ge(sse, c.se)
        s.wait_ge(ste, c.te)
        s.wait_ge(snb, c.nb)
        for n, dap in d_dbg.items():
            s.dma_start(out=dap, in_=DUMPABLE[n][0][:]).then_inc(sw, 16)
            c.w += 16
        s.wait_ge(sx, c.x)
        s.wait_ge(sw, c.w)
        s.wait_ge(sgp, c.gp)
    rec.sync(_)
    def _(g):
        g.wait_ge(sw, c.w)
    rec.gpsimd(_)


def _prep_shared(inputs):
    f32 = np.float32
    ca = np.ascontiguousarray
    sh = {}
    sh["wqT"] = ca(np.asarray(inputs["L0_attn_Wq"], f32).T)
    sh["wkT"] = ca(np.asarray(inputs["L0_attn_Wk"], f32).T)
    sh["wvT"] = ca(np.asarray(inputs["L0_attn_Wv"], f32).T)
    sh["woT"] = ca(np.asarray(inputs["L0_attn_Wo"], f32).T)
    sh["w8T"] = ca(np.asarray(inputs["L0_ffn_weight_int8"]).T)
    sh["bq_pm"] = ca(np.asarray(inputs["L0_attn_bq"], f32).reshape(NCH, 128).T)
    sh["bk_pm"] = ca(np.asarray(inputs["L0_attn_bk"], f32).reshape(NCH, 128).T)
    sh["bv_bc"] = ca(np.broadcast_to(np.asarray(inputs["L0_attn_bv"], f32), (128, D)))
    sh["bo_bc"] = ca(np.broadcast_to(np.asarray(inputs["L0_attn_bo"], f32), (128, D)))
    sh["bf_bc"] = ca(np.broadcast_to(np.asarray(inputs["L0_ffn_bias"], f32), (128, D)))
    ws = np.asarray(inputs["L0_ffn_weight_scale"], f32)[:, 0]
    sh["ws_bc"] = ca(np.broadcast_to(ws, (128, D)))
    sh["ident"] = np.eye(128, dtype=ml_dtypes.bfloat16)
    sh["mask"] = np.triu(np.ones((128, 128), dtype=ml_dtypes.bfloat16))
    return sh


def kernel(**inputs):
    x = np.asarray(inputs["L0_attn_x"], np.float32)
    sh = _prep_shared(inputs)
    nc = build_program()
    in_maps = []
    for cid in range(8):
        shard = np.ascontiguousarray(x[B * cid:B * cid + B])
        shardT = np.ascontiguousarray(shard.transpose(0, 2, 1))
        m = dict(sh)
        m["x"] = shard
        m["xT"] = shardT
        in_maps.append(m)
    res = run_bass_kernel_spmd(nc, in_maps, core_ids=list(range(8)))
    return np.concatenate([r["out"] for r in res.results], axis=0)


# version 18
# speedup vs baseline: 1.1537x; 1.0099x over previous
import numpy as np
import ml_dtypes
import concourse.bass as bass
import concourse.mybir as mybir
from concourse.bass_utils import run_bass_kernel_spmd
from concourse.hw_specs import get_activation_tables

_ACT_SET_IDX = list(get_activation_tables("gen3").keys()).index(
    "natural_log_exp_and_others")

F32 = mybir.dt.float32
BF16 = mybir.dt.bfloat16
I8 = mybir.dt.int8
AF = mybir.ActivationFunctionType
ALU = mybir.AluOpType
AX = mybir.AxisListType

B = 2
T = 2048
D = 1024
H = 16
HD = 64
NCH = D // 128
KC = T // 128
TC = T // 128
QH = 1024
MAGIC = float(3 * 2 ** 22)
EPS = 1e-5
KB = 1024
VPAIR = 129
VCHUNK = VPAIR * (H // 2)
VW = VCHUNK * KC


def pieces(lo, hi=QH, step=512):
    out, p = [], lo
    while p < hi:
        q = min((p // step + 1) * step, hi)
        out.append((p, q))
        p = q
    return out


class Ctr:
    def __init__(self):
        self.w = 0; self.x = 0; self.te = 0; self.ve = 0
        self.se = 0; self.gp = 0; self.out = 0; self.cc = 0
        self.nb = 0; self.xa = 0; self.xb = 0


class _MockIns:
    __slots__ = ("entry",)
    def __init__(self, entry):
        self.entry = entry
    def then_inc(self, sem, n):
        self.entry[3] = (sem, n)
        return self


class _MockEng:
    __slots__ = ("q", "dma")
    def __init__(self, q, dma=None):
        self.q = q
        self.dma = dma
    def __getattr__(self, name):
        q = self.q
        if name == "dma_start" and self.dma is not None:
            sem, cnt = self.dma
            def call_dma(*a, **k):
                if cnt[0] > 0:
                    q.append(["wait_ge", (sem, 16 * cnt[0]), {}, None])
                cnt[0] += 1
                entry = [name, a, k, None]
                q.append(entry)
                return _MockIns(entry)
            return call_dma
        if name == "dma_start_noser" and self.dma is not None:
            sem, cnt = self.dma
            def call_dma_ns(*a, **k):
                cnt[0] += 1
                entry = ["dma_start", a, k, None]
                q.append(entry)
                return _MockIns(entry)
            return call_dma_ns
        if name == "dma_start_alien":
            def call_dma_al(*a, **k):
                entry = ["dma_start", a, k, None]
                q.append(entry)
                return _MockIns(entry)
            return call_dma_al
        def call(*a, **k):
            entry = [name, a, k, None]
            q.append(entry)
            return _MockIns(entry)
        return call


class _Rec:
    def __init__(self):
        self.q = {n: [] for n in ("tensor", "vector", "scalar", "gpsimd", "sync")}
        self.dma_cfg = {}
    def set_dma(self, engine, sem):
        self.dma_cfg[engine] = (sem, [0])
    def tensor(self, f): f(_MockEng(self.q["tensor"]))
    def vector(self, f): f(_MockEng(self.q["vector"]))
    def scalar(self, f): f(_MockEng(self.q["scalar"]))
    def gpsimd(self, f): f(_MockEng(self.q["gpsimd"], self.dma_cfg.get("gpsimd")))
    def sync(self, f): f(_MockEng(self.q["sync"], self.dma_cfg.get("sync")))
    def replay(self, block):
        def mk(q):
            def run(eng):
                for name, a, k, ti in q:
                    ins = getattr(eng, name)(*a, **k)
                    if ti is not None:
                        ins.then_inc(*ti)
            return run
        block.tensor(mk(self.q["tensor"]))
        block.vector(mk(self.q["vector"]))
        block.scalar(mk(self.q["scalar"]))
        block.gpsimd(mk(self.q["gpsimd"]))
        block.sync(mk(self.q["sync"]))


def build_program(stop_after="full", dump=()):
    nc = bass.Bass("TRN2", target_bir_lowering=False, debug=False)

    d_xT = nc.dram_tensor("xT", [B, D, T], F32, kind="ExternalInput").ap()
    d_x = nc.dram_tensor("x", [B, T, D], F32, kind="ExternalInput").ap()
    d_wqT = nc.dram_tensor("wqT", [D, D], F32, kind="ExternalInput").ap()
    d_wkT = nc.dram_tensor("wkT", [D, D], F32, kind="ExternalInput").ap()
    d_wvT = nc.dram_tensor("wvT", [D, D], F32, kind="ExternalInput").ap()
    d_woT = nc.dram_tensor("woT", [D, D], F32, kind="ExternalInput").ap()
    d_w8T = nc.dram_tensor("w8T", [D, D], I8, kind="ExternalInput").ap()
    d_bq = nc.dram_tensor("bq_pm", [128, NCH], F32, kind="ExternalInput").ap()
    d_bk = nc.dram_tensor("bk_pm", [128, NCH], F32, kind="ExternalInput").ap()
    d_bv = nc.dram_tensor("bv_bc", [128, D], F32, kind="ExternalInput").ap()
    d_bo = nc.dram_tensor("bo_bc", [128, D], F32, kind="ExternalInput").ap()
    d_bf = nc.dram_tensor("bf_bc", [128, D], F32, kind="ExternalInput").ap()
    d_ws = nc.dram_tensor("ws_bc", [128, D], F32, kind="ExternalInput").ap()
    d_id = nc.dram_tensor("ident", [128, 128], BF16, kind="ExternalInput").ap()
    d_mk = nc.dram_tensor("mask", [128, 128], BF16, kind="ExternalInput").ap()
    d_out = nc.dram_tensor("out", [B, T, D], F32, kind="ExternalOutput").ap()
    d_cci = nc.dram_tensor("cc_in", [1, 8], F32).ap()
    d_cco = nc.dram_tensor("cc_out", [1, 8], F32).ap()

    BASE = 16512
    at_ = nc.alloc_sbuf_tensor_at
    cur = [BASE]

    def place(name, shape, dtype, align=64):
        sz = int(np.prod(shape[1:])) * mybir.dt.size(dtype)
        cur[0] = (cur[0] + align - 1) // align * align
        h = at_(name, list(shape), dtype, offset=cur[0])
        cur[0] += sz
        return h

    ident = place("ident", [128, 128], BF16)
    mask = place("mask", [128, 128], BF16)
    bq = place("bq", [128, NCH], F32)
    bk = place("bk", [128, NCH], F32)
    bv = place("bv", [128, D], BF16)
    bo = place("bo", [128, D], BF16)
    epsc = place("epsc", [128, 1], F32)
    magicc = place("magicc", [128, 1], F32)
    nmagicc = place("nmagicc", [128, 1], F32)
    xs_col = place("xs_col", [128, 1], F32)
    rxs_col = place("rxs_col", [128, 1], F32)
    stats = place("stats", [128, 256], F32)
    gmaxc = place("gmaxc", [128, 1], F32)
    gmax1 = place("gmax1", [1, 4], F32)
    xsc_tmp = place("xsc_tmp", [1, 4], F32)
    cc_stage = place("cc_stage", [1, 8], F32)
    rcpt = place("rcpt", [128, 40], F32)
    ones_row = place("ones_row", [1, 128], F32)
    pt0 = place("pt0", [128, QH], BF16)
    pt1 = place("pt1", [128, QH], BF16)
    PT23 = (cur[0] + 63) // 64 * 64
    pt2 = place("pt2", [128, QH], BF16)
    pt3 = place("pt3", [128, QH], BF16)
    tail_bc = at_("tail_bc", [128, QH], F32, offset=PT23)
    assert cur[0] <= BASE + 18 * KB, cur[0]

    UNION = BASE + 18 * KB
    bcastt = [at_("bcast0", [128, QH], F32, offset=UNION),
              at_("bcast1", [128, QH], F32, offset=UNION + 4 * KB)]
    ostg = [at_("ostg0", [128, QH], F32, offset=UNION + 8 * KB),
            at_("ostg1", [128, QH], F32, offset=UNION + 12 * KB),
            at_("ostg2", [128, QH], F32, offset=UNION + 16 * KB),
            at_("ostg3", [128, QH], F32, offset=UNION + 20 * KB)]
    wfull = at_("wfull", [128, NCH * D], BF16, offset=UNION)
    xstage = [at_("xstage0", [128, D], F32, offset=UNION),
              at_("xstage1", [128, D], F32, offset=UNION + 4 * KB)]
    wofull = at_("wofull", [128, NCH * D], BF16, offset=UNION + 8 * KB)
    qtmp = at_("qtmp", [128, D], F32, offset=UNION)
    d1t = at_("d1t", [128, D], F32, offset=UNION + 4 * KB)
    d2t = [at_("d2t0", [128, D], F32, offset=UNION + 8 * KB),
           at_("d2t1", [128, D], F32, offset=UNION + 12 * KB)]
    bff = at_("bff", [128, D], F32, offset=UNION + 16 * KB)
    wsc = at_("wsc", [128, D], F32, offset=UNION + 20 * KB)
    UNION_END = UNION + 24 * KB

    RXT = UNION_END
    xts = at_("xts", [128, NCH * T], BF16, offset=RXT)
    at1 = at_("at1", [128, NCH * T], BF16, offset=RXT)
    w8full = at_("w8full", [128, NCH * D], BF16, offset=RXT)
    xqt = [at_("xqt0", [128, D], BF16, offset=RXT + 16 * KB),
           at_("xqt1", [128, D], BF16, offset=RXT + 18 * KB),
           at_("xqt2", [128, D], BF16, offset=RXT + 20 * KB),
           at_("xqt3", [128, D], BF16, offset=RXT + 22 * KB)]
    ost = [at_("ost0", [128, D], F32, offset=RXT + 24 * KB),
           at_("ost1", [128, D], F32, offset=RXT + 28 * KB)]

    RQ = RXT + 32 * KB
    qt = at_("qt", [128, NCH * T], BF16, offset=RQ)
    RK = RQ + 32 * KB
    kt = at_("kt", [128, NCH * T], BF16, offset=RK)
    ln1_0 = at_("ln1_0", [128, TC * D], F32, offset=RQ)
    RV = RK + 32 * KB
    vaug = at_("vaug", [128, VW], BF16, offset=RV)
    tail_s = [at_("tail_s0", [128, QH], F32, offset=RV),
              at_("tail_s1", [128, QH], F32, offset=RV + 4 * KB)]
    RA0 = RV + 33 * KB
    at0 = at_("at0", [128, NCH * T], BF16, offset=RA0)
    ln1_1 = at_("ln1_1", [128, TC * D], F32, offset=RV)
    assert RA0 + 32 * KB <= 229344, RA0 + 32 * KB

    ATB = [at0, at1]
    LN1 = [ln1_0, ln1_1]
    DUMPABLE = {"qt": (qt, BF16, NCH * T), "kt": (kt, BF16, NCH * T),
                "vaug": (vaug, BF16, VW), "at0": (at0, BF16, NCH * T),
                "at1": (at1, BF16, NCH * T), "ln1_0": (ln1_0, F32, TC * D),
                "ln1_1": (ln1_1, F32, TC * D), "stats": (stats, F32, 256),
                "xs_col": (xs_col, F32, 1)}
    d_dbg = {n: nc.dram_tensor("dbg_" + n, [128, DUMPABLE[n][2]],
                               DUMPABLE[n][1], kind="ExternalOutput").ap()
             for n in dump}

    c = Ctr()

    with (
        nc.psum_tensor("psA", [128, 2048], F32) as psA,
        nc.psum_tensor("psB", [128, 2048], F32) as psB,
        nc.semaphore("sw") as sw,
        nc.semaphore("sx") as sx,
        nc.semaphore("ste") as ste,
        nc.semaphore("sve") as sve,
        nc.semaphore("sse") as sse,
        nc.semaphore("sgp") as sgp,
        nc.semaphore("sout") as sout,
        nc.semaphore("scc") as scc,
        nc.semaphore("snb") as snb,
        nc.semaphore("sxa") as sxa,
        nc.semaphore("sxb") as sxb,
        nc.Block() as block,
    ):
        rec = _Rec()
        rec.set_dma('sync', sw)
        rec.set_dma('gpsimd', sx)
        def _(s):
            s.add_instruction(mybir.InstLoadActFuncSet(
                name=nc.get_next_instruction_name(),
                act_func_set_id=_ACT_SET_IDX, ins=[], outs=[]))
        rec.scalar(_)

        def _(s):
            for dst, src in [(ident, d_id), (mask, d_mk), (bq, d_bq), (bk, d_bk)]:
                s.dma_start(out=dst[:], in_=src).then_inc(sw, 16)
                c.w += 16
        rec.sync(_)
        consts_w = c.w

        def _(g):
            g.dma_start(out=bv[:], in_=d_bv).then_inc(sx, 16); c.x += 16
            g.dma_start(out=bo[:], in_=d_bo).then_inc(sx, 16); c.x += 16
        rec.gpsimd(_)
        bias_x = c.x

        def _(v):
            v.memset(epsc[:], EPS)
            v.memset(magicc[:], MAGIC)
            v.memset(nmagicc[:], -MAGIC)
            v.memset(cc_stage[:], 0.0)
            v.memset(ones_row[:], 1.0)
            v.memset(bass.AP(vaug, 64, [[VW, 128], [VCHUNK, KC],
                                        [VPAIR, H // 2]]), 1.0).then_inc(sve, 1)
            c.ve += 1
        rec.vector(_)
        consts_ve = c.ve
        def _(v):
            v.wait_ge(sx, bias_x)
            v.reduce_sum(rcpt[:, 32:33], bo[:], axis=AX.X).then_inc(sve, 1)
            c.ve += 1
        rec.vector(_)

        def load_xT(b, wait_te):
            def _(g):
                g.wait_ge(ste, wait_te)
                g.wait_ge(sx, c.x)
                for ch in range(NCH):
                    half = sxa if ch < 4 else sxb
                    g.dma_start_alien(
                        out=xts[:, T * ch:T * ch + T],
                        in_=d_xT[b, 128 * ch:128 * ch + 128, :],
                    ).then_inc(half, 16)
                    if ch < 4:
                        c.xa += 16
                    else:
                        c.xb += 16
            rec.gpsimd(_)
            return c.xa, c.xb

        def load_w(src, wait_te, dst=None, wait_ve=None, wait_gp=None):
            dst = wfull if dst is None else dst
            def _(g):
                g.wait_ge(ste, wait_te)
                if wait_ve is not None:
                    g.wait_ge(sve, wait_ve)
                if wait_gp is not None:
                    g.wait_ge(sgp, wait_gp)
                g.dma_start(
                    out=dst.ap().rearrange("p (ch n) -> p ch n", ch=NCH),
                    in_=src.rearrange("(ch p) n -> p ch n", p=128),
                ).then_inc(sx, 16)
                c.x += 16
            rec.gpsimd(_)
            return c.x

        def load_w_chunked(src, wait_te, wait_ve=None, wait_gp=None):
            counts = []
            def _(g):
                for ch in range(NCH):
                    if ch == 0:
                        g.wait_ge(ste, wait_te)
                        if wait_ve is not None:
                            g.wait_ge(sve, wait_ve)
                        if wait_gp is not None:
                            g.wait_ge(sgp, wait_gp)
                    g.dma_start(
                        out=wfull[:, D * ch:D * ch + D],
                        in_=src[128 * ch:128 * ch + 128, :],
                    ).then_inc(sx, 16)
                    c.x += 16
                    counts.append(c.x)
            rec.gpsimd(_)
            return counts

        def proj_qk(bias_col, dst, x_ready, w_counts, prev_ve):
            xa_t, xb_t = x_ready
            evac_after = {}
            entry_se = c.se
            for nch in range(NCH):
                arena = psA if nch % 2 == 0 else psB
                def _(t, nch=nch, arena=arena):
                    if nch < 2:
                        t.wait_ge(sse, entry_se)
                    t.wait_ge(sve, evac_after.get(nch - 2, prev_ve))
                    ins = None
                    for kc in range(NCH):
                        if nch == 0:
                            t.wait_ge(sxa if kc < 4 else sxb,
                                      xa_t if kc < 4 else xb_t)
                            t.wait_ge(sx, w_counts[kc])
                        for tt in range(4):
                            ins = t.matmul(
                                arena[:, 512 * tt:512 * tt + 512],
                                wfull[:, D * kc + 128 * nch:
                                      D * kc + 128 * nch + 128],
                                xts[:, T * kc + 512 * tt:T * kc + 512 * tt + 512],
                                start=(kc == 0), stop=(kc == NCH - 1))
                    ins.then_inc(ste, 1)
                    c.te += 1
                rec.tensor(_)
                te_done = c.te
                def _(v, nch=nch, arena=arena, te_done=te_done):
                    v.wait_ge(ste, te_done)
                    v.wait_ge(sw, consts_w)
                    ins = None
                    for tt in range(4):
                        ins = v.tensor_scalar(
                            dst[:, T * nch + 512 * tt:T * nch + 512 * tt + 512],
                            arena[:, 512 * tt:512 * tt + 512],
                            bias_col[:, nch:nch + 1], None, ALU.add)
                    ins.then_inc(sve, 1)
                    c.ve += 1
                rec.vector(_)
                evac_after[nch] = c.ve
            return c.ve

        def proj_v(x_ready, w_counts, prev_ve, tail_gp=0):
            evac_after = {}
            entry_se = c.se
            if tail_gp:
                def _(v):
                    v.wait_ge(sgp, tail_gp)
                    v.memset(bass.AP(vaug, 64, [[VW, 128], [VCHUNK, KC],
                                                [VPAIR, H // 2]]), 1.0) \
                     .then_inc(sve, 1)
                    c.ve += 1
                rec.vector(_)
            xa_t, xb_t = x_ready
            for tc in range(TC):
                arena = psA if tc % 2 == 0 else psB
                def _(t, tc=tc, arena=arena):
                    if tc < 2:
                        t.wait_ge(sse, entry_se)
                    t.wait_ge(sve, evac_after.get(tc - 2, prev_ve))
                    ins = None
                    for kc in range(NCH):
                        if tc == 0:
                            t.wait_ge(sxa if kc < 4 else sxb,
                                      xa_t if kc < 4 else xb_t)
                            t.wait_ge(sx, w_counts[kc])
                        for nt in range(2):
                            ins = t.matmul(
                                arena[:, 512 * nt:512 * nt + 512],
                                xts[:, T * kc + 128 * tc:T * kc + 128 * tc + 128],
                                wfull[:, D * kc + 512 * nt:D * kc + 512 * nt + 512],
                                start=(kc == 0), stop=(kc == NCH - 1))
                    ins.then_inc(ste, 1)
                    c.te += 1
                rec.tensor(_)
                te_done = c.te
                def _(v, tc=tc, arena=arena, te_done=te_done):
                    v.wait_ge(ste, te_done)
                    v.wait_ge(sx, bias_x)
                    ins = None
                    for nt in range(2):
                        base = VCHUNK * tc + VPAIR * 4 * nt
                        dst = bass.AP(vaug, base,
                                      [[VW, 128], [VPAIR, 4], [65, 2], [1, 64]])
                        src = bass.AP(arena.ap().tensor,
                                      arena.ap().offset + 512 * nt,
                                      [[2048, 128], [128, 4], [64, 2], [1, 64]])
                        b_ap = bass.AP(bv, 512 * nt,
                                       [[D, 128], [128, 4], [64, 2], [1, 64]])
                        ins = v.tensor_tensor(dst, src, b_ap, ALU.add)
                    ins.then_inc(sve, 1)
                    c.ve += 1
                rec.vector(_)
                evac_after[tc] = c.ve
            return c.ve

        def attention(at_dst, ready_ve):
            entry_se = c.se
            entry_gp = c.gp
            seg_i = [0]
            half_i = [0]
            exp_after = {}
            pv_after = {}
            norm_after = {}
            stage_after = {}

            class Stream:
                def __init__(self, h, hf):
                    self.h, self.hf = h, hf
                    self.q0h = QH * hf
                    self.js = [j for j in range(KC) if 128 * j < self.q0h + QH]
                    self.hidx = half_i[0]
                    half_i[0] += 1
                    self.osl = self.hidx % 2
                    self.obuf = psB[:, QH * self.osl:QH * self.osl + QH]
                    self.segs = []
                    self.ch = h // 2
                    self.po = 64 * (h % 2)
                    self.pblk = VPAIR * (h // 2 % (H // 2))

            def emit_s(st, j):
                qs = max(st.q0h, 128 * j)
                lo = qs - st.q0h
                si = seg_i[0]
                sp = si % 2
                sps = psA[:, QH * sp:QH * sp + QH]
                ptb = (pt0, pt1, pt2, pt3)[si % 4]
                def _(t):
                    t.wait_ge(sve, ready_ve)
                    if si >= 2:
                        t.wait_ge(sse, exp_after[si - 2])
                    else:
                        t.wait_ge(sse, entry_se)
                    ins = None
                    for (plo, phi) in pieces(lo):
                        ins = t.matmul(
                            sps[:, plo:phi],
                            kt[st.po:st.po + 64,
                               T * st.ch + 128 * j:T * st.ch + 128 * j + 128],
                            qt[st.po:st.po + 64,
                               T * st.ch + st.q0h + plo:T * st.ch + st.q0h + phi],
                            start=True, stop=True)
                    ins.then_inc(ste, 1)
                    c.te += 1
                rec.tensor(_)
                s_done = c.te
                def _(sc):
                    sc.wait_ge(ste, s_done)
                    if si >= 4:
                        sc.wait_ge(ste, pv_after[si - 4])
                    sc.activation(ptb[:, lo:QH], sps[:, lo:QH], AF.Exp,
                                  scale=0.125).then_inc(sse, 1)
                    c.se += 1
                rec.scalar(_)
                exp_after[si] = c.se
                e_done = c.se
                m_done = None
                if qs == 128 * j:
                    def _(v):
                        v.wait_ge(sse, e_done)
                        v.tensor_tensor(ptb[:, lo:lo + 128],
                                        ptb[:, lo:lo + 128], mask[:],
                                        ALU.mult).then_inc(sve, 1)
                        c.ve += 1
                    rec.vector(_)
                    m_done = c.ve
                st.segs.append((j, lo, ptb, e_done, m_done, si))
                seg_i[0] += 1

            def emit_pv(st, k):
                (j, lo, ptb, e_done, m_done, si) = st.segs[k]
                first = (k == 0)
                last_j = st.js[-1]
                def _(t):
                    if first:
                        if st.hidx >= 2:
                            t.wait_ge(sve, stage_after[st.hidx - 2])
                        else:
                            t.wait_ge(sve, ready_ve)
                    if m_done is not None:
                        t.wait_ge(sve, m_done)
                    else:
                        t.wait_ge(sse, e_done)
                    if st.h % 2 == 0:
                        lhs = vaug[:, VCHUNK * j + st.pblk:
                                   VCHUNK * j + st.pblk + 65]
                        nrow = 65
                    else:
                        lhs = vaug[:, VCHUNK * j + st.pblk + 1:
                                   VCHUNK * j + st.pblk + 129]
                        nrow = 128
                    ins = None
                    for (plo, phi) in pieces(lo):
                        bank_last = min((st.q0h + phi - 1) // 128, last_j)
                        ins = t.matmul(st.obuf[0:nrow, plo:phi], lhs,
                                       ptb[:, plo:phi],
                                       start=(j == 0), stop=(j == bank_last))
                    ins.then_inc(ste, 1)
                    c.te += 1
                rec.tensor(_)
                pv_after[si] = c.te

            def emit_norm_stage(st):
                pv_done = pv_after[st.segs[-1][5]]
                stg = tail_s[st.hidx - 2 * H + 2] if st.hidx >= 2 * H - 2 \
                    else ostg[st.hidx % 4]
                even = (st.h % 2 == 0)
                def _(v):
                    v.wait_ge(ste, pv_done)
                    v.wait_ge(sgp, norm_after.get(st.hidx - 4, entry_gp))
                    if even:
                        v.tensor_copy(stg[0:65, :], st.obuf[0:65, :]) \
                         .then_inc(sve, 1)
                    else:
                        v.tensor_copy(stg[32:64, :], st.obuf[32:64, :])
                        v.tensor_copy(stg[64:128, :], st.obuf[64:128, :]) \
                         .then_inc(sve, 1)
                    c.ve += 1
                    if not even:
                        v.wait_ge(sve, c.ve)
                        v.tensor_scalar(stg[32:64, :], stg[32:64, :], 1e-20,
                                        1e18, ALU.max, ALU.min).then_inc(sve, 1)
                        c.ve += 1
                rec.vector(_)
                stage_after[st.hidx] = c.ve
                return c.ve

            def emit_norm_finish(st, cp_done):
                h, ch, q0h = st.h, st.ch, st.q0h
                tail = st.hidx >= 2 * H - 2
                stg = tail_s[st.hidx - 2 * H + 2] if tail \
                    else ostg[st.hidx % 4]
                bcast = tail_bc if tail else bcastt[(st.hidx % 4) // 2]
                even = (h % 2 == 0)
                dr = stg[64:65, :] if even else stg[32:64, :]
                rr = stg[96:97, :] if even else stg[0:32, :]
                def _(sc):
                    sc.wait_ge(sve, cp_done)
                    sc.activation(rr, dr, AF.Ln).then_inc(sse, 1)
                    c.se += 1
                    sc.wait_ge(sse, c.se)
                    sc.activation(rr, rr, AF.Exp, scale=-1.0).then_inc(sse, 1)
                    c.se += 1
                rec.scalar(_)
                r_done = c.se
                r0, r1 = (0, 64) if even else (64, 128)
                if tail:
                    tail_te = c.te
                    def _(g):
                        g.wait_ge(sse, r_done)
                        g.wait_ge(ste, tail_te)
                        rsrc = stg[96:97, :] if even else stg[31:32, :]
                        g.dma_start_alien(
                            out=bcast[r0:r1, :],
                            in_=rsrc.unsqueeze(1)
                                .broadcast_to((1, r1 - r0, QH))) \
                         .then_inc(snb, 16)
                        c.nb += 16
                    rec.gpsimd(_)
                    b_done = c.nb
                else:
                    def _(sc):
                        sc.wait_ge(sse, r_done)
                        rsrc = stg[96:97, :] if even else stg[31:32, :]
                        sc.dma_start(
                            out=bcast[r0:r1, :],
                            in_=rsrc.unsqueeze(1)
                                .broadcast_to((1, r1 - r0, QH))) \
                         .then_inc(sw, 16)
                        c.w += 16
                    rec.sync(_)
                    b_done = c.w
                def _(g):
                    g.wait_ge(snb if tail else sw, b_done)
                    g.tensor_tensor(
                        at_dst[r0:r1, T * ch + q0h:T * ch + q0h + QH],
                        stg[0:64, :] if even else stg[64:128, :],
                        bcast[r0:r1, :], ALU.mult).then_inc(sgp, 1)
                    c.gp += 1
                rec.gpsimd(_)
                norm_after[st.hidx] = c.gp

            pending = []
            for hp in range(0, H, 2):
                for hf in range(2):
                    sa = Stream(hp, hf)
                    sb = Stream(hp + 1, hf)
                    n = len(sa.js)
                    for k in range(n):
                        emit_s(sa, sa.js[k])
                        emit_s(sb, sb.js[k])
                        if k in (2, 5) and pending:
                            emit_norm_finish(*pending.pop(0))
                        if k >= 1:
                            emit_pv(sa, k - 1)
                            emit_pv(sb, k - 1)
                    emit_pv(sa, n - 1)
                    emit_pv(sb, n - 1)
                    pending.append((sa, emit_norm_stage(sa)))
                    pending.append((sb, emit_norm_stage(sb)))
            thunks = [(lambda st=st, cp=cp: emit_norm_finish(st, cp))
                      for st, cp in pending]
            return c.ve, c.gp, thunks

        def wo_ln1(bidx, at_src, attn_ve, attn_se, wo_ready, first_xstage_te,
                   attn_gp=0, mid_hook=None, gp_bias=False, extra_ve=None):
            ln1 = LN1[bidx]
            ev_after = {}
            entry_ve = c.ve
            extra_ve = list(extra_ve) if extra_ve else []
            for tc in range(TC):
                if tc == 8 and mid_hook is not None:
                    mid_hook()
                arena = psA if tc % 2 == 0 else psB
                xst = xstage[tc % 2]
                def _(s):
                    if tc < 2:
                        s.wait_ge(ste, first_xstage_te)
                        s.wait_ge(sve, entry_ve)
                        s.wait_ge(sgp, attn_gp)
                    else:
                        s.wait_ge(sve, ev_after[tc - 2])
                    s.dma_start(out=xst[:],
                                in_=d_x[bidx, 128 * tc:128 * tc + 128, :]) \
                        .then_inc(sw, 16)
                    c.w += 16
                rec.sync(_)
                x_done = c.w
                def _(t):
                    t.wait_ge(sve, attn_ve)
                    t.wait_ge(sse, attn_se)
                    t.wait_ge(sgp, attn_gp)
                    t.wait_ge(sx, wo_ready)
                    t.wait_ge(sve, ev_after.get(tc - 2, entry_ve))
                    ins = None
                    for dch in range(NCH):
                        for nt in range(2):
                            ins = t.matmul(
                                arena[:, 512 * nt:512 * nt + 512],
                                at_src[:, T * dch + 128 * tc:
                                       T * dch + 128 * tc + 128],
                                wofull[:, D * dch + 512 * nt:
                                       D * dch + 512 * nt + 512],
                                start=(dch == 0), stop=(dch == NCH - 1))
                    ins.then_inc(ste, 1)
                    c.te += 1
                rec.tensor(_)
                mm_done = c.te
                si = bidx * TC + tc
                lt = ln1[:, D * tc:D * tc + D]
                gp_done = None
                if gp_bias:
                    def _(v):
                        v.wait_ge(ste, mm_done)
                        v.wait_ge(sw, x_done)
                        v.tensor_tensor(lt, arena[:, 0:D], xst[:], ALU.add) \
                         .then_inc(sve, 1)
                        c.ve += 1
                        v.wait_ge(sve, c.ve)
                        v.reduce_sum(stats[:, si:si + 1], lt, axis=AX.X) \
                         .then_inc(sve, 1)
                        c.ve += 1
                        if tc >= 2 and extra_ve:
                            extra_ve.pop(0)(v)
                    rec.vector(_)
                    ve_mid = c.ve
                    def _(g):
                        g.wait_ge(sve, ve_mid)
                        g.wait_ge(sx, bias_x)
                        g.tensor_tensor(lt, lt, bo[:], ALU.add).then_inc(sgp, 1)
                        c.gp += 1
                    rec.gpsimd(_)
                    gp_done = c.gp
                elif True:
                    def _(v):
                        v.wait_ge(ste, mm_done)
                        v.wait_ge(sw, x_done)
                        v.tensor_tensor(lt, arena[:, 0:D], bo[:], ALU.add) \
                         .then_inc(sve, 1)
                        c.ve += 1
                        v.wait_ge(sve, c.ve)
                        v.tensor_tensor(lt, lt, xst[:], ALU.add).then_inc(sve, 1)
                        c.ve += 1
                        v.wait_ge(sve, c.ve)
                        v.reduce_sum(stats[:, si:si + 1], lt, axis=AX.X) \
                         .then_inc(sve, 1)
                        c.ve += 1
                    rec.vector(_)
                r1_ve = c.ve
                def _(s):
                    if gp_done is not None:
                        s.wait_ge(sgp, gp_done)
                    else:
                        s.wait_ge(sve, r1_ve - 1)
                    s.wait_ge(sse, c.se)
                    s.activation(pt0[:], lt, AF.Square,
                                 accum_out=stats[:, 64 + si:64 + si + 1]) \
                     .then_inc(sse, 1)
                    c.se += 1
                rec.scalar(_)
                ev_after[tc] = c.ve
            return c.ve, c.se

        def ln1_var(bidx, prev_ve, prev_se, t0=0, t1=TC, bias_col=None):
            s0 = bidx * TC + t0
            n = t1 - t0
            sums = stats[:, s0:s0 + n]
            sumsq = stats[:, 64 + s0:64 + s0 + n]
            tcol = stats[:, 128 + s0:128 + s0 + n]
            ucol = stats[:, 160 + s0:160 + s0 + n]
            mean = stats[:, 192 + s0:192 + s0 + n]
            sigc = stats[:, 96 + s0:96 + s0 + n]
            def _(v):
                v.wait_ge(sve, prev_ve)
                v.wait_ge(sse, prev_se)
                if bias_col is not None:
                    v.tensor_scalar(sums, sums, bias_col, None, ALU.add) \
                     .then_inc(sve, 1)
                    c.ve += 1
                    v.wait_ge(sve, c.ve)
                v.scalar_tensor_tensor(tcol, sums, 1.0 / D, sums,
                                       ALU.mult, ALU.mult).then_inc(sve, 1)
                c.ve += 1
                v.tensor_scalar(mean, sums, 1.0 / D, None, ALU.mult) \
                 .then_inc(sve, 1)
                c.ve += 1
                v.wait_ge(sve, c.ve)
                v.tensor_tensor(ucol, sumsq, tcol, ALU.subtract).then_inc(sve, 1)
                c.ve += 1
                v.wait_ge(sve, c.ve)
                v.tensor_copy(rcpt[:, 38:39], ucol[:, 0:1]).then_inc(sve, 1)
                c.ve += 1
            rec.vector(_)
            u_ve = c.ve
            rstd = rcpt[:, 16 * bidx + t0:16 * bidx + t1]
            def _(s):
                s.wait_ge(sve, u_ve)
                s.activation(sigc, ucol, AF.Ln, bias=epsc[:], scale=1.0 / D) \
                 .then_inc(sse, 1)
                c.se += 1
                s.wait_ge(sse, c.se)
                s.activation(rstd, sigc, AF.Exp, scale=-0.5).then_inc(sse, 1)
                c.se += 1
                s.wait_ge(sse, c.se)
                s.copy(rcpt[:, 39:40], rstd[:, 0:1]).then_inc(sse, 1)
                c.se += 1
            rec.scalar(_)
            sig_se = c.se
            nmrs = stats[:, 128 + s0:128 + s0 + n]
            def _(v):
                v.wait_ge(sse, sig_se)
                v.scalar_tensor_tensor(nmrs, mean, -1.0, rstd,
                                       ALU.mult, ALU.mult).then_inc(sve, 1)
                c.ve += 1
                v.wait_ge(sve, c.ve)
                v.tensor_copy(rcpt[:, 38:39], nmrs[:, 0:1]).then_inc(sve, 1)
                c.ve += 1
            rec.vector(_)
            return c.ve

        def ln1_apply(bidx, prev_ve, t0=0, t1=TC, chunk=None):
            ln1 = LN1[bidx]
            rstd = rcpt[:, 16 * bidx:16 * bidx + TC]
            nmrs = stats[:, 128 + bidx * TC:128 + bidx * TC + TC]
            chunk = (t1 - t0) if chunk is None else chunk
            se_by_tile = {}
            def _(s):
                s.wait_ge(sve, prev_ve)
                for c0 in range(t0, t1, chunk):
                    ins = None
                    for tc in range(c0, min(c0 + chunk, t1)):
                        lt = ln1[:, D * tc:D * tc + D]
                        ins = s.activation(lt, lt, AF.Identity,
                                           bias=nmrs[:, tc:tc + 1],
                                           scale=rstd[:, tc:tc + 1])
                    ins.then_inc(sse, 1)
                    c.se += 1
                    for tc in range(c0, min(c0 + chunk, t1)):
                        se_by_tile[tc] = c.se
            rec.scalar(_)
            return se_by_tile

        def ln1_max(bidx, apply_se, t0=0, t1=TC):
            ln1 = LN1[bidx]
            s0 = bidx * TC
            def _(v):
                v.wait_ge(sse, apply_se)
                ins = None
                for tc in range(t0, t1):
                    lt = ln1[:, D * tc:D * tc + D]
                    ins = v.reduce_max(stats[:, 224 + s0 + tc:224 + s0 + tc + 1],
                                       lt, axis=AX.X, apply_absolute_value=True)
                ins.then_inc(sve, 1)
                c.ve += 1
            rec.vector(_)
            return c.ve

        def global_scale(prev_ve):
            def _(v):
                v.wait_ge(sve, prev_ve)
                v.reduce_max(gmaxc[:], stats[:, 224:224 + 2 * TC], axis=AX.X) \
                 .then_inc(sve, 1)
                c.ve += 1
                v.wait_ge(sve, c.ve)
                v.tensor_copy(rcpt[:, 38:39], gmaxc[:]).then_inc(sve, 1)
                c.ve += 1
            rec.vector(_)
            g_ve = c.ve
            def _(g):
                g.wait_ge(sve, g_ve)
                g.reduce_max(gmax1[0:1, 0:1], gmaxc[:], axis=AX.XYZWC) \
                 .then_inc(sgp, 1)
                c.gp += 1
                g.wait_ge(sgp, c.gp)
                g.tensor_copy(gmax1[0:1, 3:4], gmax1[0:1, 0:1]).then_inc(sgp, 1)
                c.gp += 1
            rec.gpsimd(_)
            g_gp = c.gp
            def _(v):
                v.wait_ge(sgp, g_gp)
                v.tensor_scalar(cc_stage[0:1, 0:1], gmax1[0:1, 0:1],
                                1.0 / 127.0, EPS, ALU.mult, ALU.max) \
                 .then_inc(sve, 1)
                c.ve += 1
                v.wait_ge(sve, c.ve)
                v.tensor_copy(xsc_tmp[0:1, 1:2], cc_stage[0:1, 0:1]) \
                 .then_inc(sve, 1)
                c.ve += 1
            rec.vector(_)
            st_ve = c.ve
            def _(g):
                g.wait_ge(sve, st_ve)
                g.dma_start(out=d_cci, in_=cc_stage[:]).then_inc(sx, 16)
                c.x += 16
                g.wait_ge(sx, c.x)
                g.collective_compute(
                    "AllReduce", ALU.max, replica_groups=[list(range(8))],
                    ins=[d_cci.opt()], outs=[d_cco.opt()]).then_inc(scc, 1)
                c.cc += 1
                g.wait_ge(scc, c.cc)
                g.dma_start(out=xsc_tmp[0:1, 0:1], in_=d_cco[0:1, 0:1]) \
                 .then_inc(sx, 16)
                c.x += 16
            rec.gpsimd(_)
            in_x = c.x
            def _(v):
                v.wait_ge(sx, in_x)
                v.reciprocal(gmax1[0:1, 2:3], xsc_tmp[0:1, 0:1]).then_inc(sve, 1)
                c.ve += 1
                v.wait_ge(sve, c.ve)
                v.tensor_copy(xsc_tmp[0:1, 1:2], gmax1[0:1, 2:3]) \
                 .then_inc(sve, 1)
                c.ve += 1
                v.wait_ge(sve, c.ve)
                v.tensor_copy(gmax1[0:1, 3:4], xsc_tmp[0:1, 1:2]) \
                 .then_inc(sve, 1)
                c.ve += 1
            rec.vector(_)
            rx_ve = c.ve
            def _(t):
                t.wait_ge(sve, rx_ve)
                t.matmul(psA[:, 0:2], ones_row[:], xsc_tmp[0:1, 0:2],
                         start=True, stop=True).then_inc(ste, 1)
                c.te += 1
            rec.tensor(_)
            bc_te = c.te
            def _(v):
                v.wait_ge(ste, bc_te)
                v.tensor_copy(xs_col[:], psA[:, 0:1])
                v.tensor_copy(rxs_col[:], psA[:, 1:2]).then_inc(sve, 1)
                c.ve += 1
            rec.vector(_)
            return c.ve

        def ffn(xs_ve, bff_x, w8_ready, wsbf_w):
            N = B * TC
            copy_after = {}
            mm_after = {}
            deq_after = {}
            dv_after = {}
            u2_after = {}
            sq_after = {}
            restve_after = {}
            out_after = {}

            def head(i):
                bidx, tc = i // TC, i % TC
                par4 = i % 4
                lt = LN1[bidx][:, D * tc:D * tc + D]
                ptr = psB[:, 512 * par4:512 * par4 + 512].bitcast(BF16)
                pfa = psA[:, QH * (i % 2):QH * (i % 2) + QH]
                xq = xqt[par4]
                def _(sc):
                    sc.wait_ge(sve, xs_ve)
                    sc.wait_ge(sx, bff_x)
                    sc.wait_ge(sw, wsbf_w)
                    sc.wait_ge(sse, c.se)
                    if i >= 4:
                        sc.wait_ge(ste, mm_after[i - 4])
                    sc.activation(qtmp[:], lt, AF.Identity, bias=magicc[:],
                                  scale=rxs_col[:]).then_inc(sse, 1)
                    c.se += 1
                    sc.wait_ge(sse, c.se)
                    sc.activation(xq[:], qtmp[:], AF.Identity,
                                  bias=nmagicc[:], scale=1.0).then_inc(sse, 1)
                    c.se += 1
                rec.scalar(_)
                q2_se = c.se
                def _(t):
                    t.wait_ge(sse, q2_se)
                    if i >= 4:
                        t.wait_ge(sve, copy_after[i - 4])
                    ins = None
                    for cc8 in range(NCH):
                        ins = t.transpose(ptr[:, 128 * cc8:128 * cc8 + 128],
                                          xq[:, 128 * cc8:128 * cc8 + 128],
                                          ident[:])
                    ins.then_inc(ste, 1)
                    c.te += 1
                rec.tensor(_)
                tr_done = c.te
                def _(v):
                    v.wait_ge(ste, tr_done)
                    v.tensor_copy(xq[:], ptr[:]).then_inc(sve, 1)
                    c.ve += 1
                rec.vector(_)
                copy_after[i] = c.ve
                def _(t):
                    t.wait_ge(sve, copy_after[i])
                    t.wait_ge(sx, w8_ready)
                    if i >= 2:
                        t.wait_ge(sve, deq_after[i - 2])
                    ins = None
                    for kc8 in range(NCH):
                        for nt in range(2):
                            ins = t.matmul(
                                pfa[:, 512 * nt:512 * nt + 512],
                                xq[:, 128 * kc8:128 * kc8 + 128],
                                w8full[:, D * kc8 + 512 * nt:
                                       D * kc8 + 512 * nt + 512],
                                start=(kc8 == 0), stop=(kc8 == NCH - 1))
                    ins.then_inc(ste, 1)
                    c.te += 1
                rec.tensor(_)
                mm_after[i] = c.te

            def deqblk(i):
                bidx, tc = i // TC, i % TC
                par = i % 2
                lt = LN1[bidx][:, D * tc:D * tc + D]
                pfa = psA[:, QH * par:QH * par + QH]
                si = 240 + 8 * par
                def _(v):
                    v.wait_ge(ste, mm_after[i])
                    if i >= 2:
                        v.wait_ge(sve, restve_after[i - 2])
                        v.wait_ge(sse, sq_after[i - 2])
                    v.scalar_tensor_tensor(d1t[:], pfa[:, 0:D], xs_col[:],
                                           wsc[:], ALU.mult, ALU.mult) \
                     .then_inc(sve, 1)
                    c.ve += 1
                    deq_after[i] = c.ve
                    v.wait_ge(sve, c.ve)
                    v.tensor_tensor(d2t[par][:], d1t[:], bff[:], ALU.add) \
                     .then_inc(sve, 1)
                    c.ve += 1
                    v.wait_ge(sve, c.ve)
                    v.tensor_tensor(d2t[par][:], d2t[par][:], lt, ALU.add) \
                     .then_inc(sve, 1)
                    c.ve += 1
                    v.wait_ge(sve, c.ve)
                    v.reduce_sum(stats[:, si:si + 1], d2t[par][:], axis=AX.X) \
                     .then_inc(sve, 1)
                    c.ve += 1
                rec.vector(_)
                dv_after[i] = c.ve

            def rest_a(i):
                bidx, tc = i // TC, i % TC
                par = i % 2
                si = 240 + 8 * par
                def _(sc):
                    sc.wait_ge(sve, dv_after[i] - 1)
                    sc.wait_ge(sse, c.se)
                    sc.activation(pt0[:], d2t[par][:], AF.Square,
                                  accum_out=stats[:, si + 1:si + 2]) \
                     .then_inc(sse, 1)
                    c.se += 1
                rec.scalar(_)
                sq_after[i] = c.se
                def _(v):
                    v.wait_ge(sve, dv_after[i])
                    v.wait_ge(sse, sq_after[i])
                    v.scalar_tensor_tensor(stats[:, si + 2:si + 3],
                                           stats[:, si:si + 1], 1.0 / D,
                                           stats[:, si:si + 1],
                                           ALU.mult, ALU.mult).then_inc(sve, 1)
                    c.ve += 1
                    v.tensor_scalar(stats[:, si + 3:si + 4],
                                    stats[:, si:si + 1], 1.0 / D, None,
                                    ALU.mult).then_inc(sve, 1)
                    c.ve += 1
                    v.wait_ge(sve, c.ve)
                    v.tensor_tensor(stats[:, si + 4:si + 5],
                                    stats[:, si + 1:si + 2],
                                    stats[:, si + 2:si + 3], ALU.subtract) \
                     .then_inc(sve, 1)
                    c.ve += 1
                    v.wait_ge(sve, c.ve)
                    v.tensor_copy(rcpt[:, 38:39],
                                  stats[:, si + 4:si + 5]).then_inc(sve, 1)
                    c.ve += 1
                rec.vector(_)
                u2_after[i] = c.ve

            def rest_b(i):
                bidx, tc = i // TC, i % TC
                par = i % 2
                si = 240 + 8 * par
                def _(sc):
                    sc.wait_ge(sve, u2_after[i])
                    sc.activation(stats[:, si + 5:si + 6],
                                  stats[:, si + 4:si + 5], AF.Ln,
                                  bias=epsc[:], scale=1.0 / D).then_inc(sse, 1)
                    c.se += 1
                    sc.wait_ge(sse, c.se)
                    sc.activation(rcpt[:, 34 + par:35 + par],
                                  stats[:, si + 5:si + 6], AF.Exp,
                                  scale=-0.5).then_inc(sse, 1)
                    c.se += 1
                    sc.wait_ge(sse, c.se)
                    sc.copy(rcpt[:, 36:37],
                            rcpt[:, 34 + par:35 + par]).then_inc(sse, 1)
                    c.se += 1
                rec.scalar(_)
                sig2_se = c.se
                def _(v):
                    v.wait_ge(sse, sig2_se)
                    if i >= 2:
                        v.wait_ge(sw, out_after[i - 2])
                    v.tensor_scalar(ost[par][:], d2t[par][:],
                                    stats[:, si + 3:si + 4],
                                    rcpt[:, 34 + par:35 + par],
                                    ALU.subtract, ALU.mult).then_inc(sve, 1)
                    c.ve += 1
                rec.vector(_)
                restve_after[i] = c.ve
                n2_ve = c.ve
                def _(sc):
                    sc.wait_ge(sve, n2_ve)
                    sc.dma_start(out=d_out[bidx, 128 * tc:128 * tc + 128, :],
                                 in_=ost[par][:]).then_inc(sw, 16)
                    c.w += 16
                rec.sync(_)
                out_after[i] = c.w

            for i in range(N):
                head(i)
                if i >= 3:
                    rest_b(i - 3)
                if i >= 1:
                    deqblk(i - 1)
                if i >= 2:
                    rest_a(i - 2)
            rest_b(N - 3)
            deqblk(N - 1)
            rest_a(N - 2)
            rest_b(N - 2)
            rest_a(N - 1)
            rest_b(N - 1)
            return c.w

        x_rdy = load_xT(0, 0)
        w_cts = load_w_chunked(d_wqT, 0)
        ve_ = proj_qk(bq, qt, x_rdy, w_cts, consts_ve)
        w_cts = load_w_chunked(d_wkT, c.te)
        ve_ = proj_qk(bk, kt, x_rdy, w_cts, ve_)
        w_cts = load_w_chunked(d_wvT, c.te)
        ve_ = proj_v(x_rdy, w_cts, ve_)
        projV0_te = c.te

        if stop_after == "proj0":
            _finish(nc, rec, c, sw, sx, ste, sve, sse, sgp, snb, sout,
                    d_dbg, DUMPABLE)
            rec.replay(block)
            return nc

        x_rdy1 = load_xT(1, projV0_te)
        a_ve, pre_gp, tails = attention(at0, ve_)
        if stop_after == "attn0":
            for f in tails:
                f()
            _finish(nc, rec, c, sw, sx, ste, sve, sse, sgp, snb, sout,
                    d_dbg, DUMPABLE)
            rec.replay(block)
            return nc

        w_cts = load_w_chunked(d_wqT, projV0_te, wait_gp=pre_gp)
        for f in tails:
            f()
        tail_gp0 = c.gp
        ve_ = proj_qk(bq, qt, x_rdy1, w_cts, a_ve)
        w_cts = load_w_chunked(d_wkT, c.te)
        ve_ = proj_qk(bk, kt, x_rdy1, w_cts, ve_)
        w_cts = load_w_chunked(d_wvT, c.te)
        ve_ = proj_v(x_rdy1, w_cts, ve_, tail_gp=tail_gp0)
        projV1_te = c.te
        a_ve, pre_gp, tails = attention(at1, ve_)

        wo_rdy = load_w(d_woT, projV1_te, dst=wofull, wait_gp=pre_gp)
        for f in tails:
            f()
        tail_gp1 = c.gp
        a_se = c.se
        ve_, se_ = wo_ln1(0, at0, a_ve, a_se, wo_rdy, projV1_te, pre_gp)
        if stop_after == "wo0r":
            _finish(nc, rec, c, sw, sx, ste, sve, sse, sgp, snb, sout,
                    d_dbg, DUMPABLE)
            rec.replay(block)
            return nc
        var0_ve = ln1_var(0, ve_, se_)
        if stop_after == "wo0":
            ln1_apply(0, var0_ve)
            _finish(nc, rec, c, sw, sx, ste, sve, sse, sgp, snb, sout,
                    d_dbg, DUMPABLE)
            rec.replay(block)
            return nc
        ap0 = ln1_apply(0, var0_ve, chunk=4)
        h1se = {}
        def _mid1():
            v_ = ln1_var(1, c.ve, c.se, 0, 8, bias_col=rcpt[:, 32:33])
            h1se.update(ln1_apply(1, v_, 0, 8))
        extra = []
        for tcx in range(TC):
            def _mx(v, tcx=tcx):
                v.wait_ge(sse, ap0[tcx])
                v.reduce_max(stats[:, 224 + tcx:224 + tcx + 1],
                             ln1_0[:, D * tcx:D * tcx + D],
                             axis=AX.X, apply_absolute_value=True) \
                 .then_inc(sve, 1)
                c.ve += 1
            extra.append(_mx)
        ve_, se_ = wo_ln1(1, at1, a_ve, a_se, wo_rdy, projV1_te, tail_gp1,
                          mid_hook=_mid1, gp_bias=True, extra_ve=extra)
        for f in extra[14:]:
            rec.vector(f)
        var1_ve = ln1_var(1, ve_, se_, 8, TC, bias_col=rcpt[:, 32:33])
        ln1_max(1, max(h1se.values()), 0, 8)
        ap1 = ln1_apply(1, var1_ve, 8, TC)
        ve_ = ln1_max(1, max(ap1.values()), 8, TC)

        wo1_te = c.te
        xs_ve = global_scale(ve_)
        w8_rdy = load_w(d_w8T, wo1_te, dst=w8full)
        def _(g):
            g.wait_ge(ste, wo1_te)
            g.dma_start(out=bff[:], in_=d_bf).then_inc(sx, 16)
            c.x += 16
        rec.gpsimd(_)
        bff_x = c.x
        def _(s):
            s.wait_ge(ste, wo1_te)
            s.dma_start(out=wsc[:], in_=d_ws).then_inc(sw, 16)
            c.w += 16
        rec.sync(_)
        wsc_w = c.w

        ffn(xs_ve, bff_x, w8_rdy, wsc_w)

        _finish(nc, rec, c, sw, sx, ste, sve, sse, sgp, snb, sout,
                d_dbg, DUMPABLE)
        rec.replay(block)
    return nc


def _finish(nc, rec, c, sw, sx, ste, sve, sse, sgp, snb, sout, d_dbg,
            DUMPABLE):
    def _(s):
        s.wait_ge(sve, c.ve)
        s.wait_ge(sse, c.se)
        s.wait_ge(ste, c.te)
        s.wait_ge(snb, c.nb)
        for n, dap in d_dbg.items():
            s.dma_start(out=dap, in_=DUMPABLE[n][0][:]).then_inc(sw, 16)
            c.w += 16
        s.wait_ge(sx, c.x)
        s.wait_ge(sw, c.w)
        s.wait_ge(sgp, c.gp)
    rec.sync(_)
    def _(g):
        g.wait_ge(sw, c.w)
    rec.gpsimd(_)


def _prep_shared(inputs):
    f32 = np.float32
    ca = np.ascontiguousarray
    sh = {}
    sh["wqT"] = ca(np.asarray(inputs["L0_attn_Wq"], f32).T)
    sh["wkT"] = ca(np.asarray(inputs["L0_attn_Wk"], f32).T)
    sh["wvT"] = ca(np.asarray(inputs["L0_attn_Wv"], f32).T)
    sh["woT"] = ca(np.asarray(inputs["L0_attn_Wo"], f32).T)
    sh["w8T"] = ca(np.asarray(inputs["L0_ffn_weight_int8"]).T)
    sh["bq_pm"] = ca(np.asarray(inputs["L0_attn_bq"], f32).reshape(NCH, 128).T)
    sh["bk_pm"] = ca(np.asarray(inputs["L0_attn_bk"], f32).reshape(NCH, 128).T)
    sh["bv_bc"] = ca(np.broadcast_to(np.asarray(inputs["L0_attn_bv"], f32), (128, D)))
    sh["bo_bc"] = ca(np.broadcast_to(np.asarray(inputs["L0_attn_bo"], f32), (128, D)))
    sh["bf_bc"] = ca(np.broadcast_to(np.asarray(inputs["L0_ffn_bias"], f32), (128, D)))
    ws = np.asarray(inputs["L0_ffn_weight_scale"], f32)[:, 0]
    sh["ws_bc"] = ca(np.broadcast_to(ws, (128, D)))
    sh["ident"] = np.eye(128, dtype=ml_dtypes.bfloat16)
    sh["mask"] = np.triu(np.ones((128, 128), dtype=ml_dtypes.bfloat16))
    return sh


def kernel(**inputs):
    x = np.asarray(inputs["L0_attn_x"], np.float32)
    sh = _prep_shared(inputs)
    nc = build_program()
    in_maps = []
    for cid in range(8):
        shard = np.ascontiguousarray(x[B * cid:B * cid + B])
        shardT = np.ascontiguousarray(shard.transpose(0, 2, 1))
        m = dict(sh)
        m["x"] = shard
        m["xT"] = shardT
        in_maps.append(m)
    res = run_bass_kernel_spmd(nc, in_maps, core_ids=list(range(8)))
    return np.concatenate([r["out"] for r in res.results], axis=0)
